# version 18
# baseline (speedup 1.0000x reference)
import contextlib
import numpy as np
import concourse.bass as bass
import concourse.mybir as mybir
from concourse.bass_utils import run_bass_kernel_spmd

F32 = mybir.dt.float32
BF16 = mybir.dt.bfloat16
AF = mybir.ActivationFunctionType
ALU = mybir.AluOpType
AXX = mybir.AxisListType.X

P = 128
D = 2048
I = 8192
T = 2048
TC = 512
NCH = T // TC
FT = D // P
FTI = I // P
MAGIC = 12582912.0

WEIGHTS = [("wi", FT, D), ("wf", FT, D), ("wg", FT, D),
           ("wgate", FT, I), ("wup", FT, I), ("wo", FT, D), ("wdown", FTI, D)]
WIDX = {w: j for j, (w, _, _) in enumerate(WEIGHTS)}
CC_OF = {"wi": 2, "wf": 4, "wg": 5, "wgate": 6, "wup": 7, "wo": 8,
         "wdown": 9}


class Prog:
    ENGS = ("sync", "tensor", "vector", "scalar", "gpsimd")

    def __init__(self):
        self.streams = {e: [] for e in self.ENGS}
        self.ctr = {}
        self.sems = None

    def emit(self, eng, fn):
        self.streams[eng].append((fn, False))

    def bump(self, sem, amt):
        self.ctr[sem] = self.ctr.get(sem, 0) + amt
        return self.ctr[sem]

    def val(self, sem):
        return self.ctr.get(sem, 0)

    def wait(self, eng, sem, v=None):
        if v is None:
            v = self.val(sem)
        if v <= 0:
            return
        self.emit(eng, lambda e, s=sem, vv=v: e.wait_ge(self.sems[s], vv))
        self.streams[eng][-1] = (self.streams[eng][-1][0], True)

    def dma(self, dst, src, sem, slow=False, eng="sync"):
        self.bump(sem, 16)
        self.emit(eng, lambda e, d=dst, s=src, sm=sem, sl=slow:
                  e.dma_start(out=d, in_=s, allow_slow_non_contiguous=sl
                              ).then_inc(self.sems[sm], 16))
        self.streams[eng][-1] = (self.streams[eng][-1][0], True)

    def dma_in(self, dst, src, sem, slow=False, eng="sync"):
        self.wait(eng, "sV")
        self.wait(eng, "sS")
        self.dma(dst, src, sem, slow, eng)

    def dma_out(self, dst, src, sem, slow=False, eng="sync"):
        self.wait(eng, "sV")
        self.dma(dst, src, sem, slow, eng)

    def dma_split(self, dst, src, sem, nsplit, in_=False, eng="sync"):
        if in_:
            self.wait(eng, "sV")
            self.wait(eng, "sS")
        else:
            self.wait(eng, "sV")
        step = 128 // nsplit
        for j in range(nsplit):
            self.dma(dst[j * step:(j + 1) * step, :],
                     src[j * step:(j + 1) * step, :], sem, eng=eng)

    def inc(self, eng, sem, amt=1):
        self.bump(sem, amt)
        st = self.streams[eng]
        assert st, "inc with no prior instruction"
        prev, wrapped = st[-1]
        assert not wrapped, f"double inc on one instruction ({eng}/{sem})"
        st[-1] = (lambda e, f=prev, s=sem, a=amt: f(e).then_inc(self.sems[s], a),
                  True)


SUBSTOP = None
DELAY_TEST = False


def build_program(stop_after=None):
    nc = bass.Bass("TRN2", target_bir_lowering=False,
                   detect_race_conditions=False)
    p = Prog()
    E, TE, V, S, G = "sync", "tensor", "vector", "scalar", "gpsimd"

    din = lambda n, s, dt=F32: nc.dram_tensor(n, s, dt, kind="ExternalInput").ap()
    xT = din("xT", [D, T])
    wTs = {w: din(w + "S", [(K // 8) * P, O]) for (w, K, O) in WEIGHTS}
    nrm = {k: din("nrm_" + k, [P, FT]) for k in ("i", "f", "g", "o", "gn")}
    lnw_in = din("lnw", [P, FTI])
    csel_in = din("csel", [P, 8])
    ident_in = din("ident", [P, P])
    outT = nc.dram_tensor("outT", [D, T], F32, kind="ExternalOutput").ap()

    dint = lambda n, s, dt=F32, sp="Local": nc.dram_tensor(
        n, s, dt, kind="Internal", addr_space=sp).ap()
    slab = {w: dint("slab_" + w, [(K // 8) * P, O], BF16) for (w, K, O) in WEIGHTS}
    wq = {w: dint("wq_" + w, [K * P, O], BF16, "Shared") for (w, K, O) in WEIGHTS}
    part_box = dint("part_box", [8])
    part_gat = dint("part_gat", [8, 8], F32, "Shared")
    sw_box = dint("sw_box", [8])
    hscr = dint("hscr", [D, T])
    fscr = dint("fscr", [D, T])
    hfscr = dint("hfscr", [I, TC])
    flag_box = dint("flag_box", [8])
    flag_gat = dint("flag_gat", [8, 8], F32, "Shared")
    carry_box = dint("carry_box", [D])
    carry_gat = dint("carry_gat", [8, D], F32, "Shared")
    rowscr = dint("rowscr", [NCH, 8, TC])

    KB = 1024

    SB_BASE = 18 * KB

    def sb_at(name, shape, dt, off):
        return nc.alloc_sbuf_tensor_at(name, shape, dt, offset=SB_BASE + off).ap()

    bx = sb_at("bx", [P, FT * TC], F32, 0)
    ba = sb_at("ba", [P, FT * TC], F32, 32 * KB)
    bb = sb_at("bb", [P, FT * TC], F32, 64 * KB)
    bc = sb_at("bc", [P, FT * TC], F32, 96 * KB)
    bnh = sb_at("bnh", [P, FTI * TC], BF16, 64 * KB)
    bn = sb_at("bn", [P, FT * TC], BF16, 128 * KB)
    bnf = sb_at("bnf", [P, 8 * TC], F32, 128 * KB)
    off = 144 * KB
    sc = []
    for j in range(8):
        sc.append(sb_at(f"sc{j}", [P, TC], F32, off)); off += 2 * KB
    nm = {}
    for t_ in ("m1", "mh", "bcA", "bcB", "bcR", "facT", "deqT", "deq2T", "Mb",
               "hfacc", "bz", "mn"):
        nm[t_] = sb_at(t_, [P, TC], F32, off); off += 2 * KB
    m1, mh, bcA, bcB, bcR = nm["m1"], nm["mh"], nm["bcA"], nm["bcB"], nm["bcR"]
    facT, deqT, deq2T, Mb, hfacc, bz = (nm["facT"], nm["deqT"], nm["deq2T"],
                                        nm["Mb"], nm["hfacc"], nm["bz"])
    mntl = nm["mn"]
    wt = []
    for j in range(4):
        wt.append(sb_at(f"wt{j}", [P, TC], BF16, off)); off += 1 * KB
    u3 = sb_at("u3", [P, TC], F32, off); off += 2 * KB
    nrm_sb = {}
    for k in ("i", "f", "g", "o", "gn"):
        nrm_sb[k] = sb_at("nsb_" + k, [P, FT], F32, off); off += 64
    lnw_sb = sb_at("lnw_sb", [P, FTI], F32, off); off += 256
    ident = sb_at("ident_sb", [P, P], F32, off); off += 512
    onecol = sb_at("onecol", [P, 1], F32, off); off += 32
    onesrow = sb_at("onesrow", [1, P], F32, off); off += 512
    hc = sb_at("hc", [P, FT], F32, off); off += 64
    fc = sb_at("fc", [P, FT], F32, off); off += 64
    carry_sb = sb_at("carry_sb", [P, FT], F32, off); off += 64
    cslots = sb_at("cslots", [P, 8 * FT], F32, off); off += 512
    csel_sb = sb_at("csel_sb", [P, 8], F32, off); off += 32
    swcol = sb_at("swcol", [P, 8], F32, off); off += 32
    swinv = sb_at("swinv", [P, 8], F32, off); off += 32
    rw = []
    for j in range(6):
        rw.append(sb_at(f"rw{j}", [1, TC], F32, off)); off += 2 * KB
    packrow = sb_at("packrow", [4, P], F32, off); off += 512
    mp = sb_at("mp", [P, 4], F32, off); off += 32
    partrow = sb_at("partrow", [1, 8], F32, off); off += 32
    partall = sb_at("partall", [8, 8], F32, off); off += 32
    srow = sb_at("srow", [1, 8], F32, off); off += 32
    assert off <= 207 * KB, f"SBUF over budget: {off}"

    pp = [nc.alloc_psum_tensor(f"pp{j}", [P, TC], F32).ap() for j in range(8)]

    def vsync():
        st_ = p.streams["vector"]
        fn, wrapped = st_[-1]
        if not wrapped:
            p.bump("sV", 1)
            st_[-1] = (lambda e, f=fn: f(e).then_inc(p.sems["sV"], 1), True)
        p.wait("vector", "sV")

    def bcast_mm(dst_tile, row_sbuf):
        p.wait(TE, "sV")
        p.emit(TE, lambda e, r=row_sbuf: e.matmul(
            pp[7], onesrow, r, start=True, stop=True))
        p.inc(TE, "sT")
        p.wait(V, "sT")
        p.emit(V, lambda e, d=dst_tile: e.tensor_copy(d, pp[7]))
        vsync()

    def pmax_to_bcast(ch, slot, dst_tile):
        p.wait(TE, "sV")
        for j in range(4):
            p.emit(TE, lambda e, j=j: e.transpose(
                pp[7][:, j * P:(j + 1) * P], m1[:, j * P:(j + 1) * P], ident))
        p.inc(TE, "sT")
        p.wait(V, "sT")
        for j in range(4):
            p.emit(V, lambda e, j=j: e.reduce_max(
                mp[:, j:j + 1], pp[7][:, j * P:(j + 1) * P], axis=AXX))
        p.inc(V, "sV")
        p.wait(TE, "sV")
        p.emit(TE, lambda e: e.transpose(pp[7][0:4, 0:P], mp, ident))
        p.inc(TE, "sT")
        p.wait(V, "sT")
        p.emit(V, lambda e: e.tensor_copy(packrow, pp[7][0:4, 0:P]))
        p.inc(V, "sV")
        dst_row = rowscr[ch, slot, :]
        dstap = bass.AP(dst_row.tensor, dst_row.offset, [[P, 4], [1, P]])
        p.dma_out(dstap, packrow, "sRow")
        p.wait(E, "sRow")
        p.wait(E, "sT")
        p.dma_in(rw[2], dst_row, "sBc")
        p.wait(TE, "sBc")
        bcast_mm(dst_tile, rw[2])

    def psum_row_of(acc_tile, bank):
        p.wait(TE, "sV")
        p.emit(TE, lambda e, a=acc_tile, b=bank: e.matmul(
            pp[b][0:1, :], onecol, a, start=True, stop=True))
        p.inc(TE, "sT")

    def sqrt_row(dst, srcr):
        p.wait(S, "sV")
        p.emit(S, lambda e, d=dst, r=srcr: e.activation(d, r, AF.Sqrt))
        p.inc(S, "sS")
        p.wait(V, "sS")

    def acc_squares(in_buf, nft=FT):
        for ft in range(nft):
            src = in_buf[:, ft * TC:(ft + 1) * TC]
            p.emit(V, lambda e, s=src: e.tensor_tensor(sc[0], s, s, ALU.mult))
            if ft == 0:
                p.emit(V, lambda e: e.tensor_copy(sc[4], sc[0]))
            else:
                p.emit(V, lambda e: e.tensor_tensor(sc[4], sc[4], sc[0], ALU.add))
        p.inc(V, "sV")

    def maxes(in_buf, nrm_tile):
        p.emit(V, lambda e: e.memset(m1, 0.0))
        p.inc(V, "sV")
        p.wait(S, "sV")
        for ft in range(FT):
            src = in_buf[:, ft * TC:(ft + 1) * TC]
            sq = sc[0] if ft % 2 == 0 else sc[1]
            if nrm_tile is not None:
                col = nrm_tile[:, ft:ft + 1]
                p.emit(S, lambda e, s=src, c=col, q=sq: e.activation(
                    q, s, AF.Square, scale=c))
            else:
                p.emit(S, lambda e, s=src, q=sq: e.activation(q, s, AF.Square))
            p.inc(S, "sS")
            p.wait(V, "sS")
            p.emit(V, lambda e, q=sq: e.tensor_tensor(m1, m1, q, ALU.max))
            p.inc(V, "sV")
            if ft >= 1:
                p.wait(S, "sV", p.val("sV") - 1)
        p.wait(V, "sV")

    def fac_deq(rfac_tile, w_name, deq_dst, fac_dst, w2_name=None, deq2_dst=None):
        wi = WIDX[w_name]
        p.wait(S, "sV")
        p.wait(S, "sBc")
        p.emit(S, lambda e: e.activation(mntl, Mb, AF.Sqrt))
        p.inc(S, "sS")
        p.wait(V, "sS")
        if rfac_tile is not None:
            p.emit(V, lambda e: e.tensor_tensor(sc[2], mntl, rfac_tile, ALU.mult))
        else:
            p.emit(V, lambda e: e.tensor_copy(sc[2], mntl))
        p.emit(V, lambda e: e.tensor_scalar(sc[2], sc[2], 1e-5, None, ALU.max))
        p.emit(V, lambda e: e.reciprocal(sc[3], sc[2]))
        vsync()
        if rfac_tile is not None:
            p.emit(V, lambda e, f=fac_dst: e.scalar_tensor_tensor(
                f, sc[3], 127.0, rfac_tile, ALU.mult, ALU.mult))
        else:
            p.emit(V, lambda e, f=fac_dst: e.tensor_scalar(
                f, sc[3], 127.0, None, ALU.mult))
        p.emit(V, lambda e, d=deq_dst, wi=wi: e.tensor_scalar(
            d, sc[2], swcol[:, wi:wi + 1], 1.0 / 127.0, ALU.mult, ALU.mult))
        if w2_name is not None:
            wi2 = WIDX[w2_name]
            p.emit(V, lambda e, d=deq2_dst, wi=wi2: e.tensor_scalar(
                d, sc[2], swcol[:, wi:wi + 1], 1.0 / 127.0, ALU.mult, ALU.mult))
        p.inc(V, "sV")

    def quant_to_bn(in_buf, nrm_tile, out_bf):
        base = p.val("sV")
        for ft in range(FT):
            src = in_buf[:, ft * TC:(ft + 1) * TC]
            dstn = out_bf[:, ft * TC:(ft + 1) * TC]
            if nrm_tile is not None:
                col = nrm_tile[:, ft:ft + 1]
                p.emit(V, lambda e, s=src, c=col: e.scalar_tensor_tensor(
                    sc[0], s, c, facT, ALU.mult, ALU.mult))
            else:
                p.emit(V, lambda e, s=src: e.tensor_tensor(sc[0], s, facT, ALU.mult))
            p.emit(V, lambda e, d=dstn: e.tensor_scalar(
                d, sc[0], MAGIC, MAGIC, ALU.add, ALU.subtract))
            p.inc(V, "sV")
        return base

    def mm_accum(wname, K, og, bank, rhs_bf, rhs_ready=None):
        slot_base = [p.val(f"sW{s}") for s in range(4)]
        mm_base = p.val("sT_mm")
        p.wait(E, "sCc", CC_OF[wname])
        for k in range(K):
            p.wait(E, "sT_mm", mm_base + max(k - 3, 0))
            src = wq[wname][k * P:(k + 1) * P, og * 512:(og + 1) * 512]
            p.dma(wt[k % 4], src, f"sW{k % 4}")
        for k in range(K):
            if rhs_ready is not None:
                p.wait(TE, "sV", rhs_ready + k + 1)
            p.wait(TE, f"sW{k % 4}", slot_base[k % 4] + 16 * (k // 4 + 1))
            for oc in range(4):
                p.emit(TE, lambda e, k=k, oc=oc, b=bank: e.matmul(
                    pp[b + oc], wt[k % 4][:, oc * P:(oc + 1) * P],
                    rhs_bf[:, k * TC:(k + 1) * TC],
                    start=(k == 0), stop=(k == K - 1)))
            p.inc(TE, "sT_mm")

    def mm_evac(wname, K, rhs_bf, n_og, out_buf=None, deq_tile=None, post=None,
                entry_sv=None, rhs_base=None):
        p.wait(TE, "sV", entry_sv if entry_sv is not None else p.val("sV"))
        ev_after = []
        for og in range(n_og):
            bank = (og % 2) * 4
            if og >= 2:
                p.wait(TE, "sV", ev_after[og - 2])
            mm_accum(wname, K, og, bank, rhs_bf,
                     rhs_ready=(rhs_base if og == 0 else None))
            p.wait(V, "sT_mm")
            if post is None:
                for oc in range(4):
                    ftile = og * 4 + oc
                    dst = out_buf[:, ftile * TC:(ftile + 1) * TC]
                    p.emit(V, lambda e, b=bank, oc=oc, d=dst, q=deq_tile:
                           e.tensor_tensor(d, pp[b + oc], q, ALU.mult))
                p.inc(V, "sV")
                p.wait(V, "sV")
            else:
                post(og, bank)
            ev_after.append(p.val("sV"))

    p.dma(ident, ident_in, "sX")
    for k in ("i", "f", "g", "o", "gn"):
        p.dma(nrm_sb[k], nrm[k], "sX")
    p.dma(lnw_sb, lnw_in, "sX")
    p.dma(csel_sb, csel_in, "sX")
    p.emit(V, lambda e: e.memset(onecol, 1.0))
    p.emit(V, lambda e: e.memset(onesrow, 1.0))
    p.emit(V, lambda e: e.memset(partrow, 0.0))
    p.inc(V, "sV")
    p.dma_out(bass.AP(flag_box.tensor, 0, [[8, 1], [1, 8]]), partrow, "sScr")
    p.emit(V, lambda e: e.memset(srow, 0.0))
    p.emit(V, lambda e: e.memset(bz, 0.0))
    p.emit(V, lambda e: e.memset(hc, 0.0))
    p.emit(V, lambda e: e.memset(fc, 1.0))
    p.inc(V, "sV")

    def w_partials(subset):
        for w in subset:
            widx = WIDX[w]
            K, O = WEIGHTS[widx][1], WEIGHTS[widx][2]
            kper = K // 8
            for j in range(kper):
                p.dma_split(bx[:, 0:O], wTs[w][j * P:(j + 1) * P, :], "sX", 1,
                            in_=True)
                p.wait(V, "sX")
                p.wait(V, "sT")
                nseg = O // TC
                for sgi in range(nseg):
                    seg = bx[:, sgi * TC:(sgi + 1) * TC]
                    p.emit(V, lambda e, s=seg: e.tensor_scalar(
                        sc[2], s, -1.0, None, ALU.mult))
                    p.emit(V, lambda e, s=seg: e.tensor_tensor(
                        sc[2], sc[2], s, ALU.max))
                    p.emit(V, lambda e, sgi=sgi: e.reduce_sum(
                        sc[4][:, sgi:sgi + 1], sc[2], axis=AXX))
                vsync()
                p.emit(V, lambda e, n=nseg: e.reduce_sum(
                    sc[5][:, 0:1], sc[4][:, 0:n], axis=AXX))
                p.inc(V, "sV")
                p.wait(TE, "sV")
                p.emit(TE, lambda e, f=(j == 0), l=(j == kper - 1): e.matmul(
                    pp[6][0:1, 0:1], onecol, sc[5][:, 0:1], start=f, stop=l))
                p.inc(TE, "sT")
            p.wait(V, "sT")
            p.emit(V, lambda e, wi=widx: e.tensor_copy(partrow[:, wi:wi + 1],
                                                       pp[6][0:1, 0:1]))
            p.inc(V, "sV")

    def w_gather_partials():
        p.dma_out(bass.AP(part_box.tensor, 0, [[8, 1], [1, 8]]), partrow, "sScr")
        p.wait(G, "sScr")
        p.bump("sCc", 1)
        p.emit(G, lambda e: e.collective_compute(
            "AllGather", ALU.bypass, replica_groups=[list(range(8))],
            ins=[part_box], outs=[part_gat]).then_inc(p.sems["sCc"], 1))
        return p.val("sCc")

    def w_srow(cc_need):
        p.wait(E, "sCc", cc_need)
        p.dma_in(partall, part_gat, "sScr")
        p.wait(TE, "sScr")
        p.wait(TE, "sV")
        p.emit(TE, lambda e: e.matmul(pp[6][0:1, 0:8], onecol[0:8, :], partall,
                                      start=True, stop=True))
        p.inc(TE, "sT")
        p.wait(V, "sT")
        p.emit(V, lambda e: e.tensor_copy(srow, pp[6][0:1, 0:8]))
        vsync()
        for widx, (w, K, O) in enumerate(WEIGHTS):
            numel = float(K * P * O)
            p.emit(V, lambda e, wi=widx, n=numel: e.tensor_scalar(
                srow[:, wi:wi + 1], srow[:, wi:wi + 1], 1.0 / n, 1e-5,
                ALU.mult, ALU.max))
        p.inc(V, "sV")
        p.dma_out(bass.AP(sw_box.tensor, 0, [[8, 1], [1, 8]]), srow, "sScr")
        p.wait(E, "sScr")
        p.dma_in(swcol, bass.AP(sw_box.tensor, sw_box.offset, [[0, P], [1, 8]]),
                 "sBc")
        p.wait(V, "sBc")
        for widx in range(len(WEIGHTS)):
            p.emit(V, lambda e, wi=widx: e.reciprocal(swinv[:, wi:wi + 1],
                                                      swcol[:, wi:wi + 1]))
        p.inc(V, "sV")
        p.wait(V, "sV")

    def w_quant(subset):
        for w in subset:
            widx = WIDX[w]
            K, O = WEIGHTS[widx][1], WEIGHTS[widx][2]
            kper = K // 8
            for j in range(kper):
                p.dma_split(bx[:, 0:O], wTs[w][j * P:(j + 1) * P, :], "sX", 1,
                            in_=True)
                p.wait(V, "sX")
                p.wait(V, "sScr")
                nseg = O // TC
                for sgi in range(nseg):
                    seg = bx[:, sgi * TC:(sgi + 1) * TC]
                    slot = sgi % FT
                    p.emit(V, lambda e, s=seg, wi=widx: e.tensor_scalar(
                        sc[7], s, swinv[:, wi:wi + 1], MAGIC, ALU.mult, ALU.add))
                    p.emit(V, lambda e: e.tensor_scalar(
                        sc[7], sc[7], MAGIC + 1.0, MAGIC - 1.0, ALU.min, ALU.max))
                    p.emit(V, lambda e, sl=slot: e.tensor_scalar(
                        bn[:, sl * TC:(sl + 1) * TC], sc[7], MAGIC, None,
                        ALU.subtract))
                    p.inc(V, "sV")
                    p.dma_split(slab[w][j * P:(j + 1) * P, sgi * TC:(sgi + 1) * TC],
                                bn[:, slot * TC:(slot + 1) * TC], "sScr", 1)
            p.wait(G, "sScr")
            p.bump("sCc", 1)
            assert p.val("sCc") == CC_OF[w], (w, p.val("sCc"), CC_OF[w])
            p.emit(G, lambda e, w=w: e.collective_compute(
                "AllGather", ALU.bypass, replica_groups=[list(range(8))],
                ins=[slab[w]], outs=[wq[w]]).then_inc(p.sems["sCc"], 1))

    def w_barrier():
        p.bump("sCc", 1)
        p.emit(G, lambda e: e.collective_compute(
            "AllGather", ALU.bypass, replica_groups=[list(range(8))],
            ins=[flag_box], outs=[flag_gat]).then_inc(p.sems["sCc"], 1))
        return p.val("sCc")

    W_REST = ["wf", "wg", "wgate", "wup", "wo", "wdown"]

    def prologue_rest():
        w_partials(W_REST)
        w_gather_partials()
        w_srow(3)
        w_quant(W_REST)
        assert w_barrier() == CC_BARRIER

    for _ft in range(FT):
        p.dma_split(bb[:, _ft * TC:(_ft + 1) * TC],
                    xT[_ft * P:(_ft + 1) * P, 0:TC],
                    "sXg", 1, in_=True, eng=S)
    w_partials(["wi"])
    w_gather_partials()
    w_srow(1)
    w_quant(["wi"])
    CC_BARRIER = 10

    def load_x(ch, dst):
        for ft in range(FT):
            p.dma_split(dst[:, ft * TC:(ft + 1) * TC],
                        xT[ft * P:(ft + 1) * P, ch * TC:(ch + 1) * TC],
                        "sXg", 1, in_=True, eng=S)

    def ln_stats(ch, xbuf):
        p.wait(V, "sXg")
        for j in range(4):
            p.wait(V, f"sFb{j}")
        for ft in range(FT):
            src = xbuf[:, ft * TC:(ft + 1) * TC]
            if ft == 0:
                p.emit(V, lambda e, s=src: e.tensor_copy(sc[5], s))
            else:
                p.emit(V, lambda e, s=src: e.tensor_tensor(sc[5], sc[5], s, ALU.add))
        p.inc(V, "sV")
        acc_squares(xbuf)
        psum_row_of(sc[5], 6)
        psum_row_of(sc[4], 5)
        p.wait(V, "sT")
        p.emit(V, lambda e: e.tensor_scalar(rw[0], pp[6][0:1, :], 1.0 / D, None, ALU.mult))
        p.emit(V, lambda e: e.tensor_scalar(rw[1], pp[5][0:1, :], 1.0 / D, None, ALU.mult))
        vsync()
        p.emit(V, lambda e: e.tensor_tensor(rw[2], rw[0], rw[0], ALU.mult))
        p.emit(V, lambda e: e.tensor_tensor(rw[2], rw[1], rw[2], ALU.subtract))
        p.emit(V, lambda e: e.tensor_scalar(rw[1], rw[2], 1e-5, None, ALU.add))
        p.inc(V, "sV")
        sqrt_row(rw[4], rw[1])
        p.emit(V, lambda e: e.reciprocal(rw[3], rw[4]))
        vsync()
        p.emit(V, lambda e: e.tensor_tensor(rw[4], rw[0], rw[3], ALU.mult))
        p.emit(V, lambda e: e.tensor_scalar(rw[4], rw[4], -1.0, None, ALU.mult))
        p.emit(V, lambda e: e.reciprocal(rw[5], rw[1]))
        vsync()
        p.emit(V, lambda e: e.tensor_tensor(rw[5], rw[2], rw[5], ALU.mult))
        p.emit(V, lambda e: e.tensor_scalar(rw[5], rw[5], 1e-8, None, ALU.add))
        p.inc(V, "sV")
        sqrt_row(rw[1], rw[5])
        p.emit(V, lambda e: e.reciprocal(rw[5], rw[1]))
        p.inc(V, "sV")
        p.dma_out(rowscr[ch, 0, :], rw[3], "sRow")
        p.dma_out(rowscr[ch, 1, :], rw[4], "sRow")
        p.dma_out(rowscr[ch, 2, :], rw[5], "sRow")
        bcast_mm(bcA, rw[3])
        bcast_mm(bcB, rw[4])
        bcast_mm(bcR, rw[5])

    def make_xn(dst_buf, xbuf):
        p.wait(V, "sScr")
        for ft in range(FT):
            x_ = xbuf[:, ft * TC:(ft + 1) * TC]
            d_ = dst_buf[:, ft * TC:(ft + 1) * TC]
            p.emit(V, lambda e, x=x_, d=d_: e.tensor_tensor(d, x, bcA, ALU.mult))
            p.emit(V, lambda e, d=d_: e.tensor_tensor(d, d, bcB, ALU.add))
        p.inc(V, "sV")

    def section_A(ch, preloaded=False):
        if not preloaded:
            load_x(ch, bb)
        if SUBSTOP == "load":
            return
        ln_stats(ch, bb)
        if SUBSTOP == "ln":
            return
        make_xn(bc, bb)
        maxes(bc, nrm_sb["i"])
        if SUBSTOP == "max":
            return
        pmax_to_bcast(ch, 3, Mb)
        if SUBSTOP == "pmax":
            return
        fac_deq(bcR, "wi", deqT, facT)
        svA = p.val("sV")
        qi = quant_to_bn(bc, nrm_sb["i"], bn)
        if SUBSTOP == "quant":
            return
        maxes(bc, nrm_sb["f"])
        if isinstance(SUBSTOP, str) and SUBSTOP.startswith("mm_og"):
            n = int(SUBSTOP[5:])
            mm_evac("wi", FT, bn, n, out_buf=ba, deq_tile=deqT)
            return
        if SUBSTOP == "mm_noev":
            ev_entry = p.val("sV_ev")
            p.wait(TE, "sV")
            mm_accum("wi", FT, 0, 0, bn)
            p.wait(V, "sT")
            p.inc(V, "sV")
            return
        if SUBSTOP == "mm_1k":
            p.wait(TE, "sV")
            sW_base = p.val("sW")
            p.wait(E, "sCc", CC_OF["wi"])
            p.dma(wt[0], wq["wi"][0:P, 0:512], "sW")
            p.wait(TE, "sW", sW_base + 16)
            for oc in range(4):
                p.emit(TE, lambda e, oc=oc: e.matmul(
                    pp[oc], wt[0][:, oc * P:(oc + 1) * P], bn[:, 0:TC],
                    start=True, stop=True))
            p.inc(TE, "sT")
            p.wait(V, "sT")
            for oc in range(4):
                p.emit(V, lambda e, oc=oc: e.tensor_tensor(
                    ba[:, oc * TC:(oc + 1) * TC], pp[oc], deqT, ALU.mult))
            p.inc(V, "sV")
            return
        mm_evac("wi", FT, bn, 4, out_buf=ba, deq_tile=deqT,
                entry_sv=svA, rhs_base=qi)
        if ch == 0:
            prologue_rest()
        if SUBSTOP == "mmi":
            p.wait(E, "sV")
            for ft in range(FT):
                p.dma(outT[ft * P:(ft + 1) * P, 0:TC], ba[:, ft * TC:(ft + 1) * TC], "sOut")
            p.wait(E, "sOut")
            return
        pmax_to_bcast(ch, 4, Mb)
        fac_deq(bcR, "wf", deqT, facT)
        svF = p.val("sV")
        qf = quant_to_bn(bc, nrm_sb["f"], bn)

        def post_f(og, bank):
            for oc in range(4):
                ftile = og * 4 + oc
                d = bb[:, ftile * TC:(ftile + 1) * TC]
                ia = ba[:, ftile * TC:(ftile + 1) * TC]
                p.emit(V, lambda e, b=bank, oc=oc: e.tensor_tensor(
                    sc[0], pp[b + oc], deqT, ALU.mult))
                p.inc(V, "sV")
                p.wait(S, "sV")
                p.emit(S, lambda e, d=d: e.activation(d, sc[0], AF.Sigmoid))
                p.inc(S, "sS")
                p.emit(S, lambda e, s=ia: e.activation(sc[1], s, AF.Silu))
                p.inc(S, "sS")
                p.wait(V, "sS")
                p.emit(V, lambda e, f=d: e.tensor_scalar(
                    sc[2], f, -1.0, 1.0, ALU.mult, ALU.add))
                p.emit(V, lambda e, d2=ia: e.tensor_tensor(d2, sc[1], sc[2], ALU.mult))
                p.inc(V, "sV")
        mm_evac("wf", FT, bn, 4, post=post_f, entry_sv=svF, rhs_base=qf)
        if SUBSTOP == "mmf":
            return
        for ft in range(FT):
            fb = bb[:, ft * TC:(ft + 1) * TC]
            ib = ba[:, ft * TC:(ft + 1) * TC]
            hd = bc[:, ft * TC:(ft + 1) * TC]
            p.emit(V, lambda e, f=fb, i=ib, d=hd, ft=ft: e.tensor_tensor_scan(
                d, f, i, hc[:, ft:ft + 1], ALU.mult, ALU.add))
            vsync()
            p.emit(V, lambda e, d=hd, ft=ft: e.tensor_copy(
                hc[:, ft:ft + 1], d[:, TC - 1:TC]))
            p.inc(V, "sV")
            p.dma_split(hscr[ft * P:(ft + 1) * P, ch * TC:(ch + 1) * TC], hd,
                        "sScr", 1, eng=S)
            fs = sc[1 + (ft % 4)]
            if ft >= 4:
                p.wait(V, f"sFb{ft % 4}")
            p.emit(V, lambda e, f=fb, d=fs, ft=ft: e.tensor_tensor_scan(
                d, f, bz, fc[:, ft:ft + 1], ALU.mult, ALU.add))
            vsync()
            p.emit(V, lambda e, d=fs, ft=ft: e.tensor_copy(
                fc[:, ft:ft + 1], d[:, TC - 1:TC]))
            p.inc(V, "sV")
            p.dma_split(fscr[ft * P:(ft + 1) * P, ch * TC:(ch + 1) * TC], fs,
                        f"sFb{ft % 4}", 1, eng=S)

    def carry_exchange():
        dst = bass.AP(carry_box.tensor, carry_box.offset, [[FT, P], [1, FT]])
        p.dma_out(dst, hc, "sScr")
        p.wait(G, "sScr")
        p.bump("sCc", 1)
        cc_t = p.val("sCc")
        p.emit(G, lambda e: e.collective_compute(
            "AllGather", ALU.bypass, replica_groups=[list(range(8))],
            ins=[carry_box], outs=[carry_gat]).then_inc(p.sems["sCc"], 1))
        p.bump("sCc", 1)
        cc_t = p.val("sCc")
        p.emit(G, lambda e: e.collective_compute(
            "AllGather", ALU.bypass, replica_groups=[list(range(8))],
            ins=[flag_box], outs=[flag_gat]).then_inc(p.sems["sCc"], 1))
        p.wait(E, "sCc", cc_t)
        for s_ in range(8):
            src = bass.AP(carry_gat.tensor, carry_gat.offset + s_ * D,
                          [[FT, P], [1, FT]])
            p.dma_in(cslots[:, s_ * FT:(s_ + 1) * FT], src, "sScr")
        p.wait(V, "sScr")
        p.emit(V, lambda e: e.memset(carry_sb, 0.0))
        for s_ in range(8):
            p.emit(V, lambda e, s=s_: e.scalar_tensor_tensor(
                carry_sb, cslots[:, s * FT:(s + 1) * FT], csel_sb[:, s:s + 1],
                carry_sb, ALU.mult, ALU.add))
        p.inc(V, "sV")

    def section_B(ch):
        load_x(ch, bx)
        p.wait(E, "sRow")
        p.dma_in(rw[3], rowscr[ch, 0, :], "sBc")
        p.dma_in(rw[4], rowscr[ch, 1, :], "sBc")
        p.dma_in(rw[5], rowscr[ch, 2, :], "sBc")
        p.wait(TE, "sBc")
        bcast_mm(bcA, rw[3])
        bcast_mm(bcB, rw[4])
        bcast_mm(bcR, rw[5])
        p.wait(V, "sXg")
        make_xn(bc, bx)
        maxes(bc, nrm_sb["g"])
        pmax_to_bcast(ch, 3, Mb)
        fac_deq(bcR, "wg", deqT, facT)
        svG = p.val("sV")
        qg = quant_to_bn(bc, nrm_sb["g"], bn)
        maxes(bc, None)
        mm_evac("wg", FT, bn, 4, out_buf=ba, deq_tile=deqT,
                entry_sv=svG, rhs_base=qg)
        pmax_to_bcast(ch, 4, Mb)
        fac_deq(None, "wgate", deqT, facT, w2_name="wup", deq2_dst=deq2T)
        svU = p.val("sV")
        qffn = quant_to_bn(bc, None, bn)
        p.wait(E, "sScr")
        for j in range(4):
            p.wait(E, f"sFb{j}")
        for ft in range(FT):
            p.dma_split(bb[:, ft * TC:(ft + 1) * TC],
                        hscr[ft * P:(ft + 1) * P, ch * TC:(ch + 1) * TC],
                        "sHfL", 1, in_=True, eng=S)
            p.dma_split(bc[:, ft * TC:(ft + 1) * TC],
                        fscr[ft * P:(ft + 1) * P, ch * TC:(ch + 1) * TC],
                        "sHfL", 1, in_=True, eng=S)
        p.emit(V, lambda e: e.memset(hfacc, 0.0))
        p.emit(V, lambda e: e.memset(mh, 0.0))
        p.inc(V, "sV")
        p.wait(S, "sOut0")
        p.wait(S, "sOut1")
        p.wait(TE, "sV", svU)
        n_og = I // 512
        ut = [sc[1], sc[2], sc[3], u3]
        ev_g = []
        ev_u = []
        for og in range(n_og):
            if og >= 1:
                p.wait(TE, "sV", ev_g[og - 1])
            mm_accum("wgate", FT, og, 0, bn,
                     rhs_ready=(qffn if og == 0 else None))
            g_done = p.val("sT_mm")
            p.wait(V, "sT_mm", g_done)
            for oc in range(4):
                gsc = sc[0] if oc % 2 == 0 else sc[6]
                if og > 0 or oc > 1:
                    p.wait(V, "sS", p.val("sS") - 1)
                p.emit(V, lambda e, oc=oc, g=gsc: e.tensor_tensor(
                    g, pp[0 + oc], deqT, ALU.mult))
                p.inc(V, "sV")
                p.wait(S, "sV")
                p.emit(S, lambda e, g=gsc, u=ut[oc]: e.activation(u, g, AF.Silu))
                p.inc(S, "sS")
            if og >= 1:
                p.wait(TE, "sV", ev_u[og - 1])
            mm_accum("wup", FT, og, 4, bn)
            u_done = p.val("sT_mm")
            ev_g.append(p.val("sV"))
            p.wait(V, "sT_mm", u_done)
            p.wait(V, "sS")
            for oc in range(4):
                ioc = og * 4 + oc
                p.emit(V, lambda e, oc=oc: e.tensor_tensor(
                    sc[0], pp[4 + oc], deq2T, ALU.mult))
                vsync()
                hftile = sc[4 + (oc % 2)]
                if og > 0 or oc > 1:
                    p.wait(V, f"sHf{oc % 2}")
                p.emit(V, lambda e, h=hftile, u=ut[oc]: e.tensor_tensor(
                    h, u, sc[0], ALU.mult))
                p.emit(V, lambda e, h=hftile: e.tensor_tensor(sc[6], h, h, ALU.mult))
                p.emit(V, lambda e: e.tensor_tensor(hfacc, hfacc, sc[6], ALU.add))
                p.inc(V, "sV")
                p.wait(S, "sV")
                p.emit(S, lambda e, h=hftile, i=ioc: e.activation(
                    sc[7], h, AF.Square, scale=lnw_sb[:, i:i + 1]))
                p.inc(S, "sS")
                p.wait(V, "sS")
                p.emit(V, lambda e: e.tensor_tensor(mh, mh, sc[7], ALU.max))
                p.inc(V, "sV")
                p.dma_split(hfscr[ioc * P:(ioc + 1) * P, :], hftile,
                            f"sHf{oc % 2}", 1, eng=S)
            ev_u.append(p.val("sV"))
        p.wait(V, "sScr")
        p.wait(V, "sHf0")
        p.wait(V, "sHf1")
        acc_squares(ba)
        psum_row_of(sc[4], 6)
        p.wait(V, "sT")
        p.emit(V, lambda e: e.tensor_scalar(
            rw[1], pp[6][0:1, :], 1.0 / D, 1e-5, ALU.mult, ALU.add))
        p.inc(V, "sV")
        sqrt_row(rw[2], rw[1])
        p.emit(V, lambda e: e.reciprocal(rw[0], rw[2]))
        p.inc(V, "sV")
        p.wait(V, "sV")
        p.wait(V, "sScr")
        p.wait(V, "sHfL")
        for ft in range(FT):
            hl = bb[:, ft * TC:(ft + 1) * TC]
            Ff = bc[:, ft * TC:(ft + 1) * TC]
            p.emit(V, lambda e, h=hl, F=Ff, ft=ft: e.scalar_tensor_tensor(
                h, F, carry_sb[:, ft:ft + 1], h, ALU.mult, ALU.add))
        p.inc(V, "sV")
        p.wait(S, "sV")
        for ft in range(FT):
            hb = bb[:, ft * TC:(ft + 1) * TC]
            dd = bc[:, ft * TC:(ft + 1) * TC]
            gb = ba[:, ft * TC:(ft + 1) * TC]
            p.emit(S, lambda e, h=hb, d=dd: e.activation(d, h, AF.Silu))
            p.inc(S, "sS")
            p.wait(V, "sS")
            p.emit(V, lambda e, g=gb, d=dd, ft=ft: e.scalar_tensor_tensor(
                d, g, nrm_sb["gn"][:, ft:ft + 1], d, ALU.mult, ALU.mult))
            p.inc(V, "sV")
            p.wait(S, "sV")
        acc_squares(bc)
        psum_row_of(sc[4], 5)
        p.wait(V, "sT")
        p.emit(V, lambda e: e.tensor_scalar(rw[1], pp[5][0:1, :], 1.0 / D, None, ALU.mult))
        vsync()
        p.emit(V, lambda e: e.tensor_tensor(rw[2], rw[0], rw[0], ALU.mult))
        p.emit(V, lambda e: e.tensor_tensor(rw[1], rw[1], rw[2], ALU.mult))
        p.emit(V, lambda e: e.tensor_scalar(rw[1], rw[1], 1e-8, None, ALU.add))
        p.inc(V, "sV")
        sqrt_row(rw[2], rw[1])
        p.emit(V, lambda e: e.reciprocal(rw[1], rw[2]))
        vsync()
        p.emit(V, lambda e: e.tensor_tensor(rw[2], rw[0], rw[1], ALU.mult))
        p.inc(V, "sV")
        bcast_mm(bcR, rw[2])
        maxes(bc, nrm_sb["o"])
        pmax_to_bcast(ch, 6, Mb)
        fac_deq(bcR, "wo", deqT, facT)
        svO = p.val("sV")
        qo = quant_to_bn(bc, nrm_sb["o"], bn)
        mm_evac("wo", FT, bn, 4, out_buf=ba, deq_tile=deqT,
                entry_sv=svO, rhs_base=qo)
        psum_row_of(hfacc, 6)
        p.wait(V, "sT")
        p.emit(V, lambda e: e.tensor_scalar(
            rw[3], pp[6][0:1, :], 1.0 / I, 1e-6, ALU.mult, ALU.add))
        p.inc(V, "sV")
        sqrt_row(rw[1], rw[3])
        p.emit(V, lambda e: e.reciprocal(rw[4], rw[1]))
        p.inc(V, "sV")
        bcast_mm(bcR, rw[4])
        p.emit(V, lambda e: e.tensor_copy(m1, mh))
        p.inc(V, "sV")
        pmax_to_bcast(ch, 3, Mb)
        fac_deq(bcR, "wdown", deqT, facT)
        p.wait(S, "sHf0")
        p.wait(S, "sHf1")
        p.wait(S, "sV")
        p.wait(S, "sT")
        ring = [sc[4], sc[5], m1, mh]
        slot_base = [p.val(f"sRl{j}") for j in range(4)]
        sv_base = p.val("sV")
        for k in range(FTI):
            if k >= 4:
                p.wait(S, "sV", sv_base + (k - 3))
            p.dma(ring[k % 4], hfscr[k * P:(k + 1) * P, :], f"sRl{k % 4}", eng=S)
            p.wait(V, f"sRl{k % 4}", slot_base[k % 4] + 16 * (k // 4 + 1))
            src = ring[k % 4]
            p.emit(V, lambda e, s=src, k=k: e.scalar_tensor_tensor(
                sc[6], s, lnw_sb[:, k:k + 1], facT, ALU.mult, ALU.mult))
            p.emit(V, lambda e, k=k: e.tensor_scalar(
                bnh[:, k * TC:(k + 1) * TC], sc[6], MAGIC, MAGIC, ALU.add, ALU.subtract))
            p.inc(V, "sV")
        def post_down(og, bank):
            for oc in range(4):
                ftile = og * 4 + oc
                xb = bx[:, ftile * TC:(ftile + 1) * TC]
                sb_ = ba[:, ftile * TC:(ftile + 1) * TC]
                p.emit(V, lambda e, b=bank, oc=oc: e.tensor_tensor(
                    sc[0], pp[b + oc], deqT, ALU.mult))
                p.inc(V, "sV")
                p.wait(S, "sV")
                p.emit(S, lambda e: e.activation(sc[1], sc[0], AF.Silu))
                p.inc(S, "sS")
                p.wait(V, "sS")
                otile = sc[3] if oc % 2 == 0 else sc[7]
                if og > 0 or oc > 1:
                    p.wait(V, f"sOut{oc % 2}")
                p.emit(V, lambda e, s=sb_: e.tensor_tensor(sc[2], s, sc[1], ALU.mult))
                p.emit(V, lambda e, x=xb, o=otile: e.tensor_tensor(o, x, sc[2], ALU.add))
                p.inc(V, "sV")
                p.dma_out(outT[ftile * P:(ftile + 1) * P, ch * TC:(ch + 1) * TC],
                          otile, f"sOut{oc % 2}")
        mm_evac("wdown", FTI, bnh, 4, post=post_down,
                entry_sv=sv_base, rhs_base=sv_base)
        p.wait(V, "sOut0")
        p.wait(V, "sOut1")

    def finish_debug():
        if SUBSTOP is None:
            for ft in range(FT):
                p.dma_out(outT[ft * P:(ft + 1) * P, 0:TC], bx[:, ft * TC:(ft + 1) * TC], "sOut")
        p.wait(E, "sOut")
        p.wait(E, "sScr")

    done = False
    if stop_after == "dumprow":
        p.wait(E, "sV")
        p.dma(outT[0:1, 0:8], partrow, "sOut")
        p.dma(outT[1:9, 0:8], partall, "sOut")
        p.dma(outT[9:10, 0:8], srow, "sOut")
        p.dma(outT[16:144, 0:8], swcol, "sOut")
        p.dma(outT[144:272, 0:8], swinv, "sOut")
        p.wait(E, "sOut")
        p.wait(E, "sScr")
        done = True
    if stop_after == "dumpwq":
        p.wait("gpsimd", "sCc")
        for ft in range(FT):
            p.bump("sOut", 16)
            p.emit("gpsimd", lambda e, ft=ft: e.dma_start(
                out=outT[ft * P:(ft + 1) * P, :],
                in_=wq["wi"][ft * P:(ft + 1) * P, 0:T]).then_inc(p.sems["sOut"], 16))
            p.streams["gpsimd"][-1] = (p.streams["gpsimd"][-1][0], True)
        p.wait(E, "sOut")
        p.wait(E, "sScr")
        done = True
    if stop_after == "prologue":
        p.wait(E, "sCc")
        p.dma_in(bx[:, 0:TC], xT[0:P, 0:TC], "sX")
        p.wait(E, "sX")
        finish_debug(); done = True
    if not done:
        for ch in range(NCH):
            section_A(ch, preloaded=(ch == 0))
            if stop_after == f"A{ch}":
                finish_debug(); done = True
                break
    if not done:
        carry_exchange()
        if stop_after == "carry":
            finish_debug(); done = True
        if stop_after == "dumpc":
            p.wait(E, "sV")
            p.wait(E, "sScr")
            p.dma(outT[0:P, 0:FT], hc, "sOut")
            p.dma(outT[0:P, FT:2 * FT], fc, "sOut")
            p.dma(outT[0:P, 2 * FT:3 * FT], carry_sb, "sOut")
            p.dma(outT[0:P, 3 * FT:3 * FT + 8 * FT], cslots, "sOut")
            p.dma(outT[0:P, 11 * FT:11 * FT + 8], csel_sb, "sOut")
            p.wait(E, "sOut")
            done = True
        if stop_after == "dumph2":
            p.wait(E, "sV")
            p.wait(E, "sScr")
            for ch in range(NCH):
                for ft in range(FT):
                    p.dma(bx[:, ft * TC:(ft + 1) * TC],
                          hscr[ft * P:(ft + 1) * P, ch * TC:(ch + 1) * TC], "sX")
                p.wait(E, "sX")
                for ft in range(FT):
                    p.dma(outT[ft * P:(ft + 1) * P, ch * TC:(ch + 1) * TC],
                          bx[:, ft * TC:(ft + 1) * TC], "sOut")
                p.wait(E, "sOut")
            done = True
    if not done:
        for ch in range(NCH):
            section_B(ch)
            if stop_after == f"B{ch}":
                finish_debug(); done = True
                break
    if not done:
        p.wait(E, "sOut")
        p.wait(E, "sScr")

    sem_names = ["sX", "sW", "sW0", "sW1", "sW2", "sW3", "sScr", "sBc",
                 "sOut", "sOut0", "sOut1", "sCc", "sV", "sS", "sT",
                 "sG", "sRow", "sT_mm", "sFb0", "sFb1", "sFb2", "sFb3",
                 "sHf0", "sHf1", "sHfL", "sXg", "sRl0", "sRl1", "sRl2", "sRl3"]
    with contextlib.ExitStack() as st:
        p.sems = {n_: st.enter_context(nc.semaphore(n_)) for n_ in sem_names}
        blk = st.enter_context(nc.Block())
        for eng_name in Prog.ENGS:
            stream = p.streams[eng_name]
            if not stream:
                continue
            getattr(blk, eng_name)(lambda eng, s=stream: [fn(eng) for fn, _ in s])
    return nc


def _run_on_device(stop_after, inputs):
    return None


def kernel(L0_attn_hidden_states, L0_attn_Wi, L0_attn_Wf, L0_attn_Wg, L0_attn_Wo,
           L0_attn_norm_i, L0_attn_norm_f, L0_attn_norm_g, L0_attn_norm_o,
           L0_attn_g_norm_w, L0_ffn_w_gate, L0_ffn_w_up, L0_ffn_w_down,
           L0_ffn_ln_weight):
    x = np.ascontiguousarray(np.asarray(L0_attn_hidden_states, dtype=np.float32))
    B, L, _ = x.shape
    wTf = {
        "wi": np.ascontiguousarray(np.asarray(L0_attn_Wi, dtype=np.float32).T),
        "wf": np.ascontiguousarray(np.asarray(L0_attn_Wf, dtype=np.float32).T),
        "wg": np.ascontiguousarray(np.asarray(L0_attn_Wg, dtype=np.float32).T),
        "wo": np.ascontiguousarray(np.asarray(L0_attn_Wo, dtype=np.float32).T),
        "wgate": np.ascontiguousarray(np.asarray(L0_ffn_w_gate, dtype=np.float32).T),
        "wup": np.ascontiguousarray(np.asarray(L0_ffn_w_up, dtype=np.float32).T),
        "wdown": np.ascontiguousarray(np.asarray(L0_ffn_w_down, dtype=np.float32).T),
    }
    nrm_np = {
        "i": np.ascontiguousarray(np.asarray(L0_attn_norm_i, np.float32).reshape(FT, P).T),
        "f": np.ascontiguousarray(np.asarray(L0_attn_norm_f, np.float32).reshape(FT, P).T),
        "g": np.ascontiguousarray(np.asarray(L0_attn_norm_g, np.float32).reshape(FT, P).T),
        "o": np.ascontiguousarray(np.asarray(L0_attn_norm_o, np.float32).reshape(FT, P).T),
        "gn": np.ascontiguousarray(np.asarray(L0_attn_g_norm_w, np.float32).reshape(FT, P).T),
    }
    lnw_np = np.ascontiguousarray(np.asarray(L0_ffn_ln_weight, np.float32).reshape(FTI, P).T)
    ident_np = np.eye(P, dtype=np.float32)

    nc = build_program()
    in_maps = []
    for c in range(8):
        seq, half = c // 2, c % 2
        xs = np.ascontiguousarray(x[seq, half * T:(half + 1) * T, :].T)
        csel_np = np.zeros((P, 8), np.float32)
        if half == 1:
            csel_np[:, c - 1] = 1.0
        m = {"xT": xs, "csel": csel_np, "ident": ident_np, "lnw": lnw_np}
        for k, v in nrm_np.items():
            m["nrm_" + k] = v
        for (w, K, O) in WEIGHTS:
            kper = K // 8
            m[w + "S"] = np.ascontiguousarray(wTf[w][c * kper * P:(c + 1) * kper * P, :])
        in_maps.append(m)
    res = run_bass_kernel_spmd(nc, in_maps, core_ids=list(range(8)))
    out = np.empty((B, L, D), np.float32)
    for c in range(8):
        seq, half = c // 2, c % 2
        out[seq, half * T:(half + 1) * T, :] = np.asarray(res.results[c]["outT"]).T
    return out


# version 19
# speedup vs baseline: 1.0088x; 1.0088x over previous
import contextlib
import numpy as np
import concourse.bass as bass
import concourse.mybir as mybir
from concourse.bass_utils import run_bass_kernel_spmd

F32 = mybir.dt.float32
BF16 = mybir.dt.bfloat16
AF = mybir.ActivationFunctionType
ALU = mybir.AluOpType
AXX = mybir.AxisListType.X

P = 128
D = 2048
I = 8192
T = 2048
TC = 512
NCH = T // TC
FT = D // P
FTI = I // P
MAGIC = 12582912.0

WEIGHTS = [("wi", FT, D), ("wf", FT, D), ("wg", FT, D),
           ("wgate", FT, I), ("wup", FT, I), ("wo", FT, D), ("wdown", FTI, D)]
WIDX = {w: j for j, (w, _, _) in enumerate(WEIGHTS)}
CC_OF = {w: 2 + j for j, (w, _, _) in enumerate(WEIGHTS)}


class Prog:
    ENGS = ("sync", "tensor", "vector", "scalar", "gpsimd")

    def __init__(self):
        self.streams = {e: [] for e in self.ENGS}
        self.ctr = {}
        self.sems = None

    def emit(self, eng, fn):
        self.streams[eng].append((fn, False))

    def bump(self, sem, amt):
        self.ctr[sem] = self.ctr.get(sem, 0) + amt
        return self.ctr[sem]

    def val(self, sem):
        return self.ctr.get(sem, 0)

    def wait(self, eng, sem, v=None):
        if v is None:
            v = self.val(sem)
        if v <= 0:
            return
        self.emit(eng, lambda e, s=sem, vv=v: e.wait_ge(self.sems[s], vv))
        self.streams[eng][-1] = (self.streams[eng][-1][0], True)

    def dma(self, dst, src, sem, slow=False, eng="sync"):
        self.bump(sem, 16)
        self.emit(eng, lambda e, d=dst, s=src, sm=sem, sl=slow:
                  e.dma_start(out=d, in_=s, allow_slow_non_contiguous=sl
                              ).then_inc(self.sems[sm], 16))
        self.streams[eng][-1] = (self.streams[eng][-1][0], True)

    def dma_in(self, dst, src, sem, slow=False, eng="sync"):
        self.wait(eng, "sV")
        self.wait(eng, "sS")
        self.dma(dst, src, sem, slow, eng)

    def dma_out(self, dst, src, sem, slow=False, eng="sync"):
        self.wait(eng, "sV")
        self.dma(dst, src, sem, slow, eng)

    def dma_split(self, dst, src, sem, nsplit, in_=False, eng="sync"):
        if in_:
            self.wait(eng, "sV")
            self.wait(eng, "sS")
        else:
            self.wait(eng, "sV")
        step = 128 // nsplit
        for j in range(nsplit):
            self.dma(dst[j * step:(j + 1) * step, :],
                     src[j * step:(j + 1) * step, :], sem, eng=eng)

    def inc(self, eng, sem, amt=1):
        self.bump(sem, amt)
        st = self.streams[eng]
        assert st, "inc with no prior instruction"
        prev, wrapped = st[-1]
        assert not wrapped, f"double inc on one instruction ({eng}/{sem})"
        st[-1] = (lambda e, f=prev, s=sem, a=amt: f(e).then_inc(self.sems[s], a),
                  True)


SUBSTOP = None
DELAY_TEST = False


def build_program(stop_after=None):
    nc = bass.Bass("TRN2", target_bir_lowering=False,
                   detect_race_conditions=False)
    p = Prog()
    E, TE, V, S, G = "sync", "tensor", "vector", "scalar", "gpsimd"

    din = lambda n, s, dt=F32: nc.dram_tensor(n, s, dt, kind="ExternalInput").ap()
    xT = din("xT", [D, T])
    wTs = {w: din(w + "S", [(K // 8) * P, O]) for (w, K, O) in WEIGHTS}
    nrm = {k: din("nrm_" + k, [P, FT]) for k in ("i", "f", "g", "o", "gn")}
    lnw_in = din("lnw", [P, FTI])
    csel_in = din("csel", [P, 8])
    ident_in = din("ident", [P, P])
    outT = nc.dram_tensor("outT", [D, T], F32, kind="ExternalOutput").ap()

    dint = lambda n, s, dt=F32, sp="Local": nc.dram_tensor(
        n, s, dt, kind="Internal", addr_space=sp).ap()
    slab = {w: dint("slab_" + w, [(K // 8) * P, O], BF16) for (w, K, O) in WEIGHTS}
    wq = {w: dint("wq_" + w, [K * P, O], BF16, "Shared") for (w, K, O) in WEIGHTS}
    part_box = dint("part_box", [8])
    part_gat = dint("part_gat", [8, 8], F32, "Shared")
    sw_box = dint("sw_box", [8])
    hscr = dint("hscr", [D, T])
    fscr = dint("fscr", [D, T])
    hfscr = dint("hfscr", [I, TC])
    flag_box = dint("flag_box", [8])
    flag_gat = dint("flag_gat", [8, 8], F32, "Shared")
    carry_box = dint("carry_box", [D])
    carry_gat = dint("carry_gat", [8, D], F32, "Shared")
    rowscr = dint("rowscr", [NCH, 8, TC])

    KB = 1024

    SB_BASE = 18 * KB

    def sb_at(name, shape, dt, off):
        return nc.alloc_sbuf_tensor_at(name, shape, dt, offset=SB_BASE + off).ap()

    bx = sb_at("bx", [P, FT * TC], F32, 0)
    ba = sb_at("ba", [P, FT * TC], F32, 32 * KB)
    bb = sb_at("bb", [P, FT * TC], F32, 64 * KB)
    bc = sb_at("bc", [P, FT * TC], F32, 96 * KB)
    bnh = sb_at("bnh", [P, FTI * TC], BF16, 64 * KB)
    bn = sb_at("bn", [P, FT * TC], BF16, 128 * KB)
    bnf = sb_at("bnf", [P, 8 * TC], F32, 128 * KB)
    off = 144 * KB
    sc = []
    for j in range(8):
        sc.append(sb_at(f"sc{j}", [P, TC], F32, off)); off += 2 * KB
    nm = {}
    for t_ in ("m1", "mh", "bcA", "bcB", "bcR", "facT", "deqT", "deq2T", "Mb",
               "hfacc", "bz", "mn"):
        nm[t_] = sb_at(t_, [P, TC], F32, off); off += 2 * KB
    m1, mh, bcA, bcB, bcR = nm["m1"], nm["mh"], nm["bcA"], nm["bcB"], nm["bcR"]
    facT, deqT, deq2T, Mb, hfacc, bz = (nm["facT"], nm["deqT"], nm["deq2T"],
                                        nm["Mb"], nm["hfacc"], nm["bz"])
    mntl = nm["mn"]
    wt = []
    for j in range(4):
        wt.append(sb_at(f"wt{j}", [P, TC], BF16, off)); off += 1 * KB
    u3 = sb_at("u3", [P, TC], F32, off); off += 2 * KB
    nrm_sb = {}
    for k in ("i", "f", "g", "o", "gn"):
        nrm_sb[k] = sb_at("nsb_" + k, [P, FT], F32, off); off += 64
    lnw_sb = sb_at("lnw_sb", [P, FTI], F32, off); off += 256
    ident = sb_at("ident_sb", [P, P], F32, off); off += 512
    onecol = sb_at("onecol", [P, 1], F32, off); off += 32
    onesrow = sb_at("onesrow", [1, P], F32, off); off += 512
    hc = sb_at("hc", [P, FT], F32, off); off += 64
    fc = sb_at("fc", [P, FT], F32, off); off += 64
    carry_sb = sb_at("carry_sb", [P, FT], F32, off); off += 64
    cslots = sb_at("cslots", [P, 8 * FT], F32, off); off += 512
    csel_sb = sb_at("csel_sb", [P, 8], F32, off); off += 32
    swcol = sb_at("swcol", [P, 8], F32, off); off += 32
    swinv = sb_at("swinv", [P, 8], F32, off); off += 32
    rw = []
    for j in range(6):
        rw.append(sb_at(f"rw{j}", [1, TC], F32, off)); off += 2 * KB
    packrow = sb_at("packrow", [4, P], F32, off); off += 512
    mp = sb_at("mp", [P, 4], F32, off); off += 32
    partrow = sb_at("partrow", [1, 8], F32, off); off += 32
    partall = sb_at("partall", [8, 8], F32, off); off += 32
    srow = sb_at("srow", [1, 8], F32, off); off += 32
    assert off <= 207 * KB, f"SBUF over budget: {off}"

    pp = [nc.alloc_psum_tensor(f"pp{j}", [P, TC], F32).ap() for j in range(8)]

    def vsync():
        st_ = p.streams["vector"]
        fn, wrapped = st_[-1]
        if not wrapped:
            p.bump("sV", 1)
            st_[-1] = (lambda e, f=fn: f(e).then_inc(p.sems["sV"], 1), True)
        p.wait("vector", "sV")

    def bcast_mm(dst_tile, row_sbuf):
        p.wait(TE, "sV")
        p.emit(TE, lambda e, r=row_sbuf: e.matmul(
            pp[7], onesrow, r, start=True, stop=True))
        p.inc(TE, "sT")
        p.wait(V, "sT")
        p.emit(V, lambda e, d=dst_tile: e.tensor_copy(d, pp[7]))
        vsync()

    def pmax_to_bcast(ch, slot, dst_tile):
        p.wait(TE, "sV")
        for j in range(4):
            p.emit(TE, lambda e, j=j: e.transpose(
                pp[7][:, j * P:(j + 1) * P], m1[:, j * P:(j + 1) * P], ident))
        p.inc(TE, "sT")
        p.wait(V, "sT")
        for j in range(4):
            p.emit(V, lambda e, j=j: e.reduce_max(
                mp[:, j:j + 1], pp[7][:, j * P:(j + 1) * P], axis=AXX))
        p.inc(V, "sV")
        p.wait(TE, "sV")
        p.emit(TE, lambda e: e.transpose(pp[7][0:4, 0:P], mp, ident))
        p.inc(TE, "sT")
        p.wait(V, "sT")
        p.emit(V, lambda e: e.tensor_copy(packrow, pp[7][0:4, 0:P]))
        p.inc(V, "sV")
        dst_row = rowscr[ch, slot, :]
        dstap = bass.AP(dst_row.tensor, dst_row.offset, [[P, 4], [1, P]])
        p.dma_out(dstap, packrow, "sRow")
        p.wait(E, "sRow")
        p.wait(E, "sT")
        p.dma_in(rw[2], dst_row, "sBc")
        p.wait(TE, "sBc")
        bcast_mm(dst_tile, rw[2])

    def psum_row_of(acc_tile, bank):
        p.wait(TE, "sV")
        p.emit(TE, lambda e, a=acc_tile, b=bank: e.matmul(
            pp[b][0:1, :], onecol, a, start=True, stop=True))
        p.inc(TE, "sT")

    def sqrt_row(dst, srcr):
        p.wait(S, "sV")
        p.emit(S, lambda e, d=dst, r=srcr: e.activation(d, r, AF.Sqrt))
        p.inc(S, "sS")
        p.wait(V, "sS")

    def acc_squares(in_buf, nft=FT):
        for ft in range(nft):
            src = in_buf[:, ft * TC:(ft + 1) * TC]
            p.emit(V, lambda e, s=src: e.tensor_tensor(sc[0], s, s, ALU.mult))
            if ft == 0:
                p.emit(V, lambda e: e.tensor_copy(sc[4], sc[0]))
            else:
                p.emit(V, lambda e: e.tensor_tensor(sc[4], sc[4], sc[0], ALU.add))
        p.inc(V, "sV")

    def maxes(in_buf, nrm_tile):
        p.emit(V, lambda e: e.memset(m1, 0.0))
        p.inc(V, "sV")
        p.wait(S, "sV")
        for ft in range(FT):
            src = in_buf[:, ft * TC:(ft + 1) * TC]
            sq = sc[0] if ft % 2 == 0 else sc[1]
            if nrm_tile is not None:
                col = nrm_tile[:, ft:ft + 1]
                p.emit(S, lambda e, s=src, c=col, q=sq: e.activation(
                    q, s, AF.Square, scale=c))
            else:
                p.emit(S, lambda e, s=src, q=sq: e.activation(q, s, AF.Square))
            p.inc(S, "sS")
            p.wait(V, "sS")
            p.emit(V, lambda e, q=sq: e.tensor_tensor(m1, m1, q, ALU.max))
            p.inc(V, "sV")
            if ft >= 1:
                p.wait(S, "sV", p.val("sV") - 1)
        p.wait(V, "sV")

    def fac_deq(rfac_tile, w_name, deq_dst, fac_dst, w2_name=None, deq2_dst=None):
        wi = WIDX[w_name]
        p.wait(S, "sV")
        p.wait(S, "sBc")
        p.emit(S, lambda e: e.activation(mntl, Mb, AF.Sqrt))
        p.inc(S, "sS")
        p.wait(V, "sS")
        if rfac_tile is not None:
            p.emit(V, lambda e: e.tensor_tensor(sc[2], mntl, rfac_tile, ALU.mult))
        else:
            p.emit(V, lambda e: e.tensor_copy(sc[2], mntl))
        p.emit(V, lambda e: e.tensor_scalar(sc[2], sc[2], 1e-5, None, ALU.max))
        p.emit(V, lambda e: e.reciprocal(sc[3], sc[2]))
        vsync()
        if rfac_tile is not None:
            p.emit(V, lambda e, f=fac_dst: e.scalar_tensor_tensor(
                f, sc[3], 127.0, rfac_tile, ALU.mult, ALU.mult))
        else:
            p.emit(V, lambda e, f=fac_dst: e.tensor_scalar(
                f, sc[3], 127.0, None, ALU.mult))
        p.emit(V, lambda e, d=deq_dst, wi=wi: e.tensor_scalar(
            d, sc[2], swcol[:, wi:wi + 1], 1.0 / 127.0, ALU.mult, ALU.mult))
        if w2_name is not None:
            wi2 = WIDX[w2_name]
            p.emit(V, lambda e, d=deq2_dst, wi=wi2: e.tensor_scalar(
                d, sc[2], swcol[:, wi:wi + 1], 1.0 / 127.0, ALU.mult, ALU.mult))
        p.inc(V, "sV")

    def quant_to_bn(in_buf, nrm_tile, out_bf):
        base = p.val("sV")
        for ft in range(FT):
            src = in_buf[:, ft * TC:(ft + 1) * TC]
            dstn = out_bf[:, ft * TC:(ft + 1) * TC]
            if nrm_tile is not None:
                col = nrm_tile[:, ft:ft + 1]
                p.emit(V, lambda e, s=src, c=col: e.scalar_tensor_tensor(
                    sc[0], s, c, facT, ALU.mult, ALU.mult))
            else:
                p.emit(V, lambda e, s=src: e.tensor_tensor(sc[0], s, facT, ALU.mult))
            p.emit(V, lambda e, d=dstn: e.tensor_scalar(
                d, sc[0], MAGIC, MAGIC, ALU.add, ALU.subtract))
            p.inc(V, "sV")
        return base

    def mm_accum(wname, K, og, bank, rhs_bf, rhs_ready=None):
        slot_base = [p.val(f"sW{s}") for s in range(4)]
        mm_base = p.val("sT_mm")
        p.wait(E, "sCc", CC_OF[wname])
        for k in range(K):
            p.wait(E, "sT_mm", mm_base + max(k - 3, 0))
            src = wq[wname][k * P:(k + 1) * P, og * 512:(og + 1) * 512]
            p.dma(wt[k % 4], src, f"sW{k % 4}")
        for k in range(K):
            if rhs_ready is not None:
                p.wait(TE, "sV", rhs_ready + k + 1)
            p.wait(TE, f"sW{k % 4}", slot_base[k % 4] + 16 * (k // 4 + 1))
            for oc in range(4):
                p.emit(TE, lambda e, k=k, oc=oc, b=bank: e.matmul(
                    pp[b + oc], wt[k % 4][:, oc * P:(oc + 1) * P],
                    rhs_bf[:, k * TC:(k + 1) * TC],
                    start=(k == 0), stop=(k == K - 1)))
            p.inc(TE, "sT_mm")

    def mm_evac(wname, K, rhs_bf, n_og, out_buf=None, deq_tile=None, post=None,
                entry_sv=None, rhs_base=None):
        p.wait(TE, "sV", entry_sv if entry_sv is not None else p.val("sV"))
        ev_after = []
        for og in range(n_og):
            bank = (og % 2) * 4
            if og >= 2:
                p.wait(TE, "sV", ev_after[og - 2])
            mm_accum(wname, K, og, bank, rhs_bf,
                     rhs_ready=(rhs_base if og == 0 else None))
            p.wait(V, "sT_mm")
            if post is None:
                for oc in range(4):
                    ftile = og * 4 + oc
                    dst = out_buf[:, ftile * TC:(ftile + 1) * TC]
                    p.emit(V, lambda e, b=bank, oc=oc, d=dst, q=deq_tile:
                           e.tensor_tensor(d, pp[b + oc], q, ALU.mult))
                p.inc(V, "sV")
                p.wait(V, "sV")
            else:
                post(og, bank)
            ev_after.append(p.val("sV"))

    p.dma(ident, ident_in, "sX")
    for k in ("i", "f", "g", "o", "gn"):
        p.dma(nrm_sb[k], nrm[k], "sX")
    p.dma(lnw_sb, lnw_in, "sX")
    p.dma(csel_sb, csel_in, "sX")
    p.emit(V, lambda e: e.memset(onecol, 1.0))
    p.emit(V, lambda e: e.memset(onesrow, 1.0))
    p.emit(V, lambda e: e.memset(partrow, 0.0))
    p.inc(V, "sV")
    p.dma_out(bass.AP(flag_box.tensor, 0, [[8, 1], [1, 8]]), partrow, "sScr")
    p.emit(V, lambda e: e.memset(srow, 0.0))
    p.emit(V, lambda e: e.memset(bz, 0.0))
    p.emit(V, lambda e: e.memset(hc, 0.0))
    p.emit(V, lambda e: e.memset(fc, 1.0))
    p.inc(V, "sV")

    for widx, (w, K, O) in enumerate(WEIGHTS):
        kper = K // 8
        for j in range(kper):
            p.dma_split(bx[:, 0:O], wTs[w][j * P:(j + 1) * P, :], "sX", 1,
                        in_=True)
            p.wait(V, "sX")
            p.wait(V, "sT")
            nseg = O // TC
            for sgi in range(nseg):
                seg = bx[:, sgi * TC:(sgi + 1) * TC]
                p.emit(V, lambda e, s=seg: e.tensor_scalar(
                    sc[2], s, -1.0, None, ALU.mult))
                p.emit(V, lambda e, s=seg: e.tensor_tensor(
                    sc[2], sc[2], s, ALU.max))
                p.emit(V, lambda e, sgi=sgi: e.reduce_sum(
                    sc[4][:, sgi:sgi + 1], sc[2], axis=AXX))
            vsync()
            p.emit(V, lambda e, n=nseg: e.reduce_sum(
                sc[5][:, 0:1], sc[4][:, 0:n], axis=AXX))
            p.inc(V, "sV")
            p.wait(TE, "sV")
            p.emit(TE, lambda e, f=(j == 0), l=(j == kper - 1): e.matmul(
                pp[6][0:1, 0:1], onecol, sc[5][:, 0:1], start=f, stop=l))
            p.inc(TE, "sT")
        p.wait(V, "sT")
        p.emit(V, lambda e, wi=widx: e.tensor_copy(partrow[:, wi:wi + 1],
                                                   pp[6][0:1, 0:1]))
        p.inc(V, "sV")
    p.dma_out(bass.AP(part_box.tensor, 0, [[8, 1], [1, 8]]), partrow, "sScr")
    p.wait(G, "sScr")
    p.bump("sCc", 1)
    p.emit(G, lambda e: e.collective_compute(
        "AllGather", ALU.bypass, replica_groups=[list(range(8))],
        ins=[part_box], outs=[part_gat]).then_inc(p.sems["sCc"], 1))
    p.wait(E, "sCc", 1)
    p.dma_in(partall, part_gat, "sScr")
    p.wait(TE, "sScr")
    p.wait(TE, "sV")
    p.emit(TE, lambda e: e.matmul(pp[6][0:1, 0:8], onecol[0:8, :], partall,
                                  start=True, stop=True))
    p.inc(TE, "sT")
    p.wait(V, "sT")
    p.emit(V, lambda e: e.tensor_copy(srow, pp[6][0:1, 0:8]))
    vsync()
    for widx, (w, K, O) in enumerate(WEIGHTS):
        numel = float(K * P * O)
        p.emit(V, lambda e, wi=widx, n=numel: e.tensor_scalar(
            srow[:, wi:wi + 1], srow[:, wi:wi + 1], 1.0 / n, 1e-5, ALU.mult, ALU.max))
    p.inc(V, "sV")
    p.dma_out(bass.AP(sw_box.tensor, 0, [[8, 1], [1, 8]]), srow, "sScr")
    p.wait(E, "sScr")
    p.dma_in(swcol, bass.AP(sw_box.tensor, sw_box.offset, [[0, P], [1, 8]]), "sBc")
    p.wait(V, "sBc")
    for widx in range(len(WEIGHTS)):
        p.emit(V, lambda e, wi=widx: e.reciprocal(swinv[:, wi:wi + 1],
                                                  swcol[:, wi:wi + 1]))
    p.inc(V, "sV")
    p.wait(V, "sV")
    for widx, (w, K, O) in enumerate(WEIGHTS):
        kper = K // 8
        for j in range(kper):
            p.dma_split(bx[:, 0:O], wTs[w][j * P:(j + 1) * P, :], "sX", 1,
                        in_=True)
            p.wait(V, "sX")
            p.wait(V, "sScr")
            nseg = O // TC
            for sgi in range(nseg):
                seg = bx[:, sgi * TC:(sgi + 1) * TC]
                slot = sgi % FT
                p.emit(V, lambda e, s=seg, wi=widx: e.tensor_scalar(
                    sc[7], s, swinv[:, wi:wi + 1], MAGIC, ALU.mult, ALU.add))
                p.emit(V, lambda e: e.tensor_scalar(
                    sc[7], sc[7], MAGIC + 1.0, MAGIC - 1.0, ALU.min, ALU.max))
                p.emit(V, lambda e, sl=slot: e.tensor_scalar(
                    bn[:, sl * TC:(sl + 1) * TC], sc[7], MAGIC, None, ALU.subtract))
                p.inc(V, "sV")
                p.dma_split(slab[w][j * P:(j + 1) * P, sgi * TC:(sgi + 1) * TC],
                            bn[:, slot * TC:(slot + 1) * TC], "sScr", 1)
        p.wait(G, "sScr")
        p.bump("sCc", 1)
        p.emit(G, lambda e, w=w: e.collective_compute(
            "AllGather", ALU.bypass, replica_groups=[list(range(8))],
            ins=[slab[w]], outs=[wq[w]]).then_inc(p.sems["sCc"], 1))
    p.bump("sCc", 1)
    p.emit(G, lambda e: e.collective_compute(
        "AllGather", ALU.bypass, replica_groups=[list(range(8))],
        ins=[flag_box], outs=[flag_gat]).then_inc(p.sems["sCc"], 1))
    CC_BARRIER = p.val("sCc")

    if DELAY_TEST:
        for _ in range(1500):
            p.emit(V, lambda e: e.tensor_copy(sc[0], sc[1]))
        p.inc(V, "sV")
        p.wait(E, "sV")

    def load_x(ch):
        for ft in range(FT):
            p.dma_split(bx[:, ft * TC:(ft + 1) * TC],
                        xT[ft * P:(ft + 1) * P, ch * TC:(ch + 1) * TC],
                        "sXg", 1, in_=True, eng=S)

    def ln_stats(ch):
        p.wait(V, "sXg")
        for j in range(4):
            p.wait(V, f"sFb{j}")
        for ft in range(FT):
            src = bx[:, ft * TC:(ft + 1) * TC]
            if ft == 0:
                p.emit(V, lambda e, s=src: e.tensor_copy(sc[5], s))
            else:
                p.emit(V, lambda e, s=src: e.tensor_tensor(sc[5], sc[5], s, ALU.add))
        p.inc(V, "sV")
        acc_squares(bx)
        psum_row_of(sc[5], 6)
        psum_row_of(sc[4], 5)
        p.wait(V, "sT")
        p.emit(V, lambda e: e.tensor_scalar(rw[0], pp[6][0:1, :], 1.0 / D, None, ALU.mult))
        p.emit(V, lambda e: e.tensor_scalar(rw[1], pp[5][0:1, :], 1.0 / D, None, ALU.mult))
        vsync()
        p.emit(V, lambda e: e.tensor_tensor(rw[2], rw[0], rw[0], ALU.mult))
        p.emit(V, lambda e: e.tensor_tensor(rw[2], rw[1], rw[2], ALU.subtract))
        p.emit(V, lambda e: e.tensor_scalar(rw[1], rw[2], 1e-5, None, ALU.add))
        p.inc(V, "sV")
        sqrt_row(rw[4], rw[1])
        p.emit(V, lambda e: e.reciprocal(rw[3], rw[4]))
        vsync()
        p.emit(V, lambda e: e.tensor_tensor(rw[4], rw[0], rw[3], ALU.mult))
        p.emit(V, lambda e: e.tensor_scalar(rw[4], rw[4], -1.0, None, ALU.mult))
        p.emit(V, lambda e: e.reciprocal(rw[5], rw[1]))
        vsync()
        p.emit(V, lambda e: e.tensor_tensor(rw[5], rw[2], rw[5], ALU.mult))
        p.emit(V, lambda e: e.tensor_scalar(rw[5], rw[5], 1e-8, None, ALU.add))
        p.inc(V, "sV")
        sqrt_row(rw[1], rw[5])
        p.emit(V, lambda e: e.reciprocal(rw[5], rw[1]))
        p.inc(V, "sV")
        p.dma_out(rowscr[ch, 0, :], rw[3], "sRow")
        p.dma_out(rowscr[ch, 1, :], rw[4], "sRow")
        p.dma_out(rowscr[ch, 2, :], rw[5], "sRow")
        bcast_mm(bcA, rw[3])
        bcast_mm(bcB, rw[4])
        bcast_mm(bcR, rw[5])

    def make_xn(dst_buf):
        p.wait(V, "sScr")
        p.wait(V, "sHsc")
        for ft in range(FT):
            x_ = bx[:, ft * TC:(ft + 1) * TC]
            d_ = dst_buf[:, ft * TC:(ft + 1) * TC]
            p.emit(V, lambda e, x=x_, d=d_: e.tensor_tensor(d, x, bcA, ALU.mult))
            p.emit(V, lambda e, d=d_: e.tensor_tensor(d, d, bcB, ALU.add))
        p.inc(V, "sV")

    def section_A(ch):
        load_x(ch)
        if SUBSTOP == "load":
            return
        ln_stats(ch)
        if SUBSTOP == "ln":
            return
        make_xn(bc)
        maxes(bc, nrm_sb["i"])
        if SUBSTOP == "max":
            return
        pmax_to_bcast(ch, 3, Mb)
        if SUBSTOP == "pmax":
            return
        fac_deq(bcR, "wi", deqT, facT)
        svA = p.val("sV")
        qi = quant_to_bn(bc, nrm_sb["i"], bn)
        if SUBSTOP == "quant":
            return
        maxes(bc, nrm_sb["f"])
        if isinstance(SUBSTOP, str) and SUBSTOP.startswith("mm_og"):
            n = int(SUBSTOP[5:])
            mm_evac("wi", FT, bn, n, out_buf=ba, deq_tile=deqT)
            return
        if SUBSTOP == "mm_noev":
            ev_entry = p.val("sV_ev")
            p.wait(TE, "sV")
            mm_accum("wi", FT, 0, 0, bn)
            p.wait(V, "sT")
            p.inc(V, "sV")
            return
        if SUBSTOP == "mm_1k":
            p.wait(TE, "sV")
            sW_base = p.val("sW")
            p.wait(E, "sCc", CC_OF["wi"])
            p.dma(wt[0], wq["wi"][0:P, 0:512], "sW")
            p.wait(TE, "sW", sW_base + 16)
            for oc in range(4):
                p.emit(TE, lambda e, oc=oc: e.matmul(
                    pp[oc], wt[0][:, oc * P:(oc + 1) * P], bn[:, 0:TC],
                    start=True, stop=True))
            p.inc(TE, "sT")
            p.wait(V, "sT")
            for oc in range(4):
                p.emit(V, lambda e, oc=oc: e.tensor_tensor(
                    ba[:, oc * TC:(oc + 1) * TC], pp[oc], deqT, ALU.mult))
            p.inc(V, "sV")
            return
        mm_evac("wi", FT, bn, 4, out_buf=ba, deq_tile=deqT,
                entry_sv=svA, rhs_base=qi)
        if SUBSTOP == "mmi":
            p.wait(E, "sV")
            for ft in range(FT):
                p.dma(outT[ft * P:(ft + 1) * P, 0:TC], ba[:, ft * TC:(ft + 1) * TC], "sOut")
            p.wait(E, "sOut")
            return
        pmax_to_bcast(ch, 4, Mb)
        fac_deq(bcR, "wf", deqT, facT)
        svF = p.val("sV")
        qf = quant_to_bn(bc, nrm_sb["f"], bn)

        def post_f(og, bank):
            for oc in range(4):
                ftile = og * 4 + oc
                d = bb[:, ftile * TC:(ftile + 1) * TC]
                ia = ba[:, ftile * TC:(ftile + 1) * TC]
                p.emit(V, lambda e, b=bank, oc=oc: e.tensor_tensor(
                    sc[0], pp[b + oc], deqT, ALU.mult))
                p.inc(V, "sV")
                p.wait(S, "sV")
                p.emit(S, lambda e, d=d: e.activation(d, sc[0], AF.Sigmoid))
                p.inc(S, "sS")
                p.emit(S, lambda e, s=ia: e.activation(sc[1], s, AF.Silu))
                p.inc(S, "sS")
                p.wait(V, "sS")
                p.emit(V, lambda e, f=d: e.tensor_scalar(
                    sc[2], f, -1.0, 1.0, ALU.mult, ALU.add))
                p.emit(V, lambda e, d2=ia: e.tensor_tensor(d2, sc[1], sc[2], ALU.mult))
                p.inc(V, "sV")
        mm_evac("wf", FT, bn, 4, post=post_f, entry_sv=svF, rhs_base=qf)
        if SUBSTOP == "mmf":
            return
        for ft in range(FT):
            fb = bb[:, ft * TC:(ft + 1) * TC]
            ib = ba[:, ft * TC:(ft + 1) * TC]
            hd = bc[:, ft * TC:(ft + 1) * TC]
            p.emit(V, lambda e, f=fb, i=ib, d=hd, ft=ft: e.tensor_tensor_scan(
                d, f, i, hc[:, ft:ft + 1], ALU.mult, ALU.add))
            vsync()
            p.emit(V, lambda e, d=hd, ft=ft: e.tensor_copy(
                hc[:, ft:ft + 1], d[:, TC - 1:TC]))
            p.inc(V, "sV")
            p.dma_split(hscr[ft * P:(ft + 1) * P, ch * TC:(ch + 1) * TC], hd,
                        "sHsc", 1, eng=G)
            fs = sc[1 + (ft % 4)]
            if ft >= 4:
                p.wait(V, f"sFb{ft % 4}")
            p.emit(V, lambda e, f=fb, d=fs, ft=ft: e.tensor_tensor_scan(
                d, f, bz, fc[:, ft:ft + 1], ALU.mult, ALU.add))
            vsync()
            p.emit(V, lambda e, d=fs, ft=ft: e.tensor_copy(
                fc[:, ft:ft + 1], d[:, TC - 1:TC]))
            p.inc(V, "sV")
            p.dma_split(fscr[ft * P:(ft + 1) * P, ch * TC:(ch + 1) * TC], fs,
                        f"sFb{ft % 4}", 1, eng=S)

    carry_cc = [None]

    def carry_exchange():
        dst = bass.AP(carry_box.tensor, carry_box.offset, [[FT, P], [1, FT]])
        p.dma_out(dst, hc, "sScr")
        p.wait(G, "sScr")
        p.bump("sCc", 1)
        p.emit(G, lambda e: e.collective_compute(
            "AllGather", ALU.bypass, replica_groups=[list(range(8))],
            ins=[carry_box], outs=[carry_gat]).then_inc(p.sems["sCc"], 1))
        p.bump("sCc", 1)
        carry_cc[0] = p.val("sCc")
        p.emit(G, lambda e: e.collective_compute(
            "AllGather", ALU.bypass, replica_groups=[list(range(8))],
            ins=[flag_box], outs=[flag_gat]).then_inc(p.sems["sCc"], 1))

    def carry_fold():
        p.wait(E, "sCc", carry_cc[0])
        for s_ in range(8):
            src = bass.AP(carry_gat.tensor, carry_gat.offset + s_ * D,
                          [[FT, P], [1, FT]])
            p.dma(cslots[:, s_ * FT:(s_ + 1) * FT], src, "sScr")
        p.wait(V, "sScr")
        p.emit(V, lambda e: e.memset(carry_sb, 0.0))
        for s_ in range(8):
            p.emit(V, lambda e, s=s_: e.scalar_tensor_tensor(
                carry_sb, cslots[:, s * FT:(s + 1) * FT], csel_sb[:, s:s + 1],
                carry_sb, ALU.mult, ALU.add))
        p.inc(V, "sV")

    def section_B(ch):
        load_x(ch)
        p.wait(E, "sRow")
        p.dma_in(rw[3], rowscr[ch, 0, :], "sBc")
        p.dma_in(rw[4], rowscr[ch, 1, :], "sBc")
        p.dma_in(rw[5], rowscr[ch, 2, :], "sBc")
        p.wait(TE, "sBc")
        bcast_mm(bcA, rw[3])
        bcast_mm(bcB, rw[4])
        bcast_mm(bcR, rw[5])
        p.wait(V, "sXg")
        make_xn(bc)
        maxes(bc, nrm_sb["g"])
        pmax_to_bcast(ch, 3, Mb)
        fac_deq(bcR, "wg", deqT, facT)
        svG = p.val("sV")
        qg = quant_to_bn(bc, nrm_sb["g"], bn)
        maxes(bc, None)
        mm_evac("wg", FT, bn, 4, out_buf=ba, deq_tile=deqT,
                entry_sv=svG, rhs_base=qg)
        pmax_to_bcast(ch, 4, Mb)
        fac_deq(None, "wgate", deqT, facT, w2_name="wup", deq2_dst=deq2T)
        svU = p.val("sV")
        qffn = quant_to_bn(bc, None, bn)
        p.wait(S, "sScr")
        p.wait(S, "sHsc")
        for j in range(4):
            p.wait(S, f"sFb{j}")
        for ft in range(FT):
            p.dma_split(bb[:, ft * TC:(ft + 1) * TC],
                        hscr[ft * P:(ft + 1) * P, ch * TC:(ch + 1) * TC],
                        "sHfL", 1, in_=True, eng=S)
            p.dma_split(bc[:, ft * TC:(ft + 1) * TC],
                        fscr[ft * P:(ft + 1) * P, ch * TC:(ch + 1) * TC],
                        "sHfL", 1, in_=True, eng=S)
        p.emit(V, lambda e: e.memset(hfacc, 0.0))
        p.emit(V, lambda e: e.memset(mh, 0.0))
        p.inc(V, "sV")
        p.wait(S, "sOut0")
        p.wait(S, "sOut1")
        p.wait(TE, "sV", svU)
        n_og = I // 512
        ut = [sc[1], sc[2], sc[3], u3]
        ev_g = []
        ev_u = []
        for og in range(n_og):
            if og >= 1:
                p.wait(TE, "sV", ev_g[og - 1])
            mm_accum("wgate", FT, og, 0, bn,
                     rhs_ready=(qffn if og == 0 else None))
            g_done = p.val("sT_mm")
            p.wait(V, "sT_mm", g_done)
            for oc in range(4):
                gsc = sc[0] if oc % 2 == 0 else sc[6]
                if og > 0 or oc > 1:
                    p.wait(V, "sS", p.val("sS") - 1)
                p.emit(V, lambda e, oc=oc, g=gsc: e.tensor_tensor(
                    g, pp[0 + oc], deqT, ALU.mult))
                p.inc(V, "sV")
                p.wait(S, "sV")
                p.emit(S, lambda e, g=gsc, u=ut[oc]: e.activation(u, g, AF.Silu))
                p.inc(S, "sS")
            if og >= 1:
                p.wait(TE, "sV", ev_u[og - 1])
            mm_accum("wup", FT, og, 4, bn)
            u_done = p.val("sT_mm")
            ev_g.append(p.val("sV"))
            p.wait(V, "sT_mm", u_done)
            p.wait(V, "sS")
            for oc in range(4):
                ioc = og * 4 + oc
                p.emit(V, lambda e, oc=oc: e.tensor_tensor(
                    sc[0], pp[4 + oc], deq2T, ALU.mult))
                vsync()
                hftile = sc[4 + (oc % 2)]
                if og > 0 or oc > 1:
                    p.wait(V, f"sHf{oc % 2}")
                p.emit(V, lambda e, h=hftile, u=ut[oc]: e.tensor_tensor(
                    h, u, sc[0], ALU.mult))
                p.emit(V, lambda e, h=hftile: e.tensor_tensor(sc[6], h, h, ALU.mult))
                p.emit(V, lambda e: e.tensor_tensor(hfacc, hfacc, sc[6], ALU.add))
                p.inc(V, "sV")
                p.wait(S, "sV")
                p.emit(S, lambda e, h=hftile, i=ioc: e.activation(
                    sc[7], h, AF.Square, scale=lnw_sb[:, i:i + 1]))
                p.inc(S, "sS")
                p.wait(V, "sS")
                p.emit(V, lambda e: e.tensor_tensor(mh, mh, sc[7], ALU.max))
                p.inc(V, "sV")
                p.dma_split(hfscr[ioc * P:(ioc + 1) * P, :], hftile,
                            f"sHf{oc % 2}", 1, eng=S)
            ev_u.append(p.val("sV"))
        p.wait(V, "sScr")
        p.wait(V, "sHf0")
        p.wait(V, "sHf1")
        acc_squares(ba)
        psum_row_of(sc[4], 6)
        p.wait(V, "sT")
        p.emit(V, lambda e: e.tensor_scalar(
            rw[1], pp[6][0:1, :], 1.0 / D, 1e-5, ALU.mult, ALU.add))
        p.inc(V, "sV")
        sqrt_row(rw[2], rw[1])
        p.emit(V, lambda e: e.reciprocal(rw[0], rw[2]))
        p.inc(V, "sV")
        p.wait(V, "sV")
        if ch == 0:
            carry_fold()
        p.wait(V, "sScr")
        p.wait(V, "sHfL")
        for ft in range(FT):
            hl = bb[:, ft * TC:(ft + 1) * TC]
            Ff = bc[:, ft * TC:(ft + 1) * TC]
            p.emit(V, lambda e, h=hl, F=Ff, ft=ft: e.scalar_tensor_tensor(
                h, F, carry_sb[:, ft:ft + 1], h, ALU.mult, ALU.add))
        p.inc(V, "sV")
        p.wait(S, "sV")
        for ft in range(FT):
            hb = bb[:, ft * TC:(ft + 1) * TC]
            dd = bc[:, ft * TC:(ft + 1) * TC]
            gb = ba[:, ft * TC:(ft + 1) * TC]
            p.emit(S, lambda e, h=hb, d=dd: e.activation(d, h, AF.Silu))
            p.inc(S, "sS")
            p.wait(V, "sS")
            p.emit(V, lambda e, g=gb, d=dd, ft=ft: e.scalar_tensor_tensor(
                d, g, nrm_sb["gn"][:, ft:ft + 1], d, ALU.mult, ALU.mult))
            p.inc(V, "sV")
            p.wait(S, "sV")
        acc_squares(bc)
        psum_row_of(sc[4], 5)
        p.wait(V, "sT")
        p.emit(V, lambda e: e.tensor_scalar(rw[1], pp[5][0:1, :], 1.0 / D, None, ALU.mult))
        vsync()
        p.emit(V, lambda e: e.tensor_tensor(rw[2], rw[0], rw[0], ALU.mult))
        p.emit(V, lambda e: e.tensor_tensor(rw[1], rw[1], rw[2], ALU.mult))
        p.emit(V, lambda e: e.tensor_scalar(rw[1], rw[1], 1e-8, None, ALU.add))
        p.inc(V, "sV")
        sqrt_row(rw[2], rw[1])
        p.emit(V, lambda e: e.reciprocal(rw[1], rw[2]))
        vsync()
        p.emit(V, lambda e: e.tensor_tensor(rw[2], rw[0], rw[1], ALU.mult))
        p.inc(V, "sV")
        bcast_mm(bcR, rw[2])
        maxes(bc, nrm_sb["o"])
        pmax_to_bcast(ch, 6, Mb)
        fac_deq(bcR, "wo", deqT, facT)
        svO = p.val("sV")
        qo = quant_to_bn(bc, nrm_sb["o"], bn)
        mm_evac("wo", FT, bn, 4, out_buf=ba, deq_tile=deqT,
                entry_sv=svO, rhs_base=qo)
        psum_row_of(hfacc, 6)
        p.wait(V, "sT")
        p.emit(V, lambda e: e.tensor_scalar(
            rw[3], pp[6][0:1, :], 1.0 / I, 1e-6, ALU.mult, ALU.add))
        p.inc(V, "sV")
        sqrt_row(rw[1], rw[3])
        p.emit(V, lambda e: e.reciprocal(rw[4], rw[1]))
        p.inc(V, "sV")
        bcast_mm(bcR, rw[4])
        p.emit(V, lambda e: e.tensor_copy(m1, mh))
        p.inc(V, "sV")
        pmax_to_bcast(ch, 3, Mb)
        fac_deq(bcR, "wdown", deqT, facT)
        p.wait(S, "sHf0")
        p.wait(S, "sHf1")
        p.wait(S, "sV")
        p.wait(S, "sT")
        ring = [sc[4], sc[5], m1, mh]
        slot_base = [p.val(f"sRl{j}") for j in range(4)]
        sv_base = p.val("sV")
        for k in range(FTI):
            if k >= 4:
                p.wait(S, "sV", sv_base + (k - 3))
            p.dma(ring[k % 4], hfscr[k * P:(k + 1) * P, :], f"sRl{k % 4}", eng=S)
            p.wait(V, f"sRl{k % 4}", slot_base[k % 4] + 16 * (k // 4 + 1))
            src = ring[k % 4]
            p.emit(V, lambda e, s=src, k=k: e.scalar_tensor_tensor(
                sc[6], s, lnw_sb[:, k:k + 1], facT, ALU.mult, ALU.mult))
            p.emit(V, lambda e, k=k: e.tensor_scalar(
                bnh[:, k * TC:(k + 1) * TC], sc[6], MAGIC, MAGIC, ALU.add, ALU.subtract))
            p.inc(V, "sV")
        def post_down(og, bank):
            for oc in range(4):
                ftile = og * 4 + oc
                xb = bx[:, ftile * TC:(ftile + 1) * TC]
                sb_ = ba[:, ftile * TC:(ftile + 1) * TC]
                p.emit(V, lambda e, b=bank, oc=oc: e.tensor_tensor(
                    sc[0], pp[b + oc], deqT, ALU.mult))
                p.inc(V, "sV")
                p.wait(S, "sV")
                p.emit(S, lambda e: e.activation(sc[1], sc[0], AF.Silu))
                p.inc(S, "sS")
                p.wait(V, "sS")
                otile = sc[3] if oc % 2 == 0 else sc[7]
                if og > 0 or oc > 1:
                    p.wait(V, f"sOut{oc % 2}")
                p.emit(V, lambda e, s=sb_: e.tensor_tensor(sc[2], s, sc[1], ALU.mult))
                p.emit(V, lambda e, x=xb, o=otile: e.tensor_tensor(o, x, sc[2], ALU.add))
                p.inc(V, "sV")
                p.dma_out(outT[ftile * P:(ftile + 1) * P, ch * TC:(ch + 1) * TC],
                          otile, f"sOut{oc % 2}")
        mm_evac("wdown", FTI, bnh, 4, post=post_down,
                entry_sv=sv_base, rhs_base=sv_base)
        p.wait(V, "sOut0")
        p.wait(V, "sOut1")

    def finish_debug():
        if SUBSTOP is None:
            for ft in range(FT):
                p.dma_out(outT[ft * P:(ft + 1) * P, 0:TC], bx[:, ft * TC:(ft + 1) * TC], "sOut")
        p.wait(E, "sOut")
        p.wait(E, "sScr")

    done = False
    if stop_after == "dumprow":
        p.wait(E, "sV")
        p.dma(outT[0:1, 0:8], partrow, "sOut")
        p.dma(outT[1:9, 0:8], partall, "sOut")
        p.dma(outT[9:10, 0:8], srow, "sOut")
        p.dma(outT[16:144, 0:8], swcol, "sOut")
        p.dma(outT[144:272, 0:8], swinv, "sOut")
        p.wait(E, "sOut")
        p.wait(E, "sScr")
        done = True
    if stop_after == "dumpwq":
        p.wait("gpsimd", "sCc")
        for ft in range(FT):
            p.bump("sOut", 16)
            p.emit("gpsimd", lambda e, ft=ft: e.dma_start(
                out=outT[ft * P:(ft + 1) * P, :],
                in_=wq["wi"][ft * P:(ft + 1) * P, 0:T]).then_inc(p.sems["sOut"], 16))
            p.streams["gpsimd"][-1] = (p.streams["gpsimd"][-1][0], True)
        p.wait(E, "sOut")
        p.wait(E, "sScr")
        done = True
    if stop_after == "prologue":
        p.wait(E, "sCc")
        p.dma_in(bx[:, 0:TC], xT[0:P, 0:TC], "sX")
        p.wait(E, "sX")
        finish_debug(); done = True
    if not done:
        for ch in range(NCH):
            section_A(ch)
            if stop_after == f"A{ch}":
                finish_debug(); done = True
                break
    if not done:
        carry_exchange()
        if stop_after == "carry":
            finish_debug(); done = True
        if stop_after == "dumpc":
            p.wait(E, "sV")
            p.wait(E, "sScr")
            p.dma(outT[0:P, 0:FT], hc, "sOut")
            p.dma(outT[0:P, FT:2 * FT], fc, "sOut")
            p.dma(outT[0:P, 2 * FT:3 * FT], carry_sb, "sOut")
            p.dma(outT[0:P, 3 * FT:3 * FT + 8 * FT], cslots, "sOut")
            p.dma(outT[0:P, 11 * FT:11 * FT + 8], csel_sb, "sOut")
            p.wait(E, "sOut")
            done = True
        if stop_after == "dumph2":
            p.wait(E, "sV")
            p.wait(E, "sScr")
            for ch in range(NCH):
                for ft in range(FT):
                    p.dma(bx[:, ft * TC:(ft + 1) * TC],
                          hscr[ft * P:(ft + 1) * P, ch * TC:(ch + 1) * TC], "sX")
                p.wait(E, "sX")
                for ft in range(FT):
                    p.dma(outT[ft * P:(ft + 1) * P, ch * TC:(ch + 1) * TC],
                          bx[:, ft * TC:(ft + 1) * TC], "sOut")
                p.wait(E, "sOut")
            done = True
    if not done:
        for ch in range(NCH):
            section_B(ch)
            if stop_after == f"B{ch}":
                finish_debug(); done = True
                break
    if not done:
        p.wait(E, "sOut")
        p.wait(E, "sScr")

    sem_names = ["sX", "sW", "sW0", "sW1", "sW2", "sW3", "sScr", "sBc",
                 "sOut", "sOut0", "sOut1", "sCc", "sV", "sS", "sT",
                 "sG", "sRow", "sT_mm", "sFb0", "sFb1", "sFb2", "sFb3",
                 "sHf0", "sHf1", "sHfL", "sXg", "sRl0", "sRl1", "sRl2", "sRl3", "sHsc"]
    with contextlib.ExitStack() as st:
        p.sems = {n_: st.enter_context(nc.semaphore(n_)) for n_ in sem_names}
        blk = st.enter_context(nc.Block())
        for eng_name in Prog.ENGS:
            stream = p.streams[eng_name]
            if not stream:
                continue
            getattr(blk, eng_name)(lambda eng, s=stream: [fn(eng) for fn, _ in s])
    return nc


def _run_on_device(stop_after, inputs):
    return None


def kernel(L0_attn_hidden_states, L0_attn_Wi, L0_attn_Wf, L0_attn_Wg, L0_attn_Wo,
           L0_attn_norm_i, L0_attn_norm_f, L0_attn_norm_g, L0_attn_norm_o,
           L0_attn_g_norm_w, L0_ffn_w_gate, L0_ffn_w_up, L0_ffn_w_down,
           L0_ffn_ln_weight):
    x = np.ascontiguousarray(np.asarray(L0_attn_hidden_states, dtype=np.float32))
    B, L, _ = x.shape
    wTf = {
        "wi": np.ascontiguousarray(np.asarray(L0_attn_Wi, dtype=np.float32).T),
        "wf": np.ascontiguousarray(np.asarray(L0_attn_Wf, dtype=np.float32).T),
        "wg": np.ascontiguousarray(np.asarray(L0_attn_Wg, dtype=np.float32).T),
        "wo": np.ascontiguousarray(np.asarray(L0_attn_Wo, dtype=np.float32).T),
        "wgate": np.ascontiguousarray(np.asarray(L0_ffn_w_gate, dtype=np.float32).T),
        "wup": np.ascontiguousarray(np.asarray(L0_ffn_w_up, dtype=np.float32).T),
        "wdown": np.ascontiguousarray(np.asarray(L0_ffn_w_down, dtype=np.float32).T),
    }
    nrm_np = {
        "i": np.ascontiguousarray(np.asarray(L0_attn_norm_i, np.float32).reshape(FT, P).T),
        "f": np.ascontiguousarray(np.asarray(L0_attn_norm_f, np.float32).reshape(FT, P).T),
        "g": np.ascontiguousarray(np.asarray(L0_attn_norm_g, np.float32).reshape(FT, P).T),
        "o": np.ascontiguousarray(np.asarray(L0_attn_norm_o, np.float32).reshape(FT, P).T),
        "gn": np.ascontiguousarray(np.asarray(L0_attn_g_norm_w, np.float32).reshape(FT, P).T),
    }
    lnw_np = np.ascontiguousarray(np.asarray(L0_ffn_ln_weight, np.float32).reshape(FTI, P).T)
    ident_np = np.eye(P, dtype=np.float32)

    nc = build_program()
    in_maps = []
    for c in range(8):
        seq, half = c // 2, c % 2
        xs = np.ascontiguousarray(x[seq, half * T:(half + 1) * T, :].T)
        csel_np = np.zeros((P, 8), np.float32)
        if half == 1:
            csel_np[:, c - 1] = 1.0
        m = {"xT": xs, "csel": csel_np, "ident": ident_np, "lnw": lnw_np}
        for k, v in nrm_np.items():
            m["nrm_" + k] = v
        for (w, K, O) in WEIGHTS:
            kper = K // 8
            m[w + "S"] = np.ascontiguousarray(wTf[w][c * kper * P:(c + 1) * kper * P, :])
        in_maps.append(m)
    res = run_bass_kernel_spmd(nc, in_maps, core_ids=list(range(8)))
    out = np.empty((B, L, D), np.float32)
    for c in range(8):
        seq, half = c // 2, c % 2
        out[seq, half * T:(half + 1) * T, :] = np.asarray(res.results[c]["outT"]).T
    return out


# version 20
# speedup vs baseline: 1.0299x; 1.0209x over previous
import contextlib
import numpy as np
import concourse.bass as bass
import concourse.mybir as mybir
from concourse.bass_utils import run_bass_kernel_spmd

F32 = mybir.dt.float32
BF16 = mybir.dt.bfloat16
AF = mybir.ActivationFunctionType
ALU = mybir.AluOpType
AXX = mybir.AxisListType.X

P = 128
D = 2048
I = 8192
T = 2048
TC = 512
NCH = T // TC
FT = D // P
FTI = I // P
MAGIC = 12582912.0

WEIGHTS = [("wi", FT, D), ("wf", FT, D), ("wg", FT, D),
           ("wgate", FT, I), ("wup", FT, I), ("wo", FT, D), ("wdown", FTI, D)]
WIDX = {w: j for j, (w, _, _) in enumerate(WEIGHTS)}
CC_OF = {w: 2 + j for j, (w, _, _) in enumerate(WEIGHTS)}


class Prog:
    ENGS = ("sync", "tensor", "vector", "scalar", "gpsimd")

    def __init__(self):
        self.streams = {e: [] for e in self.ENGS}
        self.ctr = {}
        self.sems = None

    def emit(self, eng, fn):
        self.streams[eng].append((fn, False))

    def bump(self, sem, amt):
        self.ctr[sem] = self.ctr.get(sem, 0) + amt
        return self.ctr[sem]

    def val(self, sem):
        return self.ctr.get(sem, 0)

    def wait(self, eng, sem, v=None):
        if v is None:
            v = self.val(sem)
        if v <= 0:
            return
        self.emit(eng, lambda e, s=sem, vv=v: e.wait_ge(self.sems[s], vv))
        self.streams[eng][-1] = (self.streams[eng][-1][0], True)

    def dma(self, dst, src, sem, slow=False, eng="sync"):
        self.bump(sem, 16)
        self.emit(eng, lambda e, d=dst, s=src, sm=sem, sl=slow:
                  e.dma_start(out=d, in_=s, allow_slow_non_contiguous=sl
                              ).then_inc(self.sems[sm], 16))
        self.streams[eng][-1] = (self.streams[eng][-1][0], True)

    def dma_in(self, dst, src, sem, slow=False, eng="sync"):
        self.wait(eng, "sV")
        self.wait(eng, "sS")
        self.dma(dst, src, sem, slow, eng)

    def dma_out(self, dst, src, sem, slow=False, eng="sync"):
        self.wait(eng, "sV")
        self.dma(dst, src, sem, slow, eng)

    def dma_split(self, dst, src, sem, nsplit, in_=False, eng="sync"):
        if in_:
            self.wait(eng, "sV")
            self.wait(eng, "sS")
        else:
            self.wait(eng, "sV")
        step = 128 // nsplit
        for j in range(nsplit):
            self.dma(dst[j * step:(j + 1) * step, :],
                     src[j * step:(j + 1) * step, :], sem, eng=eng)

    def inc(self, eng, sem, amt=1):
        self.bump(sem, amt)
        st = self.streams[eng]
        assert st, "inc with no prior instruction"
        prev, wrapped = st[-1]
        assert not wrapped, f"double inc on one instruction ({eng}/{sem})"
        st[-1] = (lambda e, f=prev, s=sem, a=amt: f(e).then_inc(self.sems[s], a),
                  True)


SUBSTOP = None
DELAY_TEST = False


def build_program(stop_after=None):
    nc = bass.Bass("TRN2", target_bir_lowering=False,
                   detect_race_conditions=False)
    p = Prog()
    E, TE, V, S, G = "sync", "tensor", "vector", "scalar", "gpsimd"

    din = lambda n, s, dt=F32: nc.dram_tensor(n, s, dt, kind="ExternalInput").ap()
    xT = din("xT", [D, T])
    wTs = {w: din(w + "S", [(K // 8) * P, O]) for (w, K, O) in WEIGHTS}
    nrm = {k: din("nrm_" + k, [P, FT]) for k in ("i", "f", "g", "o", "gn")}
    lnw_in = din("lnw", [P, FTI])
    csel_in = din("csel", [P, 8])
    ident_in = din("ident", [P, P])
    outT = nc.dram_tensor("outT", [D, T], F32, kind="ExternalOutput").ap()

    dint = lambda n, s, dt=F32, sp="Local": nc.dram_tensor(
        n, s, dt, kind="Internal", addr_space=sp).ap()
    slab = {w: dint("slab_" + w, [(K // 8) * P, O], BF16) for (w, K, O) in WEIGHTS}
    wq = {w: dint("wq_" + w, [K * P, O], BF16, "Shared") for (w, K, O) in WEIGHTS}
    part_box = dint("part_box", [8])
    part_gat = dint("part_gat", [8, 8], F32, "Shared")
    sw_box = dint("sw_box", [8])
    hscr = dint("hscr", [D, T])
    fscr = dint("fscr", [D, T])
    hfscr = dint("hfscr", [I, TC])
    flag_box = dint("flag_box", [8])
    flag_gat = dint("flag_gat", [8, 8], F32, "Shared")
    carry_box = dint("carry_box", [D])
    carry_gat = dint("carry_gat", [8, D], F32, "Shared")
    rowscr = dint("rowscr", [NCH, 8, TC])

    KB = 1024

    SB_BASE = 18 * KB

    def sb_at(name, shape, dt, off):
        return nc.alloc_sbuf_tensor_at(name, shape, dt, offset=SB_BASE + off).ap()

    bx = sb_at("bx", [P, FT * TC], F32, 0)
    ba = sb_at("ba", [P, FT * TC], F32, 32 * KB)
    bb = sb_at("bb", [P, FT * TC], F32, 64 * KB)
    bc = sb_at("bc", [P, FT * TC], F32, 96 * KB)
    bnh = sb_at("bnh", [P, FTI * TC], BF16, 64 * KB)
    bn = sb_at("bn", [P, FT * TC], BF16, 128 * KB)
    bnf = sb_at("bnf", [P, 8 * TC], F32, 128 * KB)
    off = 144 * KB
    sc = []
    for j in range(8):
        sc.append(sb_at(f"sc{j}", [P, TC], F32, off)); off += 2 * KB
    nm = {}
    for t_ in ("m1", "mh", "bcA", "bcB", "bcR", "facT", "deqT", "deq2T", "Mb",
               "hfacc", "bz", "mn"):
        nm[t_] = sb_at(t_, [P, TC], F32, off); off += 2 * KB
    m1, mh, bcA, bcB, bcR = nm["m1"], nm["mh"], nm["bcA"], nm["bcB"], nm["bcR"]
    facT, deqT, deq2T, Mb, hfacc, bz = (nm["facT"], nm["deqT"], nm["deq2T"],
                                        nm["Mb"], nm["hfacc"], nm["bz"])
    mntl = nm["mn"]
    wt = []
    for j in range(4):
        wt.append(sb_at(f"wt{j}", [P, TC], BF16, off)); off += 1 * KB
    u3 = sb_at("u3", [P, TC], F32, off); off += 2 * KB
    nrm_sb = {}
    for k in ("i", "f", "g", "o", "gn"):
        nrm_sb[k] = sb_at("nsb_" + k, [P, FT], F32, off); off += 64
    lnw_sb = sb_at("lnw_sb", [P, FTI], F32, off); off += 256
    ident = sb_at("ident_sb", [P, P], F32, off); off += 512
    onecol = sb_at("onecol", [P, 1], F32, off); off += 32
    onesrow = sb_at("onesrow", [1, P], F32, off); off += 512
    hc = sb_at("hc", [P, FT], F32, off); off += 64
    fc = sb_at("fc", [P, FT], F32, off); off += 64
    carry_sb = sb_at("carry_sb", [P, FT], F32, off); off += 64
    cslots = sb_at("cslots", [P, 8 * FT], F32, off); off += 512
    csel_sb = sb_at("csel_sb", [P, 8], F32, off); off += 32
    swcol = sb_at("swcol", [P, 8], F32, off); off += 32
    swinv = sb_at("swinv", [P, 8], F32, off); off += 32
    rw = []
    for j in range(6):
        rw.append(sb_at(f"rw{j}", [1, TC], F32, off)); off += 2 * KB
    packrow = sb_at("packrow", [4, P], F32, off); off += 512
    mp = sb_at("mp", [P, 4], F32, off); off += 32
    partrow = sb_at("partrow", [1, 8], F32, off); off += 32
    partall = sb_at("partall", [8, 8], F32, off); off += 32
    srow = sb_at("srow", [1, 8], F32, off); off += 32
    assert off <= 207 * KB, f"SBUF over budget: {off}"

    pp = [nc.alloc_psum_tensor(f"pp{j}", [P, TC], F32).ap() for j in range(8)]

    def vsync():
        st_ = p.streams["vector"]
        fn, wrapped = st_[-1]
        if not wrapped:
            p.bump("sV", 1)
            st_[-1] = (lambda e, f=fn: f(e).then_inc(p.sems["sV"], 1), True)
        p.wait("vector", "sV")

    def bcast_mm(dst_tile, row_sbuf):
        p.wait(TE, "sV")
        p.emit(TE, lambda e, r=row_sbuf: e.matmul(
            pp[7], onesrow, r, start=True, stop=True))
        p.inc(TE, "sT")
        p.wait(V, "sT")
        p.emit(V, lambda e, d=dst_tile: e.tensor_copy(d, pp[7]))
        vsync()

    def pmax_to_bcast(ch, slot, dst_tile):
        p.wait(TE, "sV")
        for j in range(4):
            p.emit(TE, lambda e, j=j: e.transpose(
                pp[7][:, j * P:(j + 1) * P], m1[:, j * P:(j + 1) * P], ident))
        p.inc(TE, "sT")
        p.wait(V, "sT")
        for j in range(4):
            p.emit(V, lambda e, j=j: e.reduce_max(
                mp[:, j:j + 1], pp[7][:, j * P:(j + 1) * P], axis=AXX))
        p.inc(V, "sV")
        p.wait(TE, "sV")
        p.emit(TE, lambda e: e.transpose(pp[7][0:4, 0:P], mp, ident))
        p.inc(TE, "sT")
        p.wait(V, "sT")
        p.emit(V, lambda e: e.tensor_copy(packrow, pp[7][0:4, 0:P]))
        p.inc(V, "sV")
        dst_row = rowscr[ch, slot, :]
        dstap = bass.AP(dst_row.tensor, dst_row.offset, [[P, 4], [1, P]])
        p.dma_out(dstap, packrow, "sRow")
        p.wait(E, "sRow")
        p.wait(E, "sT")
        p.dma_in(rw[2], dst_row, "sBc")
        p.wait(TE, "sBc")
        bcast_mm(dst_tile, rw[2])

    def psum_row_of(acc_tile, bank):
        p.wait(TE, "sV")
        p.emit(TE, lambda e, a=acc_tile, b=bank: e.matmul(
            pp[b][0:1, :], onecol, a, start=True, stop=True))
        p.inc(TE, "sT")

    def sqrt_row(dst, srcr):
        p.wait(S, "sV")
        p.emit(S, lambda e, d=dst, r=srcr: e.activation(d, r, AF.Sqrt))
        p.inc(S, "sS")
        p.wait(V, "sS")

    def acc_squares(in_buf, nft=FT):
        for ft in range(nft):
            src = in_buf[:, ft * TC:(ft + 1) * TC]
            p.emit(V, lambda e, s=src: e.tensor_tensor(sc[0], s, s, ALU.mult))
            if ft == 0:
                p.emit(V, lambda e: e.tensor_copy(sc[4], sc[0]))
            else:
                p.emit(V, lambda e: e.tensor_tensor(sc[4], sc[4], sc[0], ALU.add))
        p.inc(V, "sV")

    def maxes(in_buf, nrm_tile):
        p.emit(V, lambda e: e.memset(m1, 0.0))
        p.inc(V, "sV")
        p.wait(S, "sV")
        for ft in range(FT):
            src = in_buf[:, ft * TC:(ft + 1) * TC]
            sq = sc[0] if ft % 2 == 0 else sc[1]
            if nrm_tile is not None:
                col = nrm_tile[:, ft:ft + 1]
                p.emit(S, lambda e, s=src, c=col, q=sq: e.activation(
                    q, s, AF.Square, scale=c))
            else:
                p.emit(S, lambda e, s=src, q=sq: e.activation(q, s, AF.Square))
            p.inc(S, "sS")
            p.wait(V, "sS")
            p.emit(V, lambda e, q=sq: e.tensor_tensor(m1, m1, q, ALU.max))
            p.inc(V, "sV")
            if ft >= 1:
                p.wait(S, "sV", p.val("sV") - 1)
        p.wait(V, "sV")

    def fac_deq(rfac_tile, w_name, deq_dst, fac_dst, w2_name=None, deq2_dst=None):
        wi = WIDX[w_name]
        p.wait(S, "sV")
        p.wait(S, "sBc")
        p.emit(S, lambda e: e.activation(mntl, Mb, AF.Sqrt))
        p.inc(S, "sS")
        p.wait(V, "sS")
        if rfac_tile is not None:
            p.emit(V, lambda e: e.tensor_tensor(sc[2], mntl, rfac_tile, ALU.mult))
        else:
            p.emit(V, lambda e: e.tensor_copy(sc[2], mntl))
        p.emit(V, lambda e: e.tensor_scalar(sc[2], sc[2], 1e-5, None, ALU.max))
        p.emit(V, lambda e: e.reciprocal(sc[3], sc[2]))
        vsync()
        if rfac_tile is not None:
            p.emit(V, lambda e, f=fac_dst: e.scalar_tensor_tensor(
                f, sc[3], 127.0, rfac_tile, ALU.mult, ALU.mult))
        else:
            p.emit(V, lambda e, f=fac_dst: e.tensor_scalar(
                f, sc[3], 127.0, None, ALU.mult))
        p.emit(V, lambda e, d=deq_dst, wi=wi: e.tensor_scalar(
            d, sc[2], swcol[:, wi:wi + 1], 1.0 / 127.0, ALU.mult, ALU.mult))
        if w2_name is not None:
            wi2 = WIDX[w2_name]
            p.emit(V, lambda e, d=deq2_dst, wi=wi2: e.tensor_scalar(
                d, sc[2], swcol[:, wi:wi + 1], 1.0 / 127.0, ALU.mult, ALU.mult))
        p.inc(V, "sV")

    def quant_to_bn(in_buf, nrm_tile, out_bf):
        base = p.val("sV")
        for ft in range(FT):
            src = in_buf[:, ft * TC:(ft + 1) * TC]
            dstn = out_bf[:, ft * TC:(ft + 1) * TC]
            if nrm_tile is not None:
                col = nrm_tile[:, ft:ft + 1]
                p.emit(V, lambda e, s=src, c=col: e.scalar_tensor_tensor(
                    sc[0], s, c, facT, ALU.mult, ALU.mult))
            else:
                p.emit(V, lambda e, s=src: e.tensor_tensor(sc[0], s, facT, ALU.mult))
            p.emit(V, lambda e, d=dstn: e.tensor_scalar(
                d, sc[0], MAGIC, MAGIC, ALU.add, ALU.subtract))
            p.inc(V, "sV")
        return base

    def mm_accum(wname, K, og, bank, rhs_bf, rhs_ready=None):
        slot_base = [p.val(f"sW{s}") for s in range(4)]
        mm_base = p.val("sT_mm")
        p.wait(E, "sCc", CC_OF[wname])
        for k in range(K):
            p.wait(E, "sT_mm", mm_base + max(k - 3, 0))
            src = wq[wname][k * P:(k + 1) * P, og * 512:(og + 1) * 512]
            p.dma(wt[k % 4], src, f"sW{k % 4}")
        for k in range(K):
            if rhs_ready is not None:
                p.wait(TE, "sV", rhs_ready + k + 1)
            p.wait(TE, f"sW{k % 4}", slot_base[k % 4] + 16 * (k // 4 + 1))
            for oc in range(4):
                p.emit(TE, lambda e, k=k, oc=oc, b=bank: e.matmul(
                    pp[b + oc], wt[k % 4][:, oc * P:(oc + 1) * P],
                    rhs_bf[:, k * TC:(k + 1) * TC],
                    start=(k == 0), stop=(k == K - 1)))
            p.inc(TE, "sT_mm")

    def mm_evac(wname, K, rhs_bf, n_og, out_buf=None, deq_tile=None, post=None,
                entry_sv=None, rhs_base=None):
        p.wait(TE, "sV", entry_sv if entry_sv is not None else p.val("sV"))
        ev_after = []
        for og in range(n_og):
            bank = (og % 2) * 4
            if og >= 2:
                p.wait(TE, "sV", ev_after[og - 2])
            mm_accum(wname, K, og, bank, rhs_bf,
                     rhs_ready=(rhs_base if og == 0 else None))
            p.wait(V, "sT_mm")
            if post is None:
                for oc in range(4):
                    ftile = og * 4 + oc
                    dst = out_buf[:, ftile * TC:(ftile + 1) * TC]
                    p.emit(V, lambda e, b=bank, oc=oc, d=dst, q=deq_tile:
                           e.tensor_tensor(d, pp[b + oc], q, ALU.mult))
                p.inc(V, "sV")
                p.wait(V, "sV")
            else:
                post(og, bank)
            ev_after.append(p.val("sV"))

    p.dma(ident, ident_in, "sX")
    for k in ("i", "f", "g", "o", "gn"):
        p.dma(nrm_sb[k], nrm[k], "sX")
    p.dma(lnw_sb, lnw_in, "sX")
    p.dma(csel_sb, csel_in, "sX")
    p.emit(V, lambda e: e.memset(onecol, 1.0))
    p.emit(V, lambda e: e.memset(onesrow, 1.0))
    p.emit(V, lambda e: e.memset(partrow, 0.0))
    p.inc(V, "sV")
    p.dma_out(bass.AP(flag_box.tensor, 0, [[8, 1], [1, 8]]), partrow, "sScr")
    p.emit(V, lambda e: e.memset(srow, 0.0))
    p.emit(V, lambda e: e.memset(bz, 0.0))
    p.emit(V, lambda e: e.memset(hc, 0.0))
    p.emit(V, lambda e: e.memset(fc, 1.0))
    p.inc(V, "sV")

    for widx, (w, K, O) in enumerate(WEIGHTS):
        kper = K // 8
        for j in range(kper):
            p.dma_split(bx[:, 0:O], wTs[w][j * P:(j + 1) * P, :], "sX", 1,
                        in_=True)
            p.wait(V, "sX")
            p.wait(V, "sT")
            nseg = O // TC
            for sgi in range(nseg):
                seg = bx[:, sgi * TC:(sgi + 1) * TC]
                p.emit(V, lambda e, s=seg: e.tensor_scalar(
                    sc[2], s, -1.0, None, ALU.mult))
                p.emit(V, lambda e, s=seg: e.tensor_tensor(
                    sc[2], sc[2], s, ALU.max))
                p.emit(V, lambda e, sgi=sgi: e.reduce_sum(
                    sc[4][:, sgi:sgi + 1], sc[2], axis=AXX))
            vsync()
            p.emit(V, lambda e, n=nseg: e.reduce_sum(
                sc[5][:, 0:1], sc[4][:, 0:n], axis=AXX))
            p.inc(V, "sV")
            p.wait(TE, "sV")
            p.emit(TE, lambda e, f=(j == 0), l=(j == kper - 1): e.matmul(
                pp[6][0:1, 0:1], onecol, sc[5][:, 0:1], start=f, stop=l))
            p.inc(TE, "sT")
        p.wait(V, "sT")
        p.emit(V, lambda e, wi=widx: e.tensor_copy(partrow[:, wi:wi + 1],
                                                   pp[6][0:1, 0:1]))
        p.inc(V, "sV")
    p.dma_out(bass.AP(part_box.tensor, 0, [[8, 1], [1, 8]]), partrow, "sScr")
    p.wait(G, "sScr")
    p.bump("sCc", 1)
    p.emit(G, lambda e: e.collective_compute(
        "AllGather", ALU.bypass, replica_groups=[list(range(8))],
        ins=[part_box], outs=[part_gat]).then_inc(p.sems["sCc"], 1))
    p.wait(E, "sCc", 1)
    p.dma_in(partall, part_gat, "sScr")
    p.wait(TE, "sScr")
    p.wait(TE, "sV")
    p.emit(TE, lambda e: e.matmul(pp[6][0:1, 0:8], onecol[0:8, :], partall,
                                  start=True, stop=True))
    p.inc(TE, "sT")
    p.wait(V, "sT")
    p.emit(V, lambda e: e.tensor_copy(srow, pp[6][0:1, 0:8]))
    vsync()
    for widx, (w, K, O) in enumerate(WEIGHTS):
        numel = float(K * P * O)
        p.emit(V, lambda e, wi=widx, n=numel: e.tensor_scalar(
            srow[:, wi:wi + 1], srow[:, wi:wi + 1], 1.0 / n, 1e-5, ALU.mult, ALU.max))
    p.inc(V, "sV")
    p.dma_out(bass.AP(sw_box.tensor, 0, [[8, 1], [1, 8]]), srow, "sScr")
    p.wait(E, "sScr")
    p.dma_in(swcol, bass.AP(sw_box.tensor, sw_box.offset, [[0, P], [1, 8]]), "sBc")
    p.wait(V, "sBc")
    for widx in range(len(WEIGHTS)):
        p.emit(V, lambda e, wi=widx: e.reciprocal(swinv[:, wi:wi + 1],
                                                  swcol[:, wi:wi + 1]))
    p.inc(V, "sV")
    p.wait(V, "sV")
    for widx, (w, K, O) in enumerate(WEIGHTS):
        kper = K // 8
        for j in range(kper):
            p.dma_split(bx[:, 0:O], wTs[w][j * P:(j + 1) * P, :], "sX", 1,
                        in_=True)
            p.wait(V, "sX")
            p.wait(V, "sScr")
            nseg = O // TC
            for sgi in range(nseg):
                seg = bx[:, sgi * TC:(sgi + 1) * TC]
                slot = sgi % FT
                p.emit(V, lambda e, s=seg, wi=widx: e.tensor_scalar(
                    sc[7], s, swinv[:, wi:wi + 1], MAGIC, ALU.mult, ALU.add))
                p.emit(V, lambda e: e.tensor_scalar(
                    sc[7], sc[7], MAGIC + 1.0, MAGIC - 1.0, ALU.min, ALU.max))
                p.emit(V, lambda e, sl=slot: e.tensor_scalar(
                    bn[:, sl * TC:(sl + 1) * TC], sc[7], MAGIC, None, ALU.subtract))
                p.inc(V, "sV")
                p.dma_split(slab[w][j * P:(j + 1) * P, sgi * TC:(sgi + 1) * TC],
                            bn[:, slot * TC:(slot + 1) * TC], "sScr", 1)
        p.wait(G, "sScr")
        p.bump("sCc", 1)
        p.emit(G, lambda e, w=w: e.collective_compute(
            "AllGather", ALU.bypass, replica_groups=[list(range(8))],
            ins=[slab[w]], outs=[wq[w]]).then_inc(p.sems["sCc"], 1))
    p.bump("sCc", 1)
    p.emit(G, lambda e: e.collective_compute(
        "AllGather", ALU.bypass, replica_groups=[list(range(8))],
        ins=[flag_box], outs=[flag_gat]).then_inc(p.sems["sCc"], 1))
    CC_BARRIER = p.val("sCc")

    if DELAY_TEST:
        for _ in range(1500):
            p.emit(V, lambda e: e.tensor_copy(sc[0], sc[1]))
        p.inc(V, "sV")
        p.wait(E, "sV")

    def load_x(ch):
        for ft in range(FT):
            p.dma_split(bx[:, ft * TC:(ft + 1) * TC],
                        xT[ft * P:(ft + 1) * P, ch * TC:(ch + 1) * TC],
                        "sXg", 1, in_=True, eng=S)

    def ln_stats(ch):
        p.wait(V, "sXg")
        for j in range(4):
            p.wait(V, f"sFb{j}")
        for ft in range(FT):
            src = bx[:, ft * TC:(ft + 1) * TC]
            if ft == 0:
                p.emit(V, lambda e, s=src: e.tensor_copy(sc[5], s))
            else:
                p.emit(V, lambda e, s=src: e.tensor_tensor(sc[5], sc[5], s, ALU.add))
        p.inc(V, "sV")
        acc_squares(bx)
        psum_row_of(sc[5], 6)
        psum_row_of(sc[4], 5)
        p.wait(V, "sT")
        p.emit(V, lambda e: e.tensor_scalar(rw[0], pp[6][0:1, :], 1.0 / D, None, ALU.mult))
        p.emit(V, lambda e: e.tensor_scalar(rw[1], pp[5][0:1, :], 1.0 / D, None, ALU.mult))
        vsync()
        p.emit(V, lambda e: e.tensor_tensor(rw[2], rw[0], rw[0], ALU.mult))
        p.emit(V, lambda e: e.tensor_tensor(rw[2], rw[1], rw[2], ALU.subtract))
        p.emit(V, lambda e: e.tensor_scalar(rw[1], rw[2], 1e-5, None, ALU.add))
        p.inc(V, "sV")
        sqrt_row(rw[4], rw[1])
        p.emit(V, lambda e: e.reciprocal(rw[3], rw[4]))
        vsync()
        p.emit(V, lambda e: e.tensor_tensor(rw[4], rw[0], rw[3], ALU.mult))
        p.emit(V, lambda e: e.tensor_scalar(rw[4], rw[4], -1.0, None, ALU.mult))
        p.emit(V, lambda e: e.reciprocal(rw[5], rw[1]))
        vsync()
        p.emit(V, lambda e: e.tensor_tensor(rw[5], rw[2], rw[5], ALU.mult))
        p.emit(V, lambda e: e.tensor_scalar(rw[5], rw[5], 1e-8, None, ALU.add))
        p.inc(V, "sV")
        sqrt_row(rw[1], rw[5])
        p.emit(V, lambda e: e.reciprocal(rw[5], rw[1]))
        p.inc(V, "sV")
        p.dma_out(rowscr[ch, 0, :], rw[3], "sRow")
        p.dma_out(rowscr[ch, 1, :], rw[4], "sRow")
        p.dma_out(rowscr[ch, 2, :], rw[5], "sRow")
        bcast_mm(bcA, rw[3])
        bcast_mm(bcB, rw[4])
        bcast_mm(bcR, rw[5])

    def make_xn(dst_buf):
        p.wait(V, "sScr")
        for ft in range(FT):
            x_ = bx[:, ft * TC:(ft + 1) * TC]
            d_ = dst_buf[:, ft * TC:(ft + 1) * TC]
            p.emit(V, lambda e, x=x_, d=d_: e.tensor_tensor(d, x, bcA, ALU.mult))
            p.emit(V, lambda e, d=d_: e.tensor_tensor(d, d, bcB, ALU.add))
        p.inc(V, "sV")

    def section_A(ch):
        load_x(ch)
        if SUBSTOP == "load":
            return
        ln_stats(ch)
        if SUBSTOP == "ln":
            return
        make_xn(bc)
        maxes(bc, nrm_sb["i"])
        if SUBSTOP == "max":
            return
        pmax_to_bcast(ch, 3, Mb)
        if SUBSTOP == "pmax":
            return
        fac_deq(bcR, "wi", deqT, facT)
        svA = p.val("sV")
        qi = quant_to_bn(bc, nrm_sb["i"], bn)
        if SUBSTOP == "quant":
            return
        maxes(bc, nrm_sb["f"])
        if isinstance(SUBSTOP, str) and SUBSTOP.startswith("mm_og"):
            n = int(SUBSTOP[5:])
            mm_evac("wi", FT, bn, n, out_buf=ba, deq_tile=deqT)
            return
        if SUBSTOP == "mm_noev":
            ev_entry = p.val("sV_ev")
            p.wait(TE, "sV")
            mm_accum("wi", FT, 0, 0, bn)
            p.wait(V, "sT")
            p.inc(V, "sV")
            return
        if SUBSTOP == "mm_1k":
            p.wait(TE, "sV")
            sW_base = p.val("sW")
            p.wait(E, "sCc", CC_OF["wi"])
            p.dma(wt[0], wq["wi"][0:P, 0:512], "sW")
            p.wait(TE, "sW", sW_base + 16)
            for oc in range(4):
                p.emit(TE, lambda e, oc=oc: e.matmul(
                    pp[oc], wt[0][:, oc * P:(oc + 1) * P], bn[:, 0:TC],
                    start=True, stop=True))
            p.inc(TE, "sT")
            p.wait(V, "sT")
            for oc in range(4):
                p.emit(V, lambda e, oc=oc: e.tensor_tensor(
                    ba[:, oc * TC:(oc + 1) * TC], pp[oc], deqT, ALU.mult))
            p.inc(V, "sV")
            return
        mm_evac("wi", FT, bn, 4, out_buf=ba, deq_tile=deqT,
                entry_sv=svA, rhs_base=qi)
        if SUBSTOP == "mmi":
            p.wait(E, "sV")
            for ft in range(FT):
                p.dma(outT[ft * P:(ft + 1) * P, 0:TC], ba[:, ft * TC:(ft + 1) * TC], "sOut")
            p.wait(E, "sOut")
            return
        pmax_to_bcast(ch, 4, Mb)
        fac_deq(bcR, "wf", deqT, facT)
        svF = p.val("sV")
        qf = quant_to_bn(bc, nrm_sb["f"], bn)

        def post_f(og, bank):
            for oc in range(4):
                ftile = og * 4 + oc
                d = bb[:, ftile * TC:(ftile + 1) * TC]
                ia = ba[:, ftile * TC:(ftile + 1) * TC]
                p.emit(V, lambda e, b=bank, oc=oc: e.tensor_tensor(
                    sc[0], pp[b + oc], deqT, ALU.mult))
                p.inc(V, "sV")
                p.wait(S, "sV")
                p.emit(S, lambda e, d=d: e.activation(d, sc[0], AF.Sigmoid))
                p.inc(S, "sS")
                p.emit(S, lambda e, s=ia: e.activation(sc[1], s, AF.Silu))
                p.inc(S, "sS")
                p.wait(V, "sS")
                p.emit(V, lambda e, f=d: e.tensor_scalar(
                    sc[2], f, -1.0, 1.0, ALU.mult, ALU.add))
                p.emit(V, lambda e, d2=ia: e.tensor_tensor(d2, sc[1], sc[2], ALU.mult))
                p.inc(V, "sV")
        mm_evac("wf", FT, bn, 4, post=post_f, entry_sv=svF, rhs_base=qf)
        if SUBSTOP == "mmf":
            return
        for ft in range(FT):
            fb = bb[:, ft * TC:(ft + 1) * TC]
            ib = ba[:, ft * TC:(ft + 1) * TC]
            hd = bc[:, ft * TC:(ft + 1) * TC]
            p.emit(V, lambda e, f=fb, i=ib, d=hd, ft=ft: e.tensor_tensor_scan(
                d, f, i, hc[:, ft:ft + 1], ALU.mult, ALU.add))
            vsync()
            p.emit(V, lambda e, d=hd, ft=ft: e.tensor_copy(
                hc[:, ft:ft + 1], d[:, TC - 1:TC]))
            p.inc(V, "sV")
            p.dma_split(hscr[ft * P:(ft + 1) * P, ch * TC:(ch + 1) * TC], hd,
                        "sScr", 1, eng=S)
            fs = sc[1 + (ft % 4)]
            if ft >= 4:
                p.wait(V, f"sFb{ft % 4}")
            p.emit(V, lambda e, f=fb, d=fs, ft=ft: e.tensor_tensor_scan(
                d, f, bz, fc[:, ft:ft + 1], ALU.mult, ALU.add))
            vsync()
            p.emit(V, lambda e, d=fs, ft=ft: e.tensor_copy(
                fc[:, ft:ft + 1], d[:, TC - 1:TC]))
            p.inc(V, "sV")
            p.dma_split(fscr[ft * P:(ft + 1) * P, ch * TC:(ch + 1) * TC], fs,
                        f"sFb{ft % 4}", 1, eng=S)

    carry_cc = [None]

    def carry_exchange():
        dst = bass.AP(carry_box.tensor, carry_box.offset, [[FT, P], [1, FT]])
        p.dma_out(dst, hc, "sScr")
        p.wait(G, "sScr")
        p.bump("sCc", 1)
        p.emit(G, lambda e: e.collective_compute(
            "AllGather", ALU.bypass, replica_groups=[list(range(8))],
            ins=[carry_box], outs=[carry_gat]).then_inc(p.sems["sCc"], 1))
        p.bump("sCc", 1)
        carry_cc[0] = p.val("sCc")
        p.emit(G, lambda e: e.collective_compute(
            "AllGather", ALU.bypass, replica_groups=[list(range(8))],
            ins=[flag_box], outs=[flag_gat]).then_inc(p.sems["sCc"], 1))

    def carry_fold():
        p.wait(E, "sCc", carry_cc[0])
        for s_ in range(8):
            src = bass.AP(carry_gat.tensor, carry_gat.offset + s_ * D,
                          [[FT, P], [1, FT]])
            p.dma(cslots[:, s_ * FT:(s_ + 1) * FT], src, "sScr")
        p.wait(V, "sScr")
        p.emit(V, lambda e: e.memset(carry_sb, 0.0))
        for s_ in range(8):
            p.emit(V, lambda e, s=s_: e.scalar_tensor_tensor(
                carry_sb, cslots[:, s * FT:(s + 1) * FT], csel_sb[:, s:s + 1],
                carry_sb, ALU.mult, ALU.add))
        p.inc(V, "sV")

    def section_B(ch):
        load_x(ch)
        p.wait(E, "sRow")
        p.dma_in(rw[3], rowscr[ch, 0, :], "sBc")
        p.dma_in(rw[4], rowscr[ch, 1, :], "sBc")
        p.dma_in(rw[5], rowscr[ch, 2, :], "sBc")
        p.wait(TE, "sBc")
        bcast_mm(bcA, rw[3])
        bcast_mm(bcB, rw[4])
        bcast_mm(bcR, rw[5])
        p.wait(V, "sXg")
        make_xn(bc)
        maxes(bc, nrm_sb["g"])
        pmax_to_bcast(ch, 3, Mb)
        fac_deq(bcR, "wg", deqT, facT)
        svG = p.val("sV")
        qg = quant_to_bn(bc, nrm_sb["g"], bn)
        maxes(bc, None)
        mm_evac("wg", FT, bn, 4, out_buf=ba, deq_tile=deqT,
                entry_sv=svG, rhs_base=qg)
        pmax_to_bcast(ch, 4, Mb)
        fac_deq(None, "wgate", deqT, facT, w2_name="wup", deq2_dst=deq2T)
        svU = p.val("sV")
        qffn = quant_to_bn(bc, None, bn)
        p.wait(E, "sScr")
        for j in range(4):
            p.wait(E, f"sFb{j}")
        for ft in range(FT):
            p.dma_split(bb[:, ft * TC:(ft + 1) * TC],
                        hscr[ft * P:(ft + 1) * P, ch * TC:(ch + 1) * TC],
                        "sHfL", 1, in_=True, eng=S)
            p.dma_split(bc[:, ft * TC:(ft + 1) * TC],
                        fscr[ft * P:(ft + 1) * P, ch * TC:(ch + 1) * TC],
                        "sHfL", 1, in_=True, eng=S)
        p.emit(V, lambda e: e.memset(hfacc, 0.0))
        p.emit(V, lambda e: e.memset(mh, 0.0))
        p.inc(V, "sV")
        p.wait(S, "sOut0")
        p.wait(S, "sOut1")
        p.wait(TE, "sV", svU)
        n_og = I // 512
        ut = [sc[1], sc[2], sc[3], u3]
        ev_g = []
        ev_u = []
        for og in range(n_og):
            if og >= 1:
                p.wait(TE, "sV", ev_g[og - 1])
            mm_accum("wgate", FT, og, 0, bn,
                     rhs_ready=(qffn if og == 0 else None))
            g_done = p.val("sT_mm")
            p.wait(V, "sT_mm", g_done)
            for oc in range(4):
                gsc = sc[0] if oc % 2 == 0 else sc[6]
                if og > 0 or oc > 1:
                    p.wait(V, "sS", p.val("sS") - 1)
                p.emit(V, lambda e, oc=oc, g=gsc: e.tensor_tensor(
                    g, pp[0 + oc], deqT, ALU.mult))
                p.inc(V, "sV")
                p.wait(S, "sV")
                p.emit(S, lambda e, g=gsc, u=ut[oc]: e.activation(u, g, AF.Silu))
                p.inc(S, "sS")
            if og >= 1:
                p.wait(TE, "sV", ev_u[og - 1])
            mm_accum("wup", FT, og, 4, bn)
            u_done = p.val("sT_mm")
            ev_g.append(p.val("sV"))
            p.wait(V, "sT_mm", u_done)
            p.wait(V, "sS")
            for oc in range(4):
                ioc = og * 4 + oc
                p.emit(V, lambda e, oc=oc: e.tensor_tensor(
                    sc[0], pp[4 + oc], deq2T, ALU.mult))
                vsync()
                hftile = sc[4 + (oc % 2)]
                if og > 0 or oc > 1:
                    p.wait(V, f"sHf{oc % 2}")
                p.emit(V, lambda e, h=hftile, u=ut[oc]: e.tensor_tensor(
                    h, u, sc[0], ALU.mult))
                p.emit(V, lambda e, h=hftile: e.tensor_tensor(sc[6], h, h, ALU.mult))
                p.emit(V, lambda e: e.tensor_tensor(hfacc, hfacc, sc[6], ALU.add))
                p.inc(V, "sV")
                p.wait(S, "sV")
                p.emit(S, lambda e, h=hftile, i=ioc: e.activation(
                    sc[7], h, AF.Square, scale=lnw_sb[:, i:i + 1]))
                p.inc(S, "sS")
                p.wait(V, "sS")
                p.emit(V, lambda e: e.tensor_tensor(mh, mh, sc[7], ALU.max))
                p.inc(V, "sV")
                p.dma_split(hfscr[ioc * P:(ioc + 1) * P, :], hftile,
                            f"sHf{oc % 2}", 1, eng=S)
            ev_u.append(p.val("sV"))
        p.wait(V, "sScr")
        p.wait(V, "sHf0")
        p.wait(V, "sHf1")
        acc_squares(ba)
        psum_row_of(sc[4], 6)
        p.wait(V, "sT")
        p.emit(V, lambda e: e.tensor_scalar(
            rw[1], pp[6][0:1, :], 1.0 / D, 1e-5, ALU.mult, ALU.add))
        p.inc(V, "sV")
        sqrt_row(rw[2], rw[1])
        p.emit(V, lambda e: e.reciprocal(rw[0], rw[2]))
        p.inc(V, "sV")
        p.wait(V, "sV")
        if ch == 0:
            carry_fold()
        p.wait(V, "sScr")
        p.wait(V, "sHfL")
        for ft in range(FT):
            hl = bb[:, ft * TC:(ft + 1) * TC]
            Ff = bc[:, ft * TC:(ft + 1) * TC]
            p.emit(V, lambda e, h=hl, F=Ff, ft=ft: e.scalar_tensor_tensor(
                h, F, carry_sb[:, ft:ft + 1], h, ALU.mult, ALU.add))
        p.inc(V, "sV")
        p.wait(S, "sV")
        for ft in range(FT):
            hb = bb[:, ft * TC:(ft + 1) * TC]
            dd = bc[:, ft * TC:(ft + 1) * TC]
            gb = ba[:, ft * TC:(ft + 1) * TC]
            p.emit(S, lambda e, h=hb, d=dd: e.activation(d, h, AF.Silu))
            p.inc(S, "sS")
            p.wait(V, "sS")
            p.emit(V, lambda e, g=gb, d=dd, ft=ft: e.scalar_tensor_tensor(
                d, g, nrm_sb["gn"][:, ft:ft + 1], d, ALU.mult, ALU.mult))
            p.inc(V, "sV")
            p.wait(S, "sV")
        acc_squares(bc)
        psum_row_of(sc[4], 5)
        p.wait(V, "sT")
        p.emit(V, lambda e: e.tensor_scalar(rw[1], pp[5][0:1, :], 1.0 / D, None, ALU.mult))
        vsync()
        p.emit(V, lambda e: e.tensor_tensor(rw[2], rw[0], rw[0], ALU.mult))
        p.emit(V, lambda e: e.tensor_tensor(rw[1], rw[1], rw[2], ALU.mult))
        p.emit(V, lambda e: e.tensor_scalar(rw[1], rw[1], 1e-8, None, ALU.add))
        p.inc(V, "sV")
        sqrt_row(rw[2], rw[1])
        p.emit(V, lambda e: e.reciprocal(rw[1], rw[2]))
        vsync()
        p.emit(V, lambda e: e.tensor_tensor(rw[2], rw[0], rw[1], ALU.mult))
        p.inc(V, "sV")
        bcast_mm(bcR, rw[2])
        maxes(bc, nrm_sb["o"])
        pmax_to_bcast(ch, 6, Mb)
        fac_deq(bcR, "wo", deqT, facT)
        svO = p.val("sV")
        qo = quant_to_bn(bc, nrm_sb["o"], bn)
        mm_evac("wo", FT, bn, 4, out_buf=ba, deq_tile=deqT,
                entry_sv=svO, rhs_base=qo)
        psum_row_of(hfacc, 6)
        p.wait(V, "sT")
        p.emit(V, lambda e: e.tensor_scalar(
            rw[3], pp[6][0:1, :], 1.0 / I, 1e-6, ALU.mult, ALU.add))
        p.inc(V, "sV")
        sqrt_row(rw[1], rw[3])
        p.emit(V, lambda e: e.reciprocal(rw[4], rw[1]))
        p.inc(V, "sV")
        bcast_mm(bcR, rw[4])
        p.emit(V, lambda e: e.tensor_copy(m1, mh))
        p.inc(V, "sV")
        pmax_to_bcast(ch, 3, Mb)
        fac_deq(bcR, "wdown", deqT, facT)
        p.wait(S, "sHf0")
        p.wait(S, "sHf1")
        p.wait(S, "sV")
        p.wait(S, "sT")
        ring = [sc[4], sc[5], m1, mh]
        slot_base = [p.val(f"sRl{j}") for j in range(4)]
        sv_base = p.val("sV")
        for k in range(FTI):
            if k >= 4:
                p.wait(S, "sV", sv_base + (k - 3))
            p.dma(ring[k % 4], hfscr[k * P:(k + 1) * P, :], f"sRl{k % 4}", eng=S)
            p.wait(V, f"sRl{k % 4}", slot_base[k % 4] + 16 * (k // 4 + 1))
            src = ring[k % 4]
            p.emit(V, lambda e, s=src, k=k: e.scalar_tensor_tensor(
                sc[6], s, lnw_sb[:, k:k + 1], facT, ALU.mult, ALU.mult))
            p.emit(V, lambda e, k=k: e.tensor_scalar(
                bnh[:, k * TC:(k + 1) * TC], sc[6], MAGIC, MAGIC, ALU.add, ALU.subtract))
            p.inc(V, "sV")
        def post_down(og, bank):
            for oc in range(4):
                ftile = og * 4 + oc
                xb = bx[:, ftile * TC:(ftile + 1) * TC]
                sb_ = ba[:, ftile * TC:(ftile + 1) * TC]
                p.emit(V, lambda e, b=bank, oc=oc: e.tensor_tensor(
                    sc[0], pp[b + oc], deqT, ALU.mult))
                p.inc(V, "sV")
                p.wait(S, "sV")
                p.emit(S, lambda e: e.activation(sc[1], sc[0], AF.Silu))
                p.inc(S, "sS")
                p.wait(V, "sS")
                otile = sc[3] if oc % 2 == 0 else sc[7]
                if og > 0 or oc > 1:
                    p.wait(V, f"sOut{oc % 2}")
                p.emit(V, lambda e, s=sb_: e.tensor_tensor(sc[2], s, sc[1], ALU.mult))
                p.emit(V, lambda e, x=xb, o=otile: e.tensor_tensor(o, x, sc[2], ALU.add))
                p.inc(V, "sV")
                p.dma_out(outT[ftile * P:(ftile + 1) * P, ch * TC:(ch + 1) * TC],
                          otile, f"sOut{oc % 2}")
        mm_evac("wdown", FTI, bnh, 4, post=post_down,
                entry_sv=sv_base, rhs_base=sv_base)
        p.wait(V, "sOut0")
        p.wait(V, "sOut1")

    def finish_debug():
        if SUBSTOP is None:
            for ft in range(FT):
                p.dma_out(outT[ft * P:(ft + 1) * P, 0:TC], bx[:, ft * TC:(ft + 1) * TC], "sOut")
        p.wait(E, "sOut")
        p.wait(E, "sScr")

    done = False
    if stop_after == "dumprow":
        p.wait(E, "sV")
        p.dma(outT[0:1, 0:8], partrow, "sOut")
        p.dma(outT[1:9, 0:8], partall, "sOut")
        p.dma(outT[9:10, 0:8], srow, "sOut")
        p.dma(outT[16:144, 0:8], swcol, "sOut")
        p.dma(outT[144:272, 0:8], swinv, "sOut")
        p.wait(E, "sOut")
        p.wait(E, "sScr")
        done = True
    if stop_after == "dumpwq":
        p.wait("gpsimd", "sCc")
        for ft in range(FT):
            p.bump("sOut", 16)
            p.emit("gpsimd", lambda e, ft=ft: e.dma_start(
                out=outT[ft * P:(ft + 1) * P, :],
                in_=wq["wi"][ft * P:(ft + 1) * P, 0:T]).then_inc(p.sems["sOut"], 16))
            p.streams["gpsimd"][-1] = (p.streams["gpsimd"][-1][0], True)
        p.wait(E, "sOut")
        p.wait(E, "sScr")
        done = True
    if stop_after == "prologue":
        p.wait(E, "sCc")
        p.dma_in(bx[:, 0:TC], xT[0:P, 0:TC], "sX")
        p.wait(E, "sX")
        finish_debug(); done = True
    if not done:
        for ch in range(NCH):
            section_A(ch)
            if stop_after == f"A{ch}":
                finish_debug(); done = True
                break
    if not done:
        carry_exchange()
        if stop_after == "carry":
            finish_debug(); done = True
        if stop_after == "dumpc":
            p.wait(E, "sV")
            p.wait(E, "sScr")
            p.dma(outT[0:P, 0:FT], hc, "sOut")
            p.dma(outT[0:P, FT:2 * FT], fc, "sOut")
            p.dma(outT[0:P, 2 * FT:3 * FT], carry_sb, "sOut")
            p.dma(outT[0:P, 3 * FT:3 * FT + 8 * FT], cslots, "sOut")
            p.dma(outT[0:P, 11 * FT:11 * FT + 8], csel_sb, "sOut")
            p.wait(E, "sOut")
            done = True
        if stop_after == "dumph2":
            p.wait(E, "sV")
            p.wait(E, "sScr")
            for ch in range(NCH):
                for ft in range(FT):
                    p.dma(bx[:, ft * TC:(ft + 1) * TC],
                          hscr[ft * P:(ft + 1) * P, ch * TC:(ch + 1) * TC], "sX")
                p.wait(E, "sX")
                for ft in range(FT):
                    p.dma(outT[ft * P:(ft + 1) * P, ch * TC:(ch + 1) * TC],
                          bx[:, ft * TC:(ft + 1) * TC], "sOut")
                p.wait(E, "sOut")
            done = True
    if not done:
        for ch in range(NCH):
            section_B(ch)
            if stop_after == f"B{ch}":
                finish_debug(); done = True
                break
    if not done:
        p.wait(E, "sOut")
        p.wait(E, "sScr")

    sem_names = ["sX", "sW", "sW0", "sW1", "sW2", "sW3", "sScr", "sBc",
                 "sOut", "sOut0", "sOut1", "sCc", "sV", "sS", "sT",
                 "sG", "sRow", "sT_mm", "sFb0", "sFb1", "sFb2", "sFb3",
                 "sHf0", "sHf1", "sHfL", "sXg", "sRl0", "sRl1", "sRl2", "sRl3"]
    with contextlib.ExitStack() as st:
        p.sems = {n_: st.enter_context(nc.semaphore(n_)) for n_ in sem_names}
        blk = st.enter_context(nc.Block())
        for eng_name in Prog.ENGS:
            stream = p.streams[eng_name]
            if not stream:
                continue
            getattr(blk, eng_name)(lambda eng, s=stream: [fn(eng) for fn, _ in s])
    return nc


def _run_on_device(stop_after, inputs):
    return None


def kernel(L0_attn_hidden_states, L0_attn_Wi, L0_attn_Wf, L0_attn_Wg, L0_attn_Wo,
           L0_attn_norm_i, L0_attn_norm_f, L0_attn_norm_g, L0_attn_norm_o,
           L0_attn_g_norm_w, L0_ffn_w_gate, L0_ffn_w_up, L0_ffn_w_down,
           L0_ffn_ln_weight):
    x = np.ascontiguousarray(np.asarray(L0_attn_hidden_states, dtype=np.float32))
    B, L, _ = x.shape
    wTf = {
        "wi": np.ascontiguousarray(np.asarray(L0_attn_Wi, dtype=np.float32).T),
        "wf": np.ascontiguousarray(np.asarray(L0_attn_Wf, dtype=np.float32).T),
        "wg": np.ascontiguousarray(np.asarray(L0_attn_Wg, dtype=np.float32).T),
        "wo": np.ascontiguousarray(np.asarray(L0_attn_Wo, dtype=np.float32).T),
        "wgate": np.ascontiguousarray(np.asarray(L0_ffn_w_gate, dtype=np.float32).T),
        "wup": np.ascontiguousarray(np.asarray(L0_ffn_w_up, dtype=np.float32).T),
        "wdown": np.ascontiguousarray(np.asarray(L0_ffn_w_down, dtype=np.float32).T),
    }
    nrm_np = {
        "i": np.ascontiguousarray(np.asarray(L0_attn_norm_i, np.float32).reshape(FT, P).T),
        "f": np.ascontiguousarray(np.asarray(L0_attn_norm_f, np.float32).reshape(FT, P).T),
        "g": np.ascontiguousarray(np.asarray(L0_attn_norm_g, np.float32).reshape(FT, P).T),
        "o": np.ascontiguousarray(np.asarray(L0_attn_norm_o, np.float32).reshape(FT, P).T),
        "gn": np.ascontiguousarray(np.asarray(L0_attn_g_norm_w, np.float32).reshape(FT, P).T),
    }
    lnw_np = np.ascontiguousarray(np.asarray(L0_ffn_ln_weight, np.float32).reshape(FTI, P).T)
    ident_np = np.eye(P, dtype=np.float32)

    nc = build_program()
    in_maps = []
    for c in range(8):
        seq, half = c // 2, c % 2
        xs = np.ascontiguousarray(x[seq, half * T:(half + 1) * T, :].T)
        csel_np = np.zeros((P, 8), np.float32)
        if half == 1:
            csel_np[:, c - 1] = 1.0
        m = {"xT": xs, "csel": csel_np, "ident": ident_np, "lnw": lnw_np}
        for k, v in nrm_np.items():
            m["nrm_" + k] = v
        for (w, K, O) in WEIGHTS:
            kper = K // 8
            m[w + "S"] = np.ascontiguousarray(wTf[w][c * kper * P:(c + 1) * kper * P, :])
        in_maps.append(m)
    res = run_bass_kernel_spmd(nc, in_maps, core_ids=list(range(8)))
    out = np.empty((B, L, D), np.float32)
    for c in range(8):
        seq, half = c // 2, c % 2
        out[seq, half * T:(half + 1) * T, :] = np.asarray(res.results[c]["outT"]).T
    return out


# version 21 (final)
# speedup vs baseline: 1.0343x; 1.0043x over previous
import contextlib
import numpy as np
import concourse.bass as bass
import concourse.mybir as mybir
from concourse.bass_utils import run_bass_kernel_spmd

F32 = mybir.dt.float32
BF16 = mybir.dt.bfloat16
AF = mybir.ActivationFunctionType
ALU = mybir.AluOpType
AXX = mybir.AxisListType.X

P = 128
D = 2048
I = 8192
T = 2048
TC = 512
NCH = T // TC
FT = D // P
FTI = I // P
MAGIC = 12582912.0

WEIGHTS = [("wi", FT, D), ("wf", FT, D), ("wg", FT, D),
           ("wgate", FT, I), ("wup", FT, I), ("wo", FT, D), ("wdown", FTI, D)]
WIDX = {w: j for j, (w, _, _) in enumerate(WEIGHTS)}
CC_OF = {w: 2 + j for j, (w, _, _) in enumerate(WEIGHTS)}


class Prog:
    ENGS = ("sync", "tensor", "vector", "scalar", "gpsimd")

    def __init__(self):
        self.streams = {e: [] for e in self.ENGS}
        self.ctr = {}
        self.sems = None

    def emit(self, eng, fn):
        self.streams[eng].append((fn, False))

    def bump(self, sem, amt):
        self.ctr[sem] = self.ctr.get(sem, 0) + amt
        return self.ctr[sem]

    def val(self, sem):
        return self.ctr.get(sem, 0)

    def wait(self, eng, sem, v=None):
        if v is None:
            v = self.val(sem)
        if v <= 0:
            return
        self.emit(eng, lambda e, s=sem, vv=v: e.wait_ge(self.sems[s], vv))
        self.streams[eng][-1] = (self.streams[eng][-1][0], True)

    def dma(self, dst, src, sem, slow=False, eng="sync"):
        self.bump(sem, 16)
        self.emit(eng, lambda e, d=dst, s=src, sm=sem, sl=slow:
                  e.dma_start(out=d, in_=s, allow_slow_non_contiguous=sl
                              ).then_inc(self.sems[sm], 16))
        self.streams[eng][-1] = (self.streams[eng][-1][0], True)

    def dma_in(self, dst, src, sem, slow=False, eng="sync"):
        self.wait(eng, "sV")
        self.wait(eng, "sS")
        self.dma(dst, src, sem, slow, eng)

    def dma_out(self, dst, src, sem, slow=False, eng="sync"):
        self.wait(eng, "sV")
        self.dma(dst, src, sem, slow, eng)

    def dma_split(self, dst, src, sem, nsplit, in_=False, eng="sync"):
        if in_:
            self.wait(eng, "sV")
            self.wait(eng, "sS")
        else:
            self.wait(eng, "sV")
        step = 128 // nsplit
        for j in range(nsplit):
            self.dma(dst[j * step:(j + 1) * step, :],
                     src[j * step:(j + 1) * step, :], sem, eng=eng)

    def inc(self, eng, sem, amt=1):
        self.bump(sem, amt)
        st = self.streams[eng]
        assert st, "inc with no prior instruction"
        prev, wrapped = st[-1]
        assert not wrapped, f"double inc on one instruction ({eng}/{sem})"
        st[-1] = (lambda e, f=prev, s=sem, a=amt: f(e).then_inc(self.sems[s], a),
                  True)


SUBSTOP = None
DELAY_TEST = False


def build_program(stop_after=None):
    nc = bass.Bass("TRN2", target_bir_lowering=False,
                   detect_race_conditions=False)
    p = Prog()
    E, TE, V, S, G = "sync", "tensor", "vector", "scalar", "gpsimd"

    din = lambda n, s, dt=F32: nc.dram_tensor(n, s, dt, kind="ExternalInput").ap()
    xT = din("xT", [D, T])
    wTs = {w: din(w + "S", [(K // 8) * P, O]) for (w, K, O) in WEIGHTS}
    nrm = {k: din("nrm_" + k, [P, FT]) for k in ("i", "f", "g", "o", "gn")}
    lnw_in = din("lnw", [P, FTI])
    csel_in = din("csel", [P, 8])
    ident_in = din("ident", [P, P])
    outT = nc.dram_tensor("outT", [D, T], F32, kind="ExternalOutput").ap()

    dint = lambda n, s, dt=F32, sp="Local": nc.dram_tensor(
        n, s, dt, kind="Internal", addr_space=sp).ap()
    slab = {w: dint("slab_" + w, [(K // 8) * P, O], BF16) for (w, K, O) in WEIGHTS}
    wq = {w: dint("wq_" + w, [K * P, O], BF16, "Shared") for (w, K, O) in WEIGHTS}
    part_box = dint("part_box", [8])
    part_gat = dint("part_gat", [8, 8], F32, "Shared")
    sw_box = dint("sw_box", [8])
    hscr = dint("hscr", [D, T])
    fscr = dint("fscr", [D, T])
    hfscr = dint("hfscr", [I, TC])
    flag_box = dint("flag_box", [8])
    flag_gat = dint("flag_gat", [8, 8], F32, "Shared")
    carry_box = dint("carry_box", [D])
    carry_gat = dint("carry_gat", [8, D], F32, "Shared")
    rowscr = dint("rowscr", [NCH, 8, TC])

    KB = 1024

    SB_BASE = 18 * KB

    def sb_at(name, shape, dt, off):
        return nc.alloc_sbuf_tensor_at(name, shape, dt, offset=SB_BASE + off).ap()

    bx = sb_at("bx", [P, FT * TC], F32, 0)
    ba = sb_at("ba", [P, FT * TC], F32, 32 * KB)
    bb = sb_at("bb", [P, FT * TC], F32, 64 * KB)
    bc = sb_at("bc", [P, FT * TC], F32, 96 * KB)
    bnh = sb_at("bnh", [P, FTI * TC], BF16, 64 * KB)
    bn = sb_at("bn", [P, FT * TC], BF16, 128 * KB)
    bnf = sb_at("bnf", [P, 8 * TC], F32, 128 * KB)
    off = 144 * KB
    sc = []
    for j in range(8):
        sc.append(sb_at(f"sc{j}", [P, TC], F32, off)); off += 2 * KB
    nm = {}
    for t_ in ("m1", "mh", "bcA", "bcB", "bcR", "facT", "deqT", "deq2T", "Mb",
               "hfacc", "bz", "mn"):
        nm[t_] = sb_at(t_, [P, TC], F32, off); off += 2 * KB
    m1, mh, bcA, bcB, bcR = nm["m1"], nm["mh"], nm["bcA"], nm["bcB"], nm["bcR"]
    facT, deqT, deq2T, Mb, hfacc, bz = (nm["facT"], nm["deqT"], nm["deq2T"],
                                        nm["Mb"], nm["hfacc"], nm["bz"])
    mntl = nm["mn"]
    wt = []
    for j in range(4):
        wt.append(sb_at(f"wt{j}", [P, TC], BF16, off)); off += 1 * KB
    u3 = sb_at("u3", [P, TC], F32, off); off += 2 * KB
    nrm_sb = {}
    for k in ("i", "f", "g", "o", "gn"):
        nrm_sb[k] = sb_at("nsb_" + k, [P, FT], F32, off); off += 64
    lnw_sb = sb_at("lnw_sb", [P, FTI], F32, off); off += 256
    ident = sb_at("ident_sb", [P, P], F32, off); off += 512
    onecol = sb_at("onecol", [P, 1], F32, off); off += 32
    onesrow = sb_at("onesrow", [1, P], F32, off); off += 512
    hc = sb_at("hc", [P, FT], F32, off); off += 64
    fc = sb_at("fc", [P, FT], F32, off); off += 64
    carry_sb = sb_at("carry_sb", [P, FT], F32, off); off += 64
    cslots = sb_at("cslots", [P, 8 * FT], F32, off); off += 512
    csel_sb = sb_at("csel_sb", [P, 8], F32, off); off += 32
    swcol = sb_at("swcol", [P, 8], F32, off); off += 32
    swinv = sb_at("swinv", [P, 8], F32, off); off += 32
    rw = []
    for j in range(6):
        rw.append(sb_at(f"rw{j}", [1, TC], F32, off)); off += 2 * KB
    packrow = sb_at("packrow", [4, P], F32, off); off += 512
    mp = sb_at("mp", [P, 4], F32, off); off += 32
    partrow = sb_at("partrow", [1, 8], F32, off); off += 32
    partall = sb_at("partall", [8, 8], F32, off); off += 32
    srow = sb_at("srow", [1, 8], F32, off); off += 32
    assert off <= 207 * KB, f"SBUF over budget: {off}"

    pp = [nc.alloc_psum_tensor(f"pp{j}", [P, TC], F32).ap() for j in range(8)]

    def vsync():
        st_ = p.streams["vector"]
        fn, wrapped = st_[-1]
        if not wrapped:
            p.bump("sV", 1)
            st_[-1] = (lambda e, f=fn: f(e).then_inc(p.sems["sV"], 1), True)
        p.wait("vector", "sV")

    def bcast_mm(dst_tile, row_sbuf):
        p.wait(TE, "sV")
        p.emit(TE, lambda e, r=row_sbuf: e.matmul(
            pp[7], onesrow, r, start=True, stop=True))
        p.inc(TE, "sT")
        p.wait(V, "sT")
        p.emit(V, lambda e, d=dst_tile: e.tensor_copy(d, pp[7]))
        vsync()

    def pmax_to_bcast(ch, slot, dst_tile):
        p.wait(TE, "sV")
        for j in range(4):
            p.emit(TE, lambda e, j=j: e.transpose(
                pp[7][:, j * P:(j + 1) * P], m1[:, j * P:(j + 1) * P], ident))
        p.inc(TE, "sT")
        p.wait(V, "sT")
        for j in range(4):
            p.emit(V, lambda e, j=j: e.reduce_max(
                mp[:, j:j + 1], pp[7][:, j * P:(j + 1) * P], axis=AXX))
        p.inc(V, "sV")
        p.wait(TE, "sV")
        p.emit(TE, lambda e: e.transpose(pp[7][0:4, 0:P], mp, ident))
        p.inc(TE, "sT")
        p.wait(V, "sT")
        p.emit(V, lambda e: e.tensor_copy(packrow, pp[7][0:4, 0:P]))
        p.inc(V, "sV")
        dst_row = rowscr[ch, slot, :]
        dstap = bass.AP(dst_row.tensor, dst_row.offset, [[P, 4], [1, P]])
        p.dma_out(dstap, packrow, "sRow")
        p.wait(E, "sRow")
        p.wait(E, "sT")
        p.dma_in(rw[2], dst_row, "sBc")
        p.wait(TE, "sBc")
        bcast_mm(dst_tile, rw[2])

    def psum_row_of(acc_tile, bank):
        p.wait(TE, "sV")
        p.emit(TE, lambda e, a=acc_tile, b=bank: e.matmul(
            pp[b][0:1, :], onecol, a, start=True, stop=True))
        p.inc(TE, "sT")

    def sqrt_row(dst, srcr):
        p.wait(S, "sV")
        p.emit(S, lambda e, d=dst, r=srcr: e.activation(d, r, AF.Sqrt))
        p.inc(S, "sS")
        p.wait(V, "sS")

    def acc_squares(in_buf, nft=FT):
        for ft in range(nft):
            src = in_buf[:, ft * TC:(ft + 1) * TC]
            p.emit(V, lambda e, s=src: e.tensor_tensor(sc[0], s, s, ALU.mult))
            if ft == 0:
                p.emit(V, lambda e: e.tensor_copy(sc[4], sc[0]))
            else:
                p.emit(V, lambda e: e.tensor_tensor(sc[4], sc[4], sc[0], ALU.add))
        p.inc(V, "sV")

    def maxes(in_buf, nrm_tile):
        p.emit(V, lambda e: e.memset(m1, 0.0))
        p.inc(V, "sV")
        p.wait(S, "sV")
        for ft in range(FT):
            src = in_buf[:, ft * TC:(ft + 1) * TC]
            sq = sc[0] if ft % 2 == 0 else sc[1]
            if nrm_tile is not None:
                col = nrm_tile[:, ft:ft + 1]
                p.emit(S, lambda e, s=src, c=col, q=sq: e.activation(
                    q, s, AF.Square, scale=c))
            else:
                p.emit(S, lambda e, s=src, q=sq: e.activation(q, s, AF.Square))
            p.inc(S, "sS")
            p.wait(V, "sS")
            p.emit(V, lambda e, q=sq: e.tensor_tensor(m1, m1, q, ALU.max))
            p.inc(V, "sV")
            if ft >= 1:
                p.wait(S, "sV", p.val("sV") - 1)
        p.wait(V, "sV")

    def fac_deq(rfac_tile, w_name, deq_dst, fac_dst, w2_name=None, deq2_dst=None):
        wi = WIDX[w_name]
        p.wait(S, "sV")
        p.wait(S, "sBc")
        p.emit(S, lambda e: e.activation(mntl, Mb, AF.Sqrt))
        p.inc(S, "sS")
        p.wait(V, "sS")
        if rfac_tile is not None:
            p.emit(V, lambda e: e.tensor_tensor(sc[2], mntl, rfac_tile, ALU.mult))
        else:
            p.emit(V, lambda e: e.tensor_copy(sc[2], mntl))
        p.emit(V, lambda e: e.tensor_scalar(sc[2], sc[2], 1e-5, None, ALU.max))
        p.emit(V, lambda e: e.reciprocal(sc[3], sc[2]))
        vsync()
        if rfac_tile is not None:
            p.emit(V, lambda e, f=fac_dst: e.scalar_tensor_tensor(
                f, sc[3], 127.0, rfac_tile, ALU.mult, ALU.mult))
        else:
            p.emit(V, lambda e, f=fac_dst: e.tensor_scalar(
                f, sc[3], 127.0, None, ALU.mult))
        p.emit(V, lambda e, d=deq_dst, wi=wi: e.tensor_scalar(
            d, sc[2], swcol[:, wi:wi + 1], 1.0 / 127.0, ALU.mult, ALU.mult))
        if w2_name is not None:
            wi2 = WIDX[w2_name]
            p.emit(V, lambda e, d=deq2_dst, wi=wi2: e.tensor_scalar(
                d, sc[2], swcol[:, wi:wi + 1], 1.0 / 127.0, ALU.mult, ALU.mult))
        p.inc(V, "sV")

    def quant_to_bn(in_buf, nrm_tile, out_bf):
        base = p.val("sV")
        for ft in range(FT):
            src = in_buf[:, ft * TC:(ft + 1) * TC]
            dstn = out_bf[:, ft * TC:(ft + 1) * TC]
            if nrm_tile is not None:
                col = nrm_tile[:, ft:ft + 1]
                p.emit(V, lambda e, s=src, c=col: e.scalar_tensor_tensor(
                    sc[0], s, c, facT, ALU.mult, ALU.mult))
            else:
                p.emit(V, lambda e, s=src: e.tensor_tensor(sc[0], s, facT, ALU.mult))
            p.emit(V, lambda e, d=dstn: e.tensor_scalar(
                d, sc[0], MAGIC, MAGIC, ALU.add, ALU.subtract))
            p.inc(V, "sV")
        return base

    def mm_accum(wname, K, og, bank, rhs_bf, rhs_ready=None):
        slot_base = [p.val(f"sW{s}") for s in range(4)]
        mm_base = p.val("sT_mm")
        p.wait(E, "sCc", CC_OF[wname])
        for k in range(K):
            p.wait(E, "sT_mm", mm_base + max(k - 3, 0))
            src = wq[wname][k * P:(k + 1) * P, og * 512:(og + 1) * 512]
            p.dma(wt[k % 4], src, f"sW{k % 4}")
        for k in range(K):
            if rhs_ready is not None:
                p.wait(TE, "sV", rhs_ready + k + 1)
            p.wait(TE, f"sW{k % 4}", slot_base[k % 4] + 16 * (k // 4 + 1))
            for oc in range(4):
                p.emit(TE, lambda e, k=k, oc=oc, b=bank: e.matmul(
                    pp[b + oc], wt[k % 4][:, oc * P:(oc + 1) * P],
                    rhs_bf[:, k * TC:(k + 1) * TC],
                    start=(k == 0), stop=(k == K - 1)))
            p.inc(TE, "sT_mm")

    def mm_evac(wname, K, rhs_bf, n_og, out_buf=None, deq_tile=None, post=None,
                entry_sv=None, rhs_base=None):
        p.wait(TE, "sV", entry_sv if entry_sv is not None else p.val("sV"))
        ev_after = []
        for og in range(n_og):
            bank = (og % 2) * 4
            if og >= 2:
                p.wait(TE, "sV", ev_after[og - 2])
            mm_accum(wname, K, og, bank, rhs_bf,
                     rhs_ready=(rhs_base if og == 0 else None))
            p.wait(V, "sT_mm")
            if post is None:
                for oc in range(4):
                    ftile = og * 4 + oc
                    dst = out_buf[:, ftile * TC:(ftile + 1) * TC]
                    p.emit(V, lambda e, b=bank, oc=oc, d=dst, q=deq_tile:
                           e.tensor_tensor(d, pp[b + oc], q, ALU.mult))
                p.inc(V, "sV")
                p.wait(V, "sV")
            else:
                post(og, bank)
            ev_after.append(p.val("sV"))

    p.dma(ident, ident_in, "sX")
    for k in ("i", "f", "g", "o", "gn"):
        p.dma(nrm_sb[k], nrm[k], "sX")
    p.dma(lnw_sb, lnw_in, "sX")
    p.dma(csel_sb, csel_in, "sX")
    p.emit(V, lambda e: e.memset(onecol, 1.0))
    p.emit(V, lambda e: e.memset(onesrow, 1.0))
    p.emit(V, lambda e: e.memset(partrow, 0.0))
    p.inc(V, "sV")
    p.dma_out(bass.AP(flag_box.tensor, 0, [[8, 1], [1, 8]]), partrow, "sScr")
    p.emit(V, lambda e: e.memset(srow, 0.0))
    p.emit(V, lambda e: e.memset(bz, 0.0))
    p.emit(V, lambda e: e.memset(hc, 0.0))
    p.emit(V, lambda e: e.memset(fc, 1.0))
    p.inc(V, "sV")

    for widx, (w, K, O) in enumerate(WEIGHTS):
        kper = K // 8
        for j in range(kper):
            p.dma_split(bx[:, 0:O], wTs[w][j * P:(j + 1) * P, :], "sX", 1,
                        in_=True)
            p.wait(V, "sX")
            p.wait(V, "sT")
            nseg = O // TC
            for sgi in range(nseg):
                seg = bx[:, sgi * TC:(sgi + 1) * TC]
                p.emit(V, lambda e, s=seg: e.tensor_scalar(
                    sc[2], s, -1.0, None, ALU.mult))
                p.emit(V, lambda e, s=seg: e.tensor_tensor(
                    sc[2], sc[2], s, ALU.max))
                p.emit(V, lambda e, sgi=sgi: e.reduce_sum(
                    sc[4][:, sgi:sgi + 1], sc[2], axis=AXX))
            vsync()
            p.emit(V, lambda e, n=nseg: e.reduce_sum(
                sc[5][:, 0:1], sc[4][:, 0:n], axis=AXX))
            p.inc(V, "sV")
            p.wait(TE, "sV")
            p.emit(TE, lambda e, f=(j == 0), l=(j == kper - 1): e.matmul(
                pp[6][0:1, 0:1], onecol, sc[5][:, 0:1], start=f, stop=l))
            p.inc(TE, "sT")
        p.wait(V, "sT")
        p.emit(V, lambda e, wi=widx: e.tensor_copy(partrow[:, wi:wi + 1],
                                                   pp[6][0:1, 0:1]))
        p.inc(V, "sV")
    p.dma_out(bass.AP(part_box.tensor, 0, [[8, 1], [1, 8]]), partrow, "sScr")
    p.wait(G, "sScr")
    p.bump("sCc", 1)
    p.emit(G, lambda e: e.collective_compute(
        "AllGather", ALU.bypass, replica_groups=[list(range(8))],
        ins=[part_box], outs=[part_gat]).then_inc(p.sems["sCc"], 1))
    p.wait(E, "sCc", 1)
    p.dma_in(partall, part_gat, "sScr")
    p.wait(TE, "sScr")
    p.wait(TE, "sV")
    p.emit(TE, lambda e: e.matmul(pp[6][0:1, 0:8], onecol[0:8, :], partall,
                                  start=True, stop=True))
    p.inc(TE, "sT")
    p.wait(V, "sT")
    p.emit(V, lambda e: e.tensor_copy(srow, pp[6][0:1, 0:8]))
    vsync()
    for widx, (w, K, O) in enumerate(WEIGHTS):
        numel = float(K * P * O)
        p.emit(V, lambda e, wi=widx, n=numel: e.tensor_scalar(
            srow[:, wi:wi + 1], srow[:, wi:wi + 1], 1.0 / n, 1e-5, ALU.mult, ALU.max))
    p.inc(V, "sV")
    p.dma_out(bass.AP(sw_box.tensor, 0, [[8, 1], [1, 8]]), srow, "sScr")
    p.wait(E, "sScr")
    p.dma_in(swcol, bass.AP(sw_box.tensor, sw_box.offset, [[0, P], [1, 8]]), "sBc")
    p.wait(V, "sBc")
    for widx in range(len(WEIGHTS)):
        p.emit(V, lambda e, wi=widx: e.reciprocal(swinv[:, wi:wi + 1],
                                                  swcol[:, wi:wi + 1]))
    p.inc(V, "sV")
    p.wait(V, "sV")
    for widx, (w, K, O) in enumerate(WEIGHTS):
        kper = K // 8
        for j in range(kper):
            p.dma_split(bx[:, 0:O], wTs[w][j * P:(j + 1) * P, :], "sX", 1,
                        in_=True)
            p.wait(V, "sX")
            p.wait(V, "sScr")
            nseg = O // TC
            for sgi in range(nseg):
                seg = bx[:, sgi * TC:(sgi + 1) * TC]
                slot = sgi % FT
                p.emit(V, lambda e, s=seg, wi=widx: e.tensor_scalar(
                    sc[7], s, swinv[:, wi:wi + 1], MAGIC, ALU.mult, ALU.add))
                p.emit(V, lambda e: e.tensor_scalar(
                    sc[7], sc[7], MAGIC + 1.0, MAGIC - 1.0, ALU.min, ALU.max))
                p.emit(V, lambda e, sl=slot: e.tensor_scalar(
                    bn[:, sl * TC:(sl + 1) * TC], sc[7], MAGIC, None, ALU.subtract))
                p.inc(V, "sV")
                p.dma_split(slab[w][j * P:(j + 1) * P, sgi * TC:(sgi + 1) * TC],
                            bn[:, slot * TC:(slot + 1) * TC], "sScr", 1)
        p.wait(G, "sScr")
        p.bump("sCc", 1)
        p.emit(G, lambda e, w=w: e.collective_compute(
            "AllGather", ALU.bypass, replica_groups=[list(range(8))],
            ins=[slab[w]], outs=[wq[w]]).then_inc(p.sems["sCc"], 1))
    p.bump("sCc", 1)
    p.emit(G, lambda e: e.collective_compute(
        "AllGather", ALU.bypass, replica_groups=[list(range(8))],
        ins=[flag_box], outs=[flag_gat]).then_inc(p.sems["sCc"], 1))
    CC_BARRIER = p.val("sCc")

    if DELAY_TEST:
        for _ in range(1500):
            p.emit(V, lambda e: e.tensor_copy(sc[0], sc[1]))
        p.inc(V, "sV")
        p.wait(E, "sV")

    def load_x(ch):
        for ft in range(FT):
            p.dma_split(bx[:, ft * TC:(ft + 1) * TC],
                        xT[ft * P:(ft + 1) * P, ch * TC:(ch + 1) * TC],
                        "sXg", 1, in_=True, eng=S)

    def ln_stats(ch):
        p.wait(V, "sXg")
        for j in range(4):
            p.wait(V, f"sFb{j}")
        for ft in range(FT):
            src = bx[:, ft * TC:(ft + 1) * TC]
            if ft == 0:
                p.emit(V, lambda e, s=src: e.tensor_copy(sc[5], s))
            else:
                p.emit(V, lambda e, s=src: e.tensor_tensor(sc[5], sc[5], s, ALU.add))
        p.inc(V, "sV")
        acc_squares(bx)
        psum_row_of(sc[5], 6)
        psum_row_of(sc[4], 5)
        p.wait(V, "sT")
        p.emit(V, lambda e: e.tensor_scalar(rw[0], pp[6][0:1, :], 1.0 / D, None, ALU.mult))
        p.emit(V, lambda e: e.tensor_scalar(rw[1], pp[5][0:1, :], 1.0 / D, None, ALU.mult))
        vsync()
        p.emit(V, lambda e: e.tensor_tensor(rw[2], rw[0], rw[0], ALU.mult))
        p.emit(V, lambda e: e.tensor_tensor(rw[2], rw[1], rw[2], ALU.subtract))
        p.emit(V, lambda e: e.tensor_scalar(rw[1], rw[2], 1e-5, None, ALU.add))
        p.inc(V, "sV")
        sqrt_row(rw[4], rw[1])
        p.emit(V, lambda e: e.reciprocal(rw[3], rw[4]))
        vsync()
        p.emit(V, lambda e: e.tensor_tensor(rw[4], rw[0], rw[3], ALU.mult))
        p.emit(V, lambda e: e.tensor_scalar(rw[4], rw[4], -1.0, None, ALU.mult))
        p.emit(V, lambda e: e.reciprocal(rw[5], rw[1]))
        vsync()
        p.emit(V, lambda e: e.tensor_tensor(rw[5], rw[2], rw[5], ALU.mult))
        p.emit(V, lambda e: e.tensor_scalar(rw[5], rw[5], 1e-8, None, ALU.add))
        p.inc(V, "sV")
        sqrt_row(rw[1], rw[5])
        p.emit(V, lambda e: e.reciprocal(rw[5], rw[1]))
        p.inc(V, "sV")
        p.dma_out(rowscr[ch, 0, :], rw[3], "sRow")
        p.dma_out(rowscr[ch, 1, :], rw[4], "sRow")
        p.dma_out(rowscr[ch, 2, :], rw[5], "sRow")
        bcast_mm(bcA, rw[3])
        bcast_mm(bcB, rw[4])
        bcast_mm(bcR, rw[5])

    def make_xn(dst_buf):
        p.wait(V, "sScr")
        for ft in range(FT):
            x_ = bx[:, ft * TC:(ft + 1) * TC]
            d_ = dst_buf[:, ft * TC:(ft + 1) * TC]
            p.emit(V, lambda e, x=x_, d=d_: e.tensor_tensor(d, x, bcA, ALU.mult))
            p.emit(V, lambda e, d=d_: e.tensor_tensor(d, d, bcB, ALU.add))
        p.inc(V, "sV")

    def section_A(ch):
        load_x(ch)
        if SUBSTOP == "load":
            return
        ln_stats(ch)
        if SUBSTOP == "ln":
            return
        make_xn(bc)
        maxes(bc, nrm_sb["i"])
        if SUBSTOP == "max":
            return
        pmax_to_bcast(ch, 3, Mb)
        if SUBSTOP == "pmax":
            return
        fac_deq(bcR, "wi", deqT, facT)
        svA = p.val("sV")
        qi = quant_to_bn(bc, nrm_sb["i"], bn)
        if SUBSTOP == "quant":
            return
        maxes(bc, nrm_sb["f"])
        if isinstance(SUBSTOP, str) and SUBSTOP.startswith("mm_og"):
            n = int(SUBSTOP[5:])
            mm_evac("wi", FT, bn, n, out_buf=ba, deq_tile=deqT)
            return
        if SUBSTOP == "mm_noev":
            ev_entry = p.val("sV_ev")
            p.wait(TE, "sV")
            mm_accum("wi", FT, 0, 0, bn)
            p.wait(V, "sT")
            p.inc(V, "sV")
            return
        if SUBSTOP == "mm_1k":
            p.wait(TE, "sV")
            sW_base = p.val("sW")
            p.wait(E, "sCc", CC_OF["wi"])
            p.dma(wt[0], wq["wi"][0:P, 0:512], "sW")
            p.wait(TE, "sW", sW_base + 16)
            for oc in range(4):
                p.emit(TE, lambda e, oc=oc: e.matmul(
                    pp[oc], wt[0][:, oc * P:(oc + 1) * P], bn[:, 0:TC],
                    start=True, stop=True))
            p.inc(TE, "sT")
            p.wait(V, "sT")
            for oc in range(4):
                p.emit(V, lambda e, oc=oc: e.tensor_tensor(
                    ba[:, oc * TC:(oc + 1) * TC], pp[oc], deqT, ALU.mult))
            p.inc(V, "sV")
            return
        mm_evac("wi", FT, bn, 4, out_buf=ba, deq_tile=deqT,
                entry_sv=svA, rhs_base=qi)
        if SUBSTOP == "mmi":
            p.wait(E, "sV")
            for ft in range(FT):
                p.dma(outT[ft * P:(ft + 1) * P, 0:TC], ba[:, ft * TC:(ft + 1) * TC], "sOut")
            p.wait(E, "sOut")
            return
        pmax_to_bcast(ch, 4, Mb)
        fac_deq(bcR, "wf", deqT, facT)
        svF = p.val("sV")
        qf = quant_to_bn(bc, nrm_sb["f"], bn)

        def post_f(og, bank):
            for oc in range(4):
                ftile = og * 4 + oc
                d = bb[:, ftile * TC:(ftile + 1) * TC]
                ia = ba[:, ftile * TC:(ftile + 1) * TC]
                p.emit(V, lambda e, b=bank, oc=oc: e.tensor_tensor(
                    sc[0], pp[b + oc], deqT, ALU.mult))
                p.inc(V, "sV")
                p.wait(S, "sV")
                p.emit(S, lambda e, d=d: e.activation(d, sc[0], AF.Sigmoid))
                p.inc(S, "sS")
                p.emit(S, lambda e, s=ia: e.activation(sc[1], s, AF.Silu))
                p.inc(S, "sS")
                p.wait(V, "sS")
                p.emit(V, lambda e, f=d: e.tensor_scalar(
                    sc[2], f, -1.0, 1.0, ALU.mult, ALU.add))
                p.emit(V, lambda e, d2=ia: e.tensor_tensor(d2, sc[1], sc[2], ALU.mult))
                p.inc(V, "sV")
        mm_evac("wf", FT, bn, 4, post=post_f, entry_sv=svF, rhs_base=qf)
        if SUBSTOP == "mmf":
            return
        for ft in range(FT):
            fb = bb[:, ft * TC:(ft + 1) * TC]
            ib = ba[:, ft * TC:(ft + 1) * TC]
            hd = bc[:, ft * TC:(ft + 1) * TC]
            p.emit(V, lambda e, f=fb, i=ib, d=hd, ft=ft: e.tensor_tensor_scan(
                d, f, i, hc[:, ft:ft + 1], ALU.mult, ALU.add))
            vsync()
            p.emit(V, lambda e, d=hd, ft=ft: e.tensor_copy(
                hc[:, ft:ft + 1], d[:, TC - 1:TC]))
            p.inc(V, "sV")
            p.dma_split(hscr[ft * P:(ft + 1) * P, ch * TC:(ch + 1) * TC], hd,
                        "sScr", 1, eng=S)
            fs = sc[1 + (ft % 4)]
            if ft >= 4:
                p.wait(V, f"sFb{ft % 4}")
            p.emit(V, lambda e, f=fb, d=fs, ft=ft: e.tensor_tensor_scan(
                d, f, bz, fc[:, ft:ft + 1], ALU.mult, ALU.add))
            vsync()
            p.emit(V, lambda e, d=fs, ft=ft: e.tensor_copy(
                fc[:, ft:ft + 1], d[:, TC - 1:TC]))
            p.inc(V, "sV")
            p.dma_split(fscr[ft * P:(ft + 1) * P, ch * TC:(ch + 1) * TC], fs,
                        f"sFb{ft % 4}", 1)

    carry_cc = [None]

    def carry_exchange():
        dst = bass.AP(carry_box.tensor, carry_box.offset, [[FT, P], [1, FT]])
        p.dma_out(dst, hc, "sScr")
        p.wait(G, "sScr")
        p.bump("sCc", 1)
        p.emit(G, lambda e: e.collective_compute(
            "AllGather", ALU.bypass, replica_groups=[list(range(8))],
            ins=[carry_box], outs=[carry_gat]).then_inc(p.sems["sCc"], 1))
        p.bump("sCc", 1)
        carry_cc[0] = p.val("sCc")
        p.emit(G, lambda e: e.collective_compute(
            "AllGather", ALU.bypass, replica_groups=[list(range(8))],
            ins=[flag_box], outs=[flag_gat]).then_inc(p.sems["sCc"], 1))

    def carry_fold():
        p.wait(E, "sCc", carry_cc[0])
        for s_ in range(8):
            src = bass.AP(carry_gat.tensor, carry_gat.offset + s_ * D,
                          [[FT, P], [1, FT]])
            p.dma(cslots[:, s_ * FT:(s_ + 1) * FT], src, "sScr")
        p.wait(V, "sScr")
        p.emit(V, lambda e: e.memset(carry_sb, 0.0))
        for s_ in range(8):
            p.emit(V, lambda e, s=s_: e.scalar_tensor_tensor(
                carry_sb, cslots[:, s * FT:(s + 1) * FT], csel_sb[:, s:s + 1],
                carry_sb, ALU.mult, ALU.add))
        p.inc(V, "sV")

    def section_B(ch):
        load_x(ch)
        p.wait(E, "sRow")
        p.dma_in(rw[3], rowscr[ch, 0, :], "sBc")
        p.dma_in(rw[4], rowscr[ch, 1, :], "sBc")
        p.dma_in(rw[5], rowscr[ch, 2, :], "sBc")
        p.wait(TE, "sBc")
        bcast_mm(bcA, rw[3])
        bcast_mm(bcB, rw[4])
        bcast_mm(bcR, rw[5])
        p.wait(V, "sXg")
        make_xn(bc)
        maxes(bc, nrm_sb["g"])
        pmax_to_bcast(ch, 3, Mb)
        fac_deq(bcR, "wg", deqT, facT)
        svG = p.val("sV")
        qg = quant_to_bn(bc, nrm_sb["g"], bn)
        maxes(bc, None)
        mm_evac("wg", FT, bn, 4, out_buf=ba, deq_tile=deqT,
                entry_sv=svG, rhs_base=qg)
        pmax_to_bcast(ch, 4, Mb)
        fac_deq(None, "wgate", deqT, facT, w2_name="wup", deq2_dst=deq2T)
        svU = p.val("sV")
        qffn = quant_to_bn(bc, None, bn)
        p.wait(E, "sScr")
        for j in range(4):
            p.wait(E, f"sFb{j}")
        for ft in range(FT):
            p.dma_split(bb[:, ft * TC:(ft + 1) * TC],
                        hscr[ft * P:(ft + 1) * P, ch * TC:(ch + 1) * TC],
                        "sHfL", 1, in_=True, eng=S)
            p.dma_split(bc[:, ft * TC:(ft + 1) * TC],
                        fscr[ft * P:(ft + 1) * P, ch * TC:(ch + 1) * TC],
                        "sHfL", 1, in_=True, eng=S)
        p.emit(V, lambda e: e.memset(hfacc, 0.0))
        p.emit(V, lambda e: e.memset(mh, 0.0))
        p.inc(V, "sV")
        p.wait(S, "sOut0")
        p.wait(S, "sOut1")
        p.wait(TE, "sV", svU)
        n_og = I // 512
        ut = [sc[1], sc[2], sc[3], u3]
        ev_g = []
        ev_u = []
        for og in range(n_og):
            if og >= 1:
                p.wait(TE, "sV", ev_g[og - 1])
            mm_accum("wgate", FT, og, 0, bn,
                     rhs_ready=(qffn if og == 0 else None))
            g_done = p.val("sT_mm")
            p.wait(V, "sT_mm", g_done)
            for oc in range(4):
                gsc = sc[0] if oc % 2 == 0 else sc[6]
                if og > 0 or oc > 1:
                    p.wait(V, "sS", p.val("sS") - 1)
                p.emit(V, lambda e, oc=oc, g=gsc: e.tensor_tensor(
                    g, pp[0 + oc], deqT, ALU.mult))
                p.inc(V, "sV")
                p.wait(S, "sV")
                p.emit(S, lambda e, g=gsc, u=ut[oc]: e.activation(u, g, AF.Silu))
                p.inc(S, "sS")
            if og >= 1:
                p.wait(TE, "sV", ev_u[og - 1])
            mm_accum("wup", FT, og, 4, bn)
            u_done = p.val("sT_mm")
            ev_g.append(p.val("sV"))
            p.wait(V, "sT_mm", u_done)
            p.wait(V, "sS")
            for oc in range(4):
                ioc = og * 4 + oc
                p.emit(V, lambda e, oc=oc: e.tensor_tensor(
                    sc[0], pp[4 + oc], deq2T, ALU.mult))
                vsync()
                hftile = sc[4 + (oc % 2)]
                if og > 0 or oc > 1:
                    p.wait(V, f"sHf{oc % 2}")
                p.emit(V, lambda e, h=hftile, u=ut[oc]: e.tensor_tensor(
                    h, u, sc[0], ALU.mult))
                p.emit(V, lambda e, h=hftile: e.tensor_tensor(sc[6], h, h, ALU.mult))
                p.emit(V, lambda e: e.tensor_tensor(hfacc, hfacc, sc[6], ALU.add))
                p.inc(V, "sV")
                p.wait(S, "sV")
                p.emit(S, lambda e, h=hftile, i=ioc: e.activation(
                    sc[7], h, AF.Square, scale=lnw_sb[:, i:i + 1]))
                p.inc(S, "sS")
                p.wait(V, "sS")
                p.emit(V, lambda e: e.tensor_tensor(mh, mh, sc[7], ALU.max))
                p.inc(V, "sV")
                p.dma_split(hfscr[ioc * P:(ioc + 1) * P, :], hftile,
                            f"sHf{oc % 2}", 1, eng=S)
            ev_u.append(p.val("sV"))
        p.wait(V, "sScr")
        p.wait(V, "sHf0")
        p.wait(V, "sHf1")
        acc_squares(ba)
        psum_row_of(sc[4], 6)
        p.wait(V, "sT")
        p.emit(V, lambda e: e.tensor_scalar(
            rw[1], pp[6][0:1, :], 1.0 / D, 1e-5, ALU.mult, ALU.add))
        p.inc(V, "sV")
        sqrt_row(rw[2], rw[1])
        p.emit(V, lambda e: e.reciprocal(rw[0], rw[2]))
        p.inc(V, "sV")
        p.wait(V, "sV")
        if ch == 0:
            carry_fold()
        p.wait(V, "sScr")
        p.wait(V, "sHfL")
        for ft in range(FT):
            hl = bb[:, ft * TC:(ft + 1) * TC]
            Ff = bc[:, ft * TC:(ft + 1) * TC]
            p.emit(V, lambda e, h=hl, F=Ff, ft=ft: e.scalar_tensor_tensor(
                h, F, carry_sb[:, ft:ft + 1], h, ALU.mult, ALU.add))
        p.inc(V, "sV")
        p.wait(S, "sV")
        for ft in range(FT):
            hb = bb[:, ft * TC:(ft + 1) * TC]
            dd = bc[:, ft * TC:(ft + 1) * TC]
            gb = ba[:, ft * TC:(ft + 1) * TC]
            p.emit(S, lambda e, h=hb, d=dd: e.activation(d, h, AF.Silu))
            p.inc(S, "sS")
            p.wait(V, "sS")
            p.emit(V, lambda e, g=gb, d=dd, ft=ft: e.scalar_tensor_tensor(
                d, g, nrm_sb["gn"][:, ft:ft + 1], d, ALU.mult, ALU.mult))
            p.inc(V, "sV")
            p.wait(S, "sV")
        acc_squares(bc)
        psum_row_of(sc[4], 5)
        p.wait(V, "sT")
        p.emit(V, lambda e: e.tensor_scalar(rw[1], pp[5][0:1, :], 1.0 / D, None, ALU.mult))
        vsync()
        p.emit(V, lambda e: e.tensor_tensor(rw[2], rw[0], rw[0], ALU.mult))
        p.emit(V, lambda e: e.tensor_tensor(rw[1], rw[1], rw[2], ALU.mult))
        p.emit(V, lambda e: e.tensor_scalar(rw[1], rw[1], 1e-8, None, ALU.add))
        p.inc(V, "sV")
        sqrt_row(rw[2], rw[1])
        p.emit(V, lambda e: e.reciprocal(rw[1], rw[2]))
        vsync()
        p.emit(V, lambda e: e.tensor_tensor(rw[2], rw[0], rw[1], ALU.mult))
        p.inc(V, "sV")
        bcast_mm(bcR, rw[2])
        maxes(bc, nrm_sb["o"])
        pmax_to_bcast(ch, 6, Mb)
        fac_deq(bcR, "wo", deqT, facT)
        svO = p.val("sV")
        qo = quant_to_bn(bc, nrm_sb["o"], bn)
        mm_evac("wo", FT, bn, 4, out_buf=ba, deq_tile=deqT,
                entry_sv=svO, rhs_base=qo)
        psum_row_of(hfacc, 6)
        p.wait(V, "sT")
        p.emit(V, lambda e: e.tensor_scalar(
            rw[3], pp[6][0:1, :], 1.0 / I, 1e-6, ALU.mult, ALU.add))
        p.inc(V, "sV")
        sqrt_row(rw[1], rw[3])
        p.emit(V, lambda e: e.reciprocal(rw[4], rw[1]))
        p.inc(V, "sV")
        bcast_mm(bcR, rw[4])
        p.emit(V, lambda e: e.tensor_copy(m1, mh))
        p.inc(V, "sV")
        pmax_to_bcast(ch, 3, Mb)
        fac_deq(bcR, "wdown", deqT, facT)
        p.wait(S, "sHf0")
        p.wait(S, "sHf1")
        p.wait(S, "sV")
        p.wait(S, "sT")
        ring = [sc[4], sc[5], m1, mh]
        slot_base = [p.val(f"sRl{j}") for j in range(4)]
        sv_base = p.val("sV")
        for k in range(FTI):
            if k >= 4:
                p.wait(S, "sV", sv_base + (k - 3))
            p.dma(ring[k % 4], hfscr[k * P:(k + 1) * P, :], f"sRl{k % 4}", eng=S)
            p.wait(V, f"sRl{k % 4}", slot_base[k % 4] + 16 * (k // 4 + 1))
            src = ring[k % 4]
            p.emit(V, lambda e, s=src, k=k: e.scalar_tensor_tensor(
                sc[6], s, lnw_sb[:, k:k + 1], facT, ALU.mult, ALU.mult))
            p.emit(V, lambda e, k=k: e.tensor_scalar(
                bnh[:, k * TC:(k + 1) * TC], sc[6], MAGIC, MAGIC, ALU.add, ALU.subtract))
            p.inc(V, "sV")
        def post_down(og, bank):
            for oc in range(4):
                ftile = og * 4 + oc
                xb = bx[:, ftile * TC:(ftile + 1) * TC]
                sb_ = ba[:, ftile * TC:(ftile + 1) * TC]
                p.emit(V, lambda e, b=bank, oc=oc: e.tensor_tensor(
                    sc[0], pp[b + oc], deqT, ALU.mult))
                p.inc(V, "sV")
                p.wait(S, "sV")
                p.emit(S, lambda e: e.activation(sc[1], sc[0], AF.Silu))
                p.inc(S, "sS")
                p.wait(V, "sS")
                otile = sc[3] if oc % 2 == 0 else sc[7]
                if og > 0 or oc > 1:
                    p.wait(V, f"sOut{oc % 2}")
                p.emit(V, lambda e, s=sb_: e.tensor_tensor(sc[2], s, sc[1], ALU.mult))
                p.emit(V, lambda e, x=xb, o=otile: e.tensor_tensor(o, x, sc[2], ALU.add))
                p.inc(V, "sV")
                p.dma_out(outT[ftile * P:(ftile + 1) * P, ch * TC:(ch + 1) * TC],
                          otile, f"sOut{oc % 2}")
        mm_evac("wdown", FTI, bnh, 4, post=post_down,
                entry_sv=sv_base, rhs_base=sv_base)
        p.wait(V, "sOut0")
        p.wait(V, "sOut1")

    def finish_debug():
        if SUBSTOP is None:
            for ft in range(FT):
                p.dma_out(outT[ft * P:(ft + 1) * P, 0:TC], bx[:, ft * TC:(ft + 1) * TC], "sOut")
        p.wait(E, "sOut")
        p.wait(E, "sScr")

    done = False
    if stop_after == "dumprow":
        p.wait(E, "sV")
        p.dma(outT[0:1, 0:8], partrow, "sOut")
        p.dma(outT[1:9, 0:8], partall, "sOut")
        p.dma(outT[9:10, 0:8], srow, "sOut")
        p.dma(outT[16:144, 0:8], swcol, "sOut")
        p.dma(outT[144:272, 0:8], swinv, "sOut")
        p.wait(E, "sOut")
        p.wait(E, "sScr")
        done = True
    if stop_after == "dumpwq":
        p.wait("gpsimd", "sCc")
        for ft in range(FT):
            p.bump("sOut", 16)
            p.emit("gpsimd", lambda e, ft=ft: e.dma_start(
                out=outT[ft * P:(ft + 1) * P, :],
                in_=wq["wi"][ft * P:(ft + 1) * P, 0:T]).then_inc(p.sems["sOut"], 16))
            p.streams["gpsimd"][-1] = (p.streams["gpsimd"][-1][0], True)
        p.wait(E, "sOut")
        p.wait(E, "sScr")
        done = True
    if stop_after == "prologue":
        p.wait(E, "sCc")
        p.dma_in(bx[:, 0:TC], xT[0:P, 0:TC], "sX")
        p.wait(E, "sX")
        finish_debug(); done = True
    if not done:
        for ch in range(NCH):
            section_A(ch)
            if stop_after == f"A{ch}":
                finish_debug(); done = True
                break
    if not done:
        carry_exchange()
        if stop_after == "carry":
            finish_debug(); done = True
        if stop_after == "dumpc":
            p.wait(E, "sV")
            p.wait(E, "sScr")
            p.dma(outT[0:P, 0:FT], hc, "sOut")
            p.dma(outT[0:P, FT:2 * FT], fc, "sOut")
            p.dma(outT[0:P, 2 * FT:3 * FT], carry_sb, "sOut")
            p.dma(outT[0:P, 3 * FT:3 * FT + 8 * FT], cslots, "sOut")
            p.dma(outT[0:P, 11 * FT:11 * FT + 8], csel_sb, "sOut")
            p.wait(E, "sOut")
            done = True
        if stop_after == "dumph2":
            p.wait(E, "sV")
            p.wait(E, "sScr")
            for ch in range(NCH):
                for ft in range(FT):
                    p.dma(bx[:, ft * TC:(ft + 1) * TC],
                          hscr[ft * P:(ft + 1) * P, ch * TC:(ch + 1) * TC], "sX")
                p.wait(E, "sX")
                for ft in range(FT):
                    p.dma(outT[ft * P:(ft + 1) * P, ch * TC:(ch + 1) * TC],
                          bx[:, ft * TC:(ft + 1) * TC], "sOut")
                p.wait(E, "sOut")
            done = True
    if not done:
        for ch in range(NCH):
            section_B(ch)
            if stop_after == f"B{ch}":
                finish_debug(); done = True
                break
    if not done:
        p.wait(E, "sOut")
        p.wait(E, "sScr")

    sem_names = ["sX", "sW", "sW0", "sW1", "sW2", "sW3", "sScr", "sBc",
                 "sOut", "sOut0", "sOut1", "sCc", "sV", "sS", "sT",
                 "sG", "sRow", "sT_mm", "sFb0", "sFb1", "sFb2", "sFb3",
                 "sHf0", "sHf1", "sHfL", "sXg", "sRl0", "sRl1", "sRl2", "sRl3"]
    with contextlib.ExitStack() as st:
        p.sems = {n_: st.enter_context(nc.semaphore(n_)) for n_ in sem_names}
        blk = st.enter_context(nc.Block())
        for eng_name in Prog.ENGS:
            stream = p.streams[eng_name]
            if not stream:
                continue
            getattr(blk, eng_name)(lambda eng, s=stream: [fn(eng) for fn, _ in s])
    return nc


def _run_on_device(stop_after, inputs):
    return None


def kernel(L0_attn_hidden_states, L0_attn_Wi, L0_attn_Wf, L0_attn_Wg, L0_attn_Wo,
           L0_attn_norm_i, L0_attn_norm_f, L0_attn_norm_g, L0_attn_norm_o,
           L0_attn_g_norm_w, L0_ffn_w_gate, L0_ffn_w_up, L0_ffn_w_down,
           L0_ffn_ln_weight):
    x = np.ascontiguousarray(np.asarray(L0_attn_hidden_states, dtype=np.float32))
    B, L, _ = x.shape
    wTf = {
        "wi": np.ascontiguousarray(np.asarray(L0_attn_Wi, dtype=np.float32).T),
        "wf": np.ascontiguousarray(np.asarray(L0_attn_Wf, dtype=np.float32).T),
        "wg": np.ascontiguousarray(np.asarray(L0_attn_Wg, dtype=np.float32).T),
        "wo": np.ascontiguousarray(np.asarray(L0_attn_Wo, dtype=np.float32).T),
        "wgate": np.ascontiguousarray(np.asarray(L0_ffn_w_gate, dtype=np.float32).T),
        "wup": np.ascontiguousarray(np.asarray(L0_ffn_w_up, dtype=np.float32).T),
        "wdown": np.ascontiguousarray(np.asarray(L0_ffn_w_down, dtype=np.float32).T),
    }
    nrm_np = {
        "i": np.ascontiguousarray(np.asarray(L0_attn_norm_i, np.float32).reshape(FT, P).T),
        "f": np.ascontiguousarray(np.asarray(L0_attn_norm_f, np.float32).reshape(FT, P).T),
        "g": np.ascontiguousarray(np.asarray(L0_attn_norm_g, np.float32).reshape(FT, P).T),
        "o": np.ascontiguousarray(np.asarray(L0_attn_norm_o, np.float32).reshape(FT, P).T),
        "gn": np.ascontiguousarray(np.asarray(L0_attn_g_norm_w, np.float32).reshape(FT, P).T),
    }
    lnw_np = np.ascontiguousarray(np.asarray(L0_ffn_ln_weight, np.float32).reshape(FTI, P).T)
    ident_np = np.eye(P, dtype=np.float32)

    nc = build_program()
    in_maps = []
    for c in range(8):
        seq, half = c // 2, c % 2
        xs = np.ascontiguousarray(x[seq, half * T:(half + 1) * T, :].T)
        csel_np = np.zeros((P, 8), np.float32)
        if half == 1:
            csel_np[:, c - 1] = 1.0
        m = {"xT": xs, "csel": csel_np, "ident": ident_np, "lnw": lnw_np}
        for k, v in nrm_np.items():
            m["nrm_" + k] = v
        for (w, K, O) in WEIGHTS:
            kper = K // 8
            m[w + "S"] = np.ascontiguousarray(wTf[w][c * kper * P:(c + 1) * kper * P, :])
        in_maps.append(m)
    res = run_bass_kernel_spmd(nc, in_maps, core_ids=list(range(8)))
    out = np.empty((B, L, D), np.float32)
    for c in range(8):
        seq, half = c // 2, c % 2
        out[seq, half * T:(half + 1) * T, :] = np.asarray(res.results[c]["outT"]).T
    return out
